# version 03
# baseline (speedup 1.0000x reference)
import math
import numpy as np
import ml_dtypes
import concourse.bass as bass
import concourse.mybir as mybir
from concourse.bass_utils import run_bass_kernel_spmd

F32 = mybir.dt.float32
BF16 = mybir.dt.bfloat16
I32 = mybir.dt.int32
U8 = mybir.dt.uint8
AF = mybir.ActivationFunctionType
ALU = mybir.AluOpType
AX = mybir.AxisListType

D = 4096
S = 4096
HD = 128
NHL = 8
QC = NHL * HD
KVC = 256
COLS = QC + 2 * KVC
NT = S // 128
ND = D // 128
NR = S // 512
NFO = 16
SCALE = 1.0 / math.sqrt(HD)
LN_EPS = 1e-5


class Plan:

    ENGS = ("sync", "gpsimd", "vector", "scalar", "tensor")

    def __init__(self, nc):
        self.nc = nc
        self.q = {e: [] for e in self.ENGS}
        self.h = {}
        self.c = {}
        self.cuts = []

    def sem(self, name):
        self.h[name] = self.nc.alloc_semaphore(name)
        self.c[name] = 0

    def emit(self, eng, fn, waits=(), inc=None, incby=1):
        if inc is not None:
            self.c[inc] += incby
        self.q[eng].append((tuple(waits), fn, inc, incby))

    def v(self, name):
        return self.c[name]

    def barrier(self):
        self.cuts.append({e: len(self.q[e]) for e in self.ENGS})

    def replay(self):
        nc = self.nc
        self.cuts.append({e: len(self.q[e]) for e in self.ENGS})
        start = {e: 0 for e in self.ENGS}
        lastw = {e: {} for e in self.ENGS}
        for seg_i, cut in enumerate(self.cuts):
            with nc.Block(f"ph{seg_i}") as block:
                for ename in self.ENGS:
                    items = self.q[ename][start[ename]:cut[ename]]
                    if not items:
                        continue

                    def make_body(items=items, lw=lastw[ename], plan=self):
                        def body(eng):
                            for waits, fn, inc, incby in items:
                                for sname, val in waits:
                                    if val > 0 and lw.get(sname, 0) < val:
                                        eng.wait_ge(plan.h[sname], val)
                                        lw[sname] = val
                                inst = fn(eng)
                                if inc is not None:
                                    inst.then_inc(plan.h[inc], incby)
                        return body

                    getattr(block, ename)(make_body())
            start = dict(cut)
            if seg_i < len(self.cuts) - 1:
                nc.all_engine_barrier()


class Carver:
    def __init__(self, arena_ap, nbytes):
        self.a = arena_ap
        self.n = nbytes
        self.cur = 0

    def take(self, nbytes, dtype, parts=128, base=0):
        off = (self.cur + 31) & ~31
        self.cur = off + nbytes
        assert self.cur <= self.n, f"arena overflow: {self.cur} > {self.n}"
        return self.a[base:base + parts, off:off + nbytes].bitcast(dtype)

    def at(self, off, nbytes, dtype, parts=128, base=0):
        return self.a[base:base + parts, off:off + nbytes].bitcast(dtype)

    def mark(self):
        return self.cur

    def reset(self, pos):
        self.cur = pos


def build_nc():
    nc = bass.Bass("TRN2", target_bir_lowering=False, debug=False)
    ap = lambda t: t.ap()

    x_batch = ap(nc.dram_tensor("x_batch", [S, D], F32, kind="ExternalInput"))
    x_own = ap(nc.dram_tensor("x_own", [1024, D], F32, kind="ExternalInput"))
    wqkvT = ap(nc.dram_tensor("wqkvT", [D, COLS], F32, kind="ExternalInput"))
    bqkv = ap(nc.dram_tensor("bqkv", [1, COLS], F32, kind="ExternalInput"))
    woT = ap(nc.dram_tensor("woT", [D, D], F32, kind="ExternalInput"))
    bo = ap(nc.dram_tensor("bo", [1, D], F32, kind="ExternalInput"))
    ffnT = ap(nc.dram_tensor("ffnT", [D, D], F32, kind="ExternalInput"))
    bffn = ap(nc.dram_tensor("bffn", [1, D], F32, kind="ExternalInput"))
    lAT = ap(nc.dram_tensor("lAT", [D, 48], F32, kind="ExternalInput"))
    lBT = ap(nc.dram_tensor("lBT", [48, D], F32, kind="ExternalInput"))
    lsi = ap(nc.dram_tensor("lsi", [4], I32, kind="ExternalInput"))
    bplus1 = ap(nc.dram_tensor("bplus1", [1, 1], F32, kind="ExternalInput"))
    selmask = ap(nc.dram_tensor("selmask", [128, 4], F32, kind="ExternalInput"))
    cosT = ap(nc.dram_tensor("cosT", [S, 64], F32, kind="ExternalInput"))
    sinT = ap(nc.dram_tensor("sinT", [S, 64], F32, kind="ExternalInput"))
    identt = ap(nc.dram_tensor("identt", [128, 128], BF16, kind="ExternalInput"))
    mask4 = ap(nc.dram_tensor("mask4", [4, 128, 512], BF16, kind="ExternalInput"))
    out = ap(nc.dram_tensor("out", [1024, D], F32, kind="ExternalOutput"))

    ln1f = ap(nc.dram_tensor("ln1f", [D, S], BF16, kind="Internal"))
    qdram = ap(nc.dram_tensor("qdram", [S, QC], BF16, kind="Internal"))
    cc2_in = ap(nc.dram_tensor("cc2_in", [QC, S], BF16, kind="Internal"))
    cc2_out = ap(nc.dram_tensor("cc2_out", [4 * QC, S], BF16, kind="Internal"))

    A = lambda *a, **k: nc.alloc_sbuf_tensor(*a, **k).ap()
    ident_sb = A("ident_sb", [128, 128], BF16)
    ones_r16 = A("ones_r16", [1, 128], BF16)
    ones_c16 = A("ones_c16", [128, 1], BF16)
    ones_rf = A("ones_rf", [1, 128], F32)
    epscol = A("epscol", [128, 1], F32)
    eps30 = A("eps30", [1, 1], F32)
    sel_sb = A("sel_sb", [128, 4], F32)
    bp1_sb = A("bp1_sb", [1, 1], F32)
    lsi_sb = A("lsi_sb", [1, 4], I32)
    idxf = A("idxf", [1, 4], F32)
    gsc = A("gsc", [1, 16], F32)
    gex = A("gex", [1, 48], F32)
    gmask = A("gmask", [48, 1], F32)
    normsq = A("normsq", [128, 10 * NT], F32)
    nmax = A("nmax", [128, 1], F32)
    nmax16 = A("nmax16", [128, 1], BF16)
    maxsq = A("maxsq", [1, 16], F32)
    negbias = A("negbias", [128, NHL], F32)
    st_sum = A("st_sum", [128, 1], F32)
    st_sumB = A("st_sumB", [128, 1], F32)
    st_sq = A("st_sq", [128, 1], F32)
    st_sqB = A("st_sqB", [128, 1], F32)
    st_m = A("st_m", [128, 1], F32)
    st_msq = A("st_msq", [128, 1], F32)
    st_var = A("st_var", [128, 1], F32)
    st_rstd = A("st_rstd", [128, 1], F32)
    st_negmr = A("st_negmr", [128, 1], F32)
    csc = A("csc", [1, 2], F32)
    recipD = A("recipD", [1, 512], F32)
    dbc_sb = A("dbc_sb", [128, 512], F32)
    masks_sb = A("masks_sb", [128, 4 * 512], BF16)
    sqscr = A("sqscr", [128, 512], BF16)

    ARENA = 190 * 1024
    arena = A("arena", [128, ARENA], U8)
    cv = Carver(arena, ARENA)

    kv_tm = cv.take(NT * 512 * 2, BF16)
    kfm = cv.take(2 * S * 2, BF16)
    ab_end = cv.mark()

    xh = [cv.take(D * 4, F32) for _ in range(2)]
    ln1_tm = [cv.take(D * 2, BF16) for _ in range(2)]
    evacb = [cv.take(D * 2, BF16) for _ in range(2)]
    wbufA = [cv.take(ND * 512 * 2, BF16)]
    lbuf = [cv.take(ND * 128 * 2, BF16) for _ in range(2)]
    qstage = [cv.take(512 * 2, BF16) for _ in range(2)]
    t1f = cv.take(256 * 4, F32)
    t2f = cv.take(256 * 4, F32)
    cos_sb = cv.take(NT * 64 * 4, F32)
    sin_sb = cv.take(NT * 64 * 4, F32)
    bqkv16 = cv.take(COLS * 2, BF16, parts=1)

    cv.reset(ab_end)
    qtm = [cv.take(512 * 2, BF16) for _ in range(2)]
    qfm = [cv.take(512 * 2, BF16) for _ in range(2)]
    attnT = [cv.take(512 * 2, BF16) for _ in range(2)]
    av_sb = [cv.take(512 * 2, BF16) for _ in range(2)]

    cv.reset(0)
    av_fm = cv.take(NT * 1024 * 2, BF16)
    x2 = cv.take(8 * D * 2, BF16)
    wbufC = [cv.take(ND * 256 * 2, BF16) for _ in range(2)]
    ln2_tm = wbufC[0][:, 0:D]
    xoC = [cv.take(256 * 4, F32) for _ in range(2)]
    chkb = [cv.take(1024 * 2, BF16) for _ in range(2)]
    p_sb = cv.take(1024 * 2, BF16, parts=48)
    lBT_sb = cv.take(D * 2, BF16, parts=48)
    lAT_sb = cv.take(ND * 48 * 2, BF16)
    bias16 = cv.take(D * 2, BF16, parts=1)
    outsb = [cv.take(256 * 4, F32) for _ in range(2)]
    ln2_fm = av_fm

    pb = [nc.alloc_psum_tensor(f"pb{i}", [128, 512], F32).ap() for i in range(8)]
    pb16 = [p.bitcast(BF16) for p in pb]

    P = Plan(nc)
    for s in ("sx0", "sx1", "slnw0", "slnw1", "slnr0", "slnr1", "sw",
              "swc0", "swc1", "sqw0", "sqw1", "sqr0", "sqr1", "savw0", "savw1",
              "scc", "sccr0", "sccr1", "sxo0", "sxo1", "sk", "sout0", "sout1",
              "spe", "sact", "sdve"):
        P.sem(s)

    war_bank = {i: [] for i in range(8)}

    P.emit("sync", lambda e: e.dma_start(out=ident_sb[:, :], in_=identt[:, :]), inc="sk", incby=16)
    P.emit("sync", lambda e: e.dma_start(out=sel_sb[:, :], in_=selmask[:, :]), inc="sk", incby=16)
    P.emit("sync", lambda e: e.dma_start(out=bp1_sb[:, :], in_=bplus1[:, :]), inc="sk", incby=16)
    P.emit("sync", lambda e: e.dma_start(out=lsi_sb[:, :], in_=lsi.unsqueeze(0)), inc="sk", incby=16)
    P.emit("sync", lambda e: e.dma_start(
        out=masks_sb.rearrange("p (d q) -> p d q", d=4),
        in_=mask4.rearrange("d p q -> p d q")), inc="sk", incby=16)
    P.emit("sync", lambda e: e.dma_start(
        out=cos_sb.rearrange("p (t f) -> p t f", f=64),
        in_=cosT.rearrange("(t p) f -> p t f", p=128)), inc="sk", incby=16)
    P.emit("sync", lambda e: e.dma_start(
        out=sin_sb.rearrange("p (t f) -> p t f", f=64),
        in_=sinT.rearrange("(t p) f -> p t f", p=128)), inc="sk", incby=16)
    SK_IDENT = SK_SEL = SK_BP1 = SK_LSI = SK_MASK = SK_COS = SK_SIN = 112
    P.emit("gpsimd", lambda e: e.dma_start(out=bqkv16[:, :], in_=bqkv[:, :]), inc="sw", incby=16)
    SW_BQKV = P.v("sw")
    P.emit("vector", lambda e: e.memset(ones_r16[:, :], 1.0))
    P.emit("vector", lambda e: e.memset(ones_c16[:, :], 1.0))
    P.emit("vector", lambda e: e.memset(ones_rf[:, :], 1.0))
    P.emit("vector", lambda e: e.memset(epscol[:, :], LN_EPS))
    P.emit("vector", lambda e: e.memset(eps30[:, :], 1e-30), inc="sdve")
    DVE_CONST = P.v("sdve")

    war_x = [[], []]
    war_lntm = [[], []]
    war_evacb = [[], []]
    war_stats = []
    for st in range(NT):
        bi = st % 2
        sxp = f"sx{bi}"
        for hf in range(2):
            P.emit("sync", lambda e, st=st, hf=hf, bi=bi: e.dma_start(
                out=xh[bi][:, hf * 2048:(hf + 1) * 2048],
                in_=x_batch[st * 128:(st + 1) * 128, hf * 2048:(hf + 1) * 2048]),
                waits=war_x[bi], inc=sxp, incby=16)
        xload = P.v(sxp)
        P.emit("vector", lambda e, bi=bi: e.reduce_sum(st_sum[:, :], xh[bi][:, 0:2048], axis=AX.X),
               waits=[(sxp, xload)] + war_stats)
        P.emit("vector", lambda e, bi=bi: e.reduce_sum(st_sumB[:, :], xh[bi][:, 2048:4096], axis=AX.X),
               inc="sdve")
        P.emit("vector", lambda e: e.tensor_add(st_sum[:, :], st_sum[:, :], st_sumB[:, :]),
               waits=[("sdve", P.v("sdve"))], inc="sdve")
        P.emit("scalar", lambda e, bi=bi: e.activation(
            ln1_tm[bi][:, 0:2048], xh[bi][:, 0:2048], AF.Square, accum_out=st_sq[:, :]),
            waits=[(sxp, xload)] + war_lntm[bi])
        P.emit("scalar", lambda e, bi=bi: e.activation(
            ln1_tm[bi][:, 2048:4096], xh[bi][:, 2048:4096], AF.Square, accum_out=st_sqB[:, :]),
            inc="sact")
        sq_done = P.v("sact")
        P.emit("vector", lambda e: e.tensor_scalar_mul(st_m[:, :], st_sum[:, :], 1.0 / D),
               waits=[("sdve", P.v("sdve"))], inc="sdve")
        P.emit("vector", lambda e: e.tensor_mul(st_msq[:, :], st_m[:, :], st_m[:, :]),
               waits=[("sdve", P.v("sdve"))], inc="sdve")
        P.emit("vector", lambda e: e.tensor_add(st_sq[:, :], st_sq[:, :], st_sqB[:, :]),
               waits=[("sact", sq_done), ("sdve", P.v("sdve"))], inc="sdve")
        P.emit("vector", lambda e: e.scalar_tensor_tensor(
            st_var[:, :], st_sq[:, :], 1.0 / D, st_msq[:, :], ALU.mult, ALU.subtract),
            waits=[("sdve", P.v("sdve"))], inc="sdve")
        var_done = P.v("sdve")
        P.emit("scalar", lambda e: e.activation(st_rstd[:, :], st_var[:, :], AF.Ln, bias=epscol[:, :]),
               waits=[("sdve", var_done)])
        P.emit("scalar", lambda e: e.activation(st_rstd[:, :], st_rstd[:, :], AF.Exp, scale=-0.5),
               inc="sact")
        rstd_done = P.v("sact")
        P.emit("vector", lambda e: e.scalar_tensor_tensor(
            st_negmr[:, :], st_m[:, :], -1.0, st_rstd[:, :], ALU.mult, ALU.mult),
            waits=[("sact", rstd_done), ("sdve", var_done)], inc="sdve")
        negmr_done = P.v("sdve")
        for hf in range(2):
            P.emit("scalar", lambda e, bi=bi, hf=hf: e.activation(
                ln1_tm[bi][:, hf * 2048:(hf + 1) * 2048], xh[bi][:, hf * 2048:(hf + 1) * 2048],
                AF.Identity, bias=st_negmr[:, :], scale=st_rstd[:, :]),
                waits=[("sdve", negmr_done), ("sact", rstd_done)], inc="sact")
        apply_done = P.v("sact")
        war_x[bi] = [("sact", apply_done)]
        war_stats = [("sact", apply_done)]
        for k in range(8):
            bank = 6 + (k % 2)
            for i in range(4):
                dt = 4 * k + i
                P.emit("tensor", lambda e, bi=bi, dt=dt, bank=bank, i=i: e.transpose(
                    pb16[bank][:, i * 128:(i + 1) * 128],
                    ln1_tm[bi][:, dt * 128:(dt + 1) * 128], ident_sb[:, :]),
                    waits=([("sact", apply_done), ("sk", SK_IDENT)] + war_bank[bank]) if i == 0 else [],
                    inc=("spe" if i == 3 else None))
            tr_done = P.v("spe")
            P.emit("vector", lambda e, bank=bank, bi=bi, k=k: e.tensor_copy(
                evacb[bi][:, k * 512:(k + 1) * 512], pb16[bank][:, 0:512]),
                waits=[("spe", tr_done)] + (war_evacb[bi] if k == 0 else []), inc="sdve")
            war_bank[bank] = [("sdve", P.v("sdve"))]
        ev_done = P.v("sdve")
        lnwp = f"slnw{bi}"
        P.emit("sync", lambda e, st=st, bi=bi: e.dma_start(
            out=ln1f.rearrange("(j p) t -> p j t", p=128)[:, :, st * 128:(st + 1) * 128],
            in_=evacb[bi][:, :].rearrange("p (j t) -> p j t", t=128)),
            waits=[("sdve", ev_done)], inc=lnwp, incby=16)
        war_evacb[bi] = [(lnwp, P.v(lnwp))]
        war_lntm[bi] = [("spe", P.v("spe"))]

    def rope_emit(bank, tt, nh, dst, pre_waits, post_inc_sem="sdve"):
        src = pb[bank][:, 0:nh * 128].rearrange("p (h r two) -> p h r two", h=nh, two=2)
        xr, xi = src[:, :, :, 0], src[:, :, :, 1]
        cosb = cos_sb.rearrange("p (t f) -> p t f", f=64)[:, tt, :].unsqueeze(1).broadcast_to([128, nh, 64])
        sinb = sin_sb.rearrange("p (t f) -> p t f", f=64)[:, tt, :].unsqueeze(1).broadcast_to([128, nh, 64])
        t1 = t1f[:, 0:nh * 64].rearrange("p (h r) -> p h r", h=nh)
        t2 = t2f[:, 0:nh * 64].rearrange("p (h r) -> p h r", h=nh)
        dstv = dst.rearrange("p (h r two) -> p h r two", h=nh, two=2)
        P.emit("vector", lambda e, t1=t1, xr=xr, cosb=cosb: e.tensor_tensor(t1, xr, cosb, ALU.mult),
               waits=pre_waits + [("sk", SK_SIN)])
        P.emit("vector", lambda e, t2=t2, xi=xi, sinb=sinb: e.tensor_tensor(t2, xi, sinb, ALU.mult))
        P.emit("vector", lambda e, dstv=dstv, t1=t1, t2=t2: e.tensor_tensor(dstv[:, :, :, 0], t1, t2, ALU.subtract))
        P.emit("vector", lambda e, t1=t1, xr=xr, sinb=sinb: e.tensor_tensor(t1, xr, sinb, ALU.mult))
        P.emit("vector", lambda e, t2=t2, xi=xi, cosb=cosb: e.tensor_tensor(t2, xi, cosb, ALU.mult))
        P.emit("vector", lambda e, dstv=dstv, t1=t1, t2=t2: e.tensor_tensor(dstv[:, :, :, 1], t1, t2, ALU.add),
               inc=post_inc_sem)

    war_w = [[], []]
    war_lbuf = [[], []]
    war_qkbank = {0: [], 1: []}
    war_qstage = [[], []]
    for ch in range(3):
        wb = 0
        P.emit("gpsimd", lambda e, ch=ch, wb=wb: e.dma_start(
            out=wbufA[wb].rearrange("p (j n) -> p j n", n=512),
            in_=wqkvT.rearrange("(j p) n -> p j n", p=128)[:, :, ch * 512:(ch + 1) * 512]),
            waits=war_w[wb], inc="sw", incby=16)
        wready = P.v("sw")
        for tt in range(NT):
            lb = tt % 2
            lnrp = f"slnr{lb}"
            P.emit("sync", lambda e, tt=tt, lb=lb: e.dma_start(
                out=lbuf[lb].rearrange("p (j t) -> p j t", t=128),
                in_=ln1f.rearrange("(j p) t -> p j t", p=128)[:, :, tt * 128:(tt + 1) * 128]),
                waits=[(f"slnw{tt % 2}", 16 * (tt // 2 + 1))] + war_lbuf[lb], inc=lnrp, incby=16)
            lready = (lnrp, P.v(lnrp))
            bank = tt % 2
            for dt in range(ND):
                P.emit("tensor", lambda e, lb=lb, wb=wb, dt=dt, bank=bank: e.matmul(
                    pb[bank][:, :], lbuf[lb][:, dt * 128:(dt + 1) * 128],
                    wbufA[wb][:, dt * 512:(dt + 1) * 512], start=(dt == 0), stop=False),
                    waits=([lready, ("sw", wready)] + war_qkbank[bank]) if dt == 0 else [])
            P.emit("tensor", lambda e, ch=ch, bank=bank: e.matmul(
                pb[bank][:, :], ones_r16[:, :], bqkv16[:, ch * 512:(ch + 1) * 512],
                start=False, stop=True),
                waits=[("sw", SW_BQKV)], inc="spe")
            mm_done = P.v("spe")
            war_lbuf[lb] = [("spe", mm_done)]
            if tt == NT - 1:
                war_w[wb] = [("spe", mm_done)]
            if ch < 2:
                qs = tt % 2
                rope_emit(bank, tt, 4, qstage[qs][:, :],
                          [("spe", mm_done)] + war_qstage[qs])
                rope_done = P.v("sdve")
                for i in range(4):
                    hd_i = ch * 4 + i
                    P.emit("scalar", lambda e, qs=qs, i=i, hd_i=hd_i, tt=tt: e.activation(
                        sqscr[:, 0:128], qstage[qs][:, i * 128:(i + 1) * 128], AF.Square,
                        accum_out=normsq[:, hd_i * NT + tt:hd_i * NT + tt + 1]),
                        waits=[("sdve", rope_done)], inc="sact")
                sqn_done = P.v("sact")
                qwp = f"sqw{qs}"
                P.emit("sync", lambda e, qs=qs, tt=tt, ch=ch: e.dma_start(
                    out=qdram[tt * 128:(tt + 1) * 128, ch * 512:(ch + 1) * 512],
                    in_=qstage[qs][:, :]),
                    waits=[("sdve", rope_done)], inc=qwp, incby=16)
                war_qstage[qs] = [(qwp, P.v(qwp)), ("sact", sqn_done)]
                war_qkbank[bank] = [("sdve", rope_done)]
            else:
                rope_emit(bank, tt, 2, kv_tm[:, tt * 512:tt * 512 + 256],
                          [("spe", mm_done)])
                krope_done = P.v("sdve")
                for i in range(2):
                    P.emit("scalar", lambda e, i=i, tt=tt: e.activation(
                        sqscr[:, 0:128], kv_tm[:, tt * 512 + i * 128:tt * 512 + (i + 1) * 128],
                        AF.Square, accum_out=normsq[:, (8 + i) * NT + tt:(8 + i) * NT + tt + 1]),
                        waits=[("sdve", krope_done)])
                P.emit("scalar", lambda e, tt=tt, bank=bank: e.activation(
                    kv_tm[:, tt * 512 + 256:tt * 512 + 512], pb[bank][:, 256:512], AF.Copy),
                    waits=[("spe", mm_done)], inc="sact")
                war_qkbank[bank] = [("sdve", krope_done), ("sact", P.v("sact"))]
    A2_ACT = P.v("sact")
    A2_DVE = P.v("sdve")

    for g in range(2):
        for k4 in range(NT // 4):
            bank = 6 + (k4 % 2)
            for i in range(4):
                tt = 4 * k4 + i
                P.emit("tensor", lambda e, g=g, tt=tt, bank=bank, i=i: e.transpose(
                    pb16[bank][:, i * 128:(i + 1) * 128],
                    kv_tm[:, tt * 512 + g * 128:tt * 512 + (g + 1) * 128], ident_sb[:, :]),
                    waits=([("sact", A2_ACT), ("sdve", A2_DVE)] + war_bank[bank]) if i == 0 else [],
                    inc=("spe" if i == 3 else None))
            tr_done = P.v("spe")
            P.emit("vector", lambda e, g=g, k4=k4, bank=bank: e.tensor_copy(
                kfm[:, (g * NT + 4 * k4) * 128:(g * NT + 4 * k4) * 128 + 512], pb16[bank][:, 0:512]),
                waits=[("spe", tr_done)], inc="sdve")
            war_bank[bank] = [("sdve", P.v("sdve"))]
    KFM_DVE = P.v("sdve")

    war_nmax16 = []
    for hd_i in range(10):
        P.emit("vector", lambda e, hd_i=hd_i: e.reduce_max(
            nmax[:, :], normsq[:, hd_i * NT:(hd_i + 1) * NT], axis=AX.X),
            waits=[("sact", A2_ACT)], inc="sdve")
        P.emit("vector", lambda e: e.tensor_copy(nmax16[:, :], nmax[:, :]),
               waits=war_nmax16 + [("sdve", P.v("sdve"))], inc="sdve")
        rm = P.v("sdve")
        P.emit("tensor", lambda e: e.transpose(pb16[7][0:1, 0:128], nmax16[:, :], ident_sb[:, :]),
               waits=[("sdve", rm)] + war_bank[7], inc="spe")
        trm = P.v("spe")
        P.emit("vector", lambda e, hd_i=hd_i: e.reduce_max(
            maxsq[:, hd_i:hd_i + 1], pb16[7][0:1, 0:128], axis=AX.X),
            waits=[("spe", trm)], inc="sdve")
        war_bank[7] = [("sdve", P.v("sdve"))]
        war_nmax16 = [("spe", trm)]
    war_csc = []
    for lh in range(NHL):
        g = lh // 4
        P.emit("vector", lambda e, lh=lh, g=g: e.tensor_mul(
            csc[:, 0:1], maxsq[:, lh:lh + 1], maxsq[:, 8 + g:9 + g]),
            waits=war_csc, inc="sdve")
        md = P.v("sdve")
        P.emit("scalar", lambda e: e.activation(csc[:, 1:2], csc[:, 0:1], AF.Ln, bias=eps30[:, :]),
               waits=[("sdve", md)])
        P.emit("scalar", lambda e: e.activation(csc[:, 1:2], csc[:, 1:2], AF.Exp, scale=0.5), inc="sact")
        cd = P.v("sact")
        P.emit("vector", lambda e: e.tensor_scalar_mul(csc[:, 0:1], csc[:, 1:2], -SCALE),
               waits=[("sact", cd)], inc="sdve")
        nd_ = P.v("sdve")
        P.emit("tensor", lambda e: e.matmul(pb[7][:, 0:1], ones_rf[:, :], csc[:, 0:1],
                                            start=True, stop=True),
               waits=[("sdve", nd_)] + war_bank[7], inc="spe")
        bd = P.v("spe")
        war_csc = [("spe", bd)]
        P.emit("scalar", lambda e, lh=lh: e.activation(negbias[:, lh:lh + 1], pb[7][:, 0:1], AF.Copy),
               waits=[("spe", bd)], inc="sact")
        war_bank[7] = [("sact", P.v("sact"))]
    NEGB_ACT = P.v("sact")

    SQW0_ALL = P.v("sqw0")
    SQW1_ALL = P.v("sqw1")

    P.barrier()

    war_qtm = [[], []]
    war_qfm = [[], []]
    war_attnT = [[], []]
    war_avsb = [[], []]
    war_pb4 = []
    war_sc = {0: [], 1: []}
    war_av = []
    war_D = []
    war_dbc = []
    for r in range(NR):
        for lh in range(NHL):
            g = lh // 4
            par = (r * NHL + lh) % 2
            qrp = f"sqr{par}"
            P.emit("sync", lambda e, r=r, lh=lh, par=par: e.dma_start(
                out=qtm[par].rearrange("p (t c) -> p t c", c=128),
                in_=qdram.rearrange("(t p) c -> p t c", p=128)[:, 4 * r:4 * r + 4, lh * 128:(lh + 1) * 128]),
                waits=[("sqw0", SQW0_ALL), ("sqw1", SQW1_ALL)] + war_qtm[par], inc=qrp, incby=16)
            qld = (qrp, P.v(qrp))
            for i in range(4):
                P.emit("tensor", lambda e, par=par, i=i: e.transpose(
                    pb16[4][:, i * 128:(i + 1) * 128], qtm[par][:, i * 128:(i + 1) * 128],
                    ident_sb[:, :]),
                    waits=([qld] + war_pb4 + war_qfm[par]) if i == 0 else [],
                    inc=("spe" if i == 3 else None))
            war_qtm[par] = [("spe", P.v("spe"))]
            qtr = P.v("spe")
            P.emit("vector", lambda e, par=par: e.tensor_copy(qfm[par][:, :], pb16[4][:, 0:512]),
                   waits=[("spe", qtr)], inc="sdve")
            war_pb4 = [("sdve", P.v("sdve"))]
            qfm_ready = P.v("sdve")

            kmax = 4 * r + 4
            exp_done = [0] * kmax
            mask_done = [0] * kmax
            pend = []

            def emit_dav(kt, last, g=g):
                at = kt % 2
                wts = [("sact", exp_done[kt])]
                if mask_done[kt]:
                    wts.append(("sdve", mask_done[kt]))
                if kt == 0:
                    wts += war_D + war_av
                P.emit("tensor", lambda e, at=at, kt=kt, last=last: e.matmul(
                    pb[3][0:1, :], ones_c16[:, :], attnT[at][:, :],
                    start=(kt == 0), stop=last), waits=wts)
                P.emit("tensor", lambda e, at=at, kt=kt, g=g, last=last: e.matmul(
                    pb[2][:, :], kv_tm[:, kt * 512 + 256 + g * 128:kt * 512 + 256 + (g + 1) * 128],
                    attnT[at][:, :], start=(kt == 0), stop=last), inc="spe")
                war_attnT[at] = [("spe", P.v("spe"))]

            for kt in range(kmax):
                sb_ = kt % 2
                P.emit("tensor", lambda e, g=g, kt=kt, sb_=sb_, par=par: e.matmul(
                    pb[sb_][:, :], kfm[:, (g * NT + kt) * 128:(g * NT + kt + 1) * 128],
                    qfm[par][:, :], start=True, stop=True),
                    waits=[("sdve", qfm_ready)] + war_sc[sb_], inc="spe")
                qk_done = P.v("spe")
                at = kt % 2
                P.emit("scalar", lambda e, sb_=sb_, at=at, lh=lh: e.activation(
                    attnT[at][:, :], pb[sb_][:, :], AF.Exp,
                    bias=negbias[:, lh:lh + 1], scale=SCALE),
                    waits=[("spe", qk_done), ("sact", NEGB_ACT)] + war_attnT[at], inc="sact")
                exp_done[kt] = P.v("sact")
                war_sc[sb_] = [("sact", exp_done[kt])]
                if kt >= 4 * r:
                    dm = kt - 4 * r
                    P.emit("vector", lambda e, at=at, dm=dm: e.tensor_mul(
                        attnT[at][:, :], attnT[at][:, :], masks_sb[:, dm * 512:(dm + 1) * 512]),
                        waits=[("sact", exp_done[kt]), ("sk", SK_MASK)], inc="sdve")
                    mask_done[kt] = P.v("sdve")
                pend.append(kt)
                if len(pend) >= 2:
                    emit_dav(pend.pop(0), False)
            emit_dav(pend.pop(0), True)
            av_done = P.v("spe")

            P.emit("vector", lambda e: e.reciprocal(recipD[:, :], pb[3][0:1, :]),
                   waits=[("spe", av_done)], inc="sdve")
            rd = P.v("sdve")
            war_D = [("sdve", rd)]
            P.emit("tensor", lambda e: e.matmul(pb[5][:, :], ones_rf[:, :], recipD[:, :],
                                                start=True, stop=True),
                   waits=[("sdve", rd)] + war_dbc, inc="spe")
            bc = P.v("spe")
            P.emit("scalar", lambda e: e.activation(dbc_sb[:, :], pb[5][:, :], AF.Copy),
                   waits=[("spe", bc)], inc="sact")
            dbce = P.v("sact")
            war_dbc = [("sact", dbce)]
            P.emit("vector", lambda e, par=par: e.tensor_mul(av_sb[par][:, :], pb[2][:, :], dbc_sb[:, :]),
                   waits=[("sact", dbce), ("spe", av_done)] + war_avsb[par], inc="sdve")
            avn = P.v("sdve")
            war_av = [("sdve", avn)]
            avp = f"savw{par}"
            P.emit("sync", lambda e, lh=lh, r=r, par=par: e.dma_start(
                out=cc2_in[lh * 128:(lh + 1) * 128, r * 512:(r + 1) * 512], in_=av_sb[par][:, :]),
                waits=[("sdve", avn)], inc=avp, incby=16)
            war_avsb[par] = [(avp, P.v(avp))]
            war_qfm[par] = [("spe", av_done)]

    P.emit("gpsimd", lambda e: e.collective_compute(
        "AllGather", ALU.bypass, replica_groups=[[0, 1, 2, 3], [4, 5, 6, 7]],
        ins=[cc2_in], outs=[cc2_out]),
        waits=[("savw0", P.v("savw0")), ("savw1", P.v("savw1"))], inc="scc")

    P.barrier()

    P.emit("vector", lambda e: e.tensor_copy(idxf[:, :], lsi_sb[:, :]),
           waits=[("sk", SK_LSI)], inc="sdve")
    P.emit("vector", lambda e: e.tensor_scalar(
        idxf[:, :], idxf[:, :], -1.0, bp1_sb[0:1, 0:1], ALU.mult, ALU.add),
        waits=[("sdve", P.v("sdve")), ("sk", SK_BP1)], inc="sdve")
    P.emit("vector", lambda e: e.tensor_scalar(
        idxf[:, :], idxf[:, :], 0.0, 1.0, ALU.max, ALU.min),
        waits=[("sdve", P.v("sdve"))], inc="sdve")
    gv = P.v("sdve")
    g0, g1, g2, g3 = (idxf[0:1, i:i + 1] for i in range(4))
    def _gstep(fn):
        P.emit("vector", fn, waits=[("sdve", P.v("sdve"))], inc="sdve")
    _gstep(lambda e: e.tensor_scalar(gsc[0:1, 0:1], g1, -1.0, 1.0, ALU.mult, ALU.add))
    _gstep(lambda e: e.tensor_mul(gsc[0:1, 1:2], g0, gsc[0:1, 0:1]))
    _gstep(lambda e: e.tensor_scalar(gsc[0:1, 2:3], g1, 2.0, None, ALU.mult))
    _gstep(lambda e: e.tensor_add(gsc[0:1, 1:2], gsc[0:1, 1:2], gsc[0:1, 2:3]))
    _gstep(lambda e: e.tensor_scalar(gsc[0:1, 3:4], g2, -1.0, 1.0, ALU.mult, ALU.add))
    _gstep(lambda e: e.tensor_mul(gsc[0:1, 1:2], gsc[0:1, 1:2], gsc[0:1, 3:4]))
    _gstep(lambda e: e.tensor_scalar(gsc[0:1, 4:5], g3, 1.0, 3.0, ALU.mult, ALU.add))
    _gstep(lambda e: e.tensor_mul(gsc[0:1, 4:5], gsc[0:1, 4:5], g2))
    _gstep(lambda e: e.tensor_add(gsc[0:1, 1:2], gsc[0:1, 1:2], gsc[0:1, 4:5]))
    rv = P.v("sdve")
    for j in range(3):
        _gstep(lambda e, j=j: e.tensor_scalar(
            gsc[0:1, 5:6], gsc[0:1, 1:2], 1.0, -float(j + 2), ALU.mult, ALU.add))
        _gstep(lambda e: e.tensor_mul(gsc[0:1, 6:7], gsc[0:1, 5:6], gsc[0:1, 5:6]))
        _gstep(lambda e: e.tensor_scalar(gsc[0:1, 6:7], gsc[0:1, 6:7], -1.0, 1.0, ALU.mult, ALU.add))
        _gstep(lambda e: e.tensor_scalar(gsc[0:1, 6:7], gsc[0:1, 6:7], 0.0, 1.0, ALU.max, ALU.min))
        _gstep(lambda e, j=j: e.tensor_scalar_mul(
            gex[0:1, 16 * j:16 * (j + 1)], ones_rf[0:1, 0:16], gsc[0:1, 6:7]))
    gexd = P.v("sdve")
    P.emit("tensor", lambda e: e.matmul(pb[7][0:48, 0:1], gex[:, :], ones_rf[0:1, 0:1],
                                        start=True, stop=True),
           waits=[("sdve", gexd)] + war_bank[7], inc="spe")
    gb = P.v("spe")
    P.emit("scalar", lambda e: e.activation(gmask[:, :], pb[7][0:48, 0:1], AF.Copy),
           waits=[("spe", gb)], inc="sact")
    GMASK_ACT = P.v("sact")
    war_bank[7] = [("sact", GMASK_ACT)]

    P.emit("gpsimd", lambda e: e.dma_start(out=bias16[:, :], in_=bo[:, :]), inc="sw", incby=16)
    P.emit("gpsimd", lambda e: e.dma_start(out=lBT_sb[:, :], in_=lBT[:, :]), inc="sw", incby=16)
    P.emit("gpsimd", lambda e: e.dma_start(
        out=lAT_sb.rearrange("p (j r) -> p j r", r=48),
        in_=lAT.rearrange("(j p) r -> p j r", p=128)), inc="sw", incby=16)
    CCONST = P.v("sw")

    war_chk = [[], []]
    for ft in range(NT):
        for j in range(4):
            cb = (ft * 4 + j) % 2
            ccp = f"sccr{cb}"
            P.emit("sync", lambda e, ft=ft, j=j, cb=cb: e.dma_start(
                out=chkb[cb][:, :],
                in_=cc2_out[ft * 128:(ft + 1) * 128, j * 1024:(j + 1) * 1024]),
                waits=[("scc", 1)] + war_chk[cb], inc=ccp, incby=16)
            ld = (ccp, P.v(ccp))
            if j == 0:
                P.emit("vector", lambda e, ft=ft, cb=cb: e.tensor_scalar_mul(
                    av_fm[:, ft * 1024:(ft + 1) * 1024], chkb[cb][:, :], sel_sb[:, 0:1]),
                    waits=[ld, ("sk", SK_SEL)], inc="sdve")
            else:
                P.emit("vector", lambda e, ft=ft, j=j, cb=cb: e.scalar_tensor_tensor(
                    av_fm[:, ft * 1024:(ft + 1) * 1024], chkb[cb][:, :], sel_sb[:, j:j + 1],
                    av_fm[:, ft * 1024:(ft + 1) * 1024], ALU.mult, ALU.add),
                    waits=[ld, ("sdve", P.v("sdve"))], inc="sdve")
            war_chk[cb] = [("sdve", P.v("sdve"))]
    AVSEL = P.v("sdve")

    war_wc = [[], []]
    war_xo = [[], []]
    war_cbank = {0: [], 1: []}
    for fo in range(NFO):
        wb = fo % 2
        wcp = f"swc{wb}"
        P.emit("gpsimd", lambda e, fo=fo, wb=wb: e.dma_start(
            out=wbufC[wb].rearrange("p (j n) -> p j n", n=256),
            in_=woT.rearrange("(j p) n -> p j n", p=128)[:, :, fo * 256:(fo + 1) * 256]),
            waits=war_wc[wb], inc=wcp, incby=16)
        wld = (wcp, P.v(wcp))
        for ts in range(8):
            xb_ = (fo * 8 + ts) % 2
            xop = f"sxo{xb_}"
            P.emit("sync", lambda e, fo=fo, ts=ts, xb_=xb_: e.dma_start(
                out=xoC[xb_][:, :],
                in_=x_own[ts * 128:(ts + 1) * 128, fo * 256:(fo + 1) * 256]),
                waits=war_xo[xb_], inc=xop, incby=16)
            xld = (xop, P.v(xop))
            bank = ts % 2
            for dt in range(ND):
                P.emit("tensor", lambda e, dt=dt, ts=ts, wb=wb, bank=bank: e.matmul(
                    pb[bank][:, 0:256], av_fm[:, dt * 1024 + ts * 128:dt * 1024 + (ts + 1) * 128],
                    wbufC[wb][:, dt * 256:(dt + 1) * 256], start=(dt == 0), stop=False),
                    waits=([("sdve", AVSEL), wld] + war_cbank[bank]) if dt == 0 else [])
            P.emit("tensor", lambda e, fo=fo, bank=bank: e.matmul(
                pb[bank][:, 0:256], ones_r16[:, :], bias16[:, fo * 256:(fo + 1) * 256],
                start=False, stop=True),
                waits=[("sw", CCONST)], inc="spe")
            mmd = P.v("spe")
            P.emit("vector", lambda e, ts=ts, fo=fo, xb_=xb_, bank=bank: e.tensor_add(
                x2[:, ts * D + fo * 256:ts * D + (fo + 1) * 256],
                pb[bank][:, 0:256], xoC[xb_][:, :]),
                waits=[("spe", mmd), xld], inc="sdve")
            war_cbank[bank] = [("sdve", P.v("sdve"))]
            war_xo[xb_] = [("sdve", P.v("sdve"))]
        war_wc[wb] = [("spe", P.v("spe"))]
    C1_DVE = P.v("sdve")
    C1_PE = P.v("spe")
    P.emit("gpsimd", lambda e: e.dma_start(out=bias16[:, :], in_=bffn[:, :]),
           waits=[("spe", C1_PE)], inc="sw", incby=16)
    BFFN_SW = P.v("sw")

    war_ln2tm = [("spe", 0)]
    war_p = []
    war_stats2 = []
    for ts in range(8):
        xs = x2[:, ts * D:(ts + 1) * D]
        P.emit("vector", lambda e, xs=xs: e.reduce_sum(st_sum[:, :], xs, axis=AX.X),
               waits=[("sdve", C1_DVE)] + war_stats2, inc="sdve")
        P.emit("scalar", lambda e, xs=xs: e.activation(
            ln2_tm[:, :], xs, AF.Square, accum_out=st_sq[:, :]),
            waits=[("sdve", C1_DVE), ("spe", C1_PE), war_ln2tm[0]], inc="sact")
        sqd = P.v("sact")
        P.emit("vector", lambda e: e.tensor_scalar_mul(st_m[:, :], st_sum[:, :], 1.0 / D),
               waits=[("sdve", P.v("sdve"))], inc="sdve")
        P.emit("vector", lambda e: e.tensor_mul(st_msq[:, :], st_m[:, :], st_m[:, :]),
               waits=[("sdve", P.v("sdve"))], inc="sdve")
        P.emit("vector", lambda e: e.scalar_tensor_tensor(
            st_var[:, :], st_sq[:, :], 1.0 / D, st_msq[:, :], ALU.mult, ALU.subtract),
            waits=[("sact", sqd), ("sdve", P.v("sdve"))], inc="sdve")
        vd = P.v("sdve")
        P.emit("scalar", lambda e: e.activation(st_rstd[:, :], st_var[:, :], AF.Ln, bias=epscol[:, :]),
               waits=[("sdve", vd)])
        P.emit("scalar", lambda e: e.activation(st_rstd[:, :], st_rstd[:, :], AF.Exp, scale=-0.5),
               inc="sact")
        rsd = P.v("sact")
        P.emit("vector", lambda e: e.scalar_tensor_tensor(
            st_negmr[:, :], st_m[:, :], -1.0, st_rstd[:, :], ALU.mult, ALU.mult),
            waits=[("sact", rsd), ("sdve", vd)], inc="sdve")
        nmd = P.v("sdve")
        P.emit("scalar", lambda e, xs=xs: e.activation(
            ln2_tm[:, :], xs, AF.Identity, bias=st_negmr[:, :], scale=st_rstd[:, :]),
            waits=[("sdve", nmd), ("sact", rsd)], inc="sact")
        appd = P.v("sact")
        war_stats2 = [("sact", appd)]
        for k in range(8):
            bank2 = 6 + (k % 2)
            for i in range(4):
                dt = 4 * k + i
                P.emit("tensor", lambda e, dt=dt, bank2=bank2, i=i: e.transpose(
                    pb16[bank2][:, i * 128:(i + 1) * 128], ln2_tm[:, dt * 128:(dt + 1) * 128],
                    ident_sb[:, :]),
                    waits=([("sact", appd)] + war_bank[bank2]) if i == 0 else [],
                    inc=("spe" if i == 3 else None))
            trd = P.v("spe")
            P.emit("vector", lambda e, k=k, ts=ts, bank2=bank2: e.tensor_copy(
                ln2_fm.rearrange("p (j t) -> p j t", t=1024)[:, 4 * k:4 * k + 4, ts * 128:(ts + 1) * 128],
                pb16[bank2][:, 0:512].rearrange("p (j t) -> p j t", t=128)),
                waits=[("spe", trd)], inc="sdve")
            war_bank[bank2] = [("sdve", P.v("sdve"))]
        war_ln2tm = [("spe", P.v("spe"))]
        ln2_done = P.v("sdve")
        for dt in range(ND):
            P.emit("tensor", lambda e, dt=dt, ts=ts: e.matmul(
                pb[2][0:48, 0:128], lAT_sb[:, dt * 48:(dt + 1) * 48],
                ln2_fm[:, dt * 1024 + ts * 128:dt * 1024 + (ts + 1) * 128],
                start=(dt == 0), stop=(dt == ND - 1)),
                waits=([("sdve", ln2_done), ("sw", CCONST), ("sact", GMASK_ACT)] + war_p) if dt == 0 else [],
                inc=("spe" if dt == ND - 1 else None))
        pd = P.v("spe")
        P.emit("vector", lambda e, ts=ts: e.tensor_scalar_mul(
            p_sb[:, ts * 128:(ts + 1) * 128], pb[2][0:48, 0:128], gmask[:, 0:1]),
            waits=[("spe", pd), ("sact", GMASK_ACT)], inc="sdve")
        war_p = [("sdve", P.v("sdve"))]
    LN2_ALL = P.v("sdve")
    P_ALL = P.v("sdve")

    war_wc2 = [war_wc[0], war_wc[1]]
    war_out = [[], []]
    war_cbank2 = {0: war_cbank[0], 1: war_cbank[1]}
    for fo in range(NFO):
        wb = fo % 2
        wts = list(war_wc2[wb])
        if fo == 0:
            wts += [war_ln2tm[0]]
        wcp = f"swc{wb}"
        P.emit("gpsimd", lambda e, fo=fo, wb=wb: e.dma_start(
            out=wbufC[wb].rearrange("p (j n) -> p j n", n=256),
            in_=ffnT.rearrange("(j p) n -> p j n", p=128)[:, :, fo * 256:(fo + 1) * 256]),
            waits=wts, inc=wcp, incby=16)
        wld = (wcp, P.v(wcp))
        for ts in range(8):
            bank = ts % 2
            for dt in range(ND):
                P.emit("tensor", lambda e, dt=dt, ts=ts, wb=wb, bank=bank: e.matmul(
                    pb[bank][:, 0:256], ln2_fm[:, dt * 1024 + ts * 128:dt * 1024 + (ts + 1) * 128],
                    wbufC[wb][:, dt * 256:(dt + 1) * 256], start=(dt == 0), stop=False),
                    waits=([("sdve", LN2_ALL), wld] + war_cbank2[bank]) if dt == 0 else [])
            P.emit("tensor", lambda e, fo=fo, bank=bank: e.matmul(
                pb[bank][:, 0:256], ones_r16[:, :], bias16[:, fo * 256:(fo + 1) * 256],
                start=False, stop=False),
                waits=[("sw", BFFN_SW)])
            P.emit("tensor", lambda e, fo=fo, ts=ts, bank=bank: e.matmul(
                pb[bank][:, 0:256], p_sb[:, ts * 128:(ts + 1) * 128],
                lBT_sb[:, fo * 256:(fo + 1) * 256], start=False, stop=True),
                waits=[("sdve", P_ALL)], inc="spe")
            mmd = P.v("spe")
            ob = (fo * 8 + ts) % 2
            P.emit("vector", lambda e, ts=ts, fo=fo, bank=bank, ob=ob: e.tensor_add(
                outsb[ob][:, :], pb[bank][:, 0:256],
                x2[:, ts * D + fo * 256:ts * D + (fo + 1) * 256]),
                waits=[("spe", mmd)] + war_out[ob], inc="sdve")
            od = P.v("sdve")
            war_cbank2[bank] = [("sdve", od)]
            sop = f"sout{ob}"
            P.emit("sync", lambda e, ts=ts, fo=fo, ob=ob: e.dma_start(
                out=out[ts * 128:(ts + 1) * 128, fo * 256:(fo + 1) * 256], in_=outsb[ob][:, :]),
                waits=[("sdve", od)], inc=sop, incby=16)
            war_out[ob] = [(sop, P.v(sop))]
        war_wc2[wb] = [("spe", P.v("spe"))]

    P.emit("sync", lambda e: e.nop(), waits=[("sout0", P.v("sout0")), ("sout1", P.v("sout1"))])

    P.replay()
    return nc


_NC_CACHE = None


def kernel(L0_attn_x, L0_attn_w_qkv, L0_attn_b_qkv, L0_attn_w_o, L0_attn_b_o,
           L0_ffn_weight, L0_ffn_bias, L0_ffn_lora_A, L0_ffn_lora_B,
           L0_ffn_lora_start_indices):
    global _NC_CACHE
    x = np.asarray(L0_attn_x, np.float32)
    w_qkv = np.asarray(L0_attn_w_qkv, np.float32)
    b_qkv = np.asarray(L0_attn_b_qkv, np.float32)
    w_o = np.asarray(L0_attn_w_o, np.float32)
    b_o = np.asarray(L0_attn_b_o, np.float32)
    w_ffn = np.asarray(L0_ffn_weight, np.float32)
    b_ffn = np.asarray(L0_ffn_bias, np.float32)
    lora_A = np.asarray(L0_ffn_lora_A, np.float32)
    lora_B = np.asarray(L0_ffn_lora_B, np.float32)
    lsi = np.asarray(L0_ffn_lora_start_indices).astype(np.int32, copy=False)
    lsi = np.ascontiguousarray(lsi)

    woT = np.ascontiguousarray(w_o.T)
    ffnT = np.ascontiguousarray(w_ffn.T)
    lAT = np.ascontiguousarray(lora_A.transpose(2, 0, 1).reshape(4096, 48))
    lBT = np.ascontiguousarray(lora_B.transpose(0, 2, 1).reshape(48, 4096))

    inv_freq = (1.0 / 10000.0 ** (np.arange(0, HD, 2, dtype=np.float32) / HD)).astype(np.float32)
    pos = np.arange(S, dtype=np.float32)
    ang = pos[:, None] * inv_freq[None, :]
    cosT = np.cos(ang).astype(np.float32)
    sinT = np.sin(ang).astype(np.float32)
    ident = np.eye(128, dtype=ml_dtypes.bfloat16)
    kk = np.arange(128)[:, None]
    qq = np.arange(512)[None, :]
    mask4 = np.stack([(kk + 128 * d <= qq) for d in range(4)]).astype(ml_dtypes.bfloat16)

    if _NC_CACHE is None:
        _NC_CACHE = build_nc()
    nc = _NC_CACHE

    xb = [np.ascontiguousarray(x[0]), np.ascontiguousarray(x[1])]
    in_maps = []
    for c in range(8):
        B, m = c // 4, c % 4
        rows = np.concatenate([
            np.arange(m * QC, (m + 1) * QC),
            np.arange(4096 + m * KVC, 4096 + (m + 1) * KVC),
            np.arange(5120 + m * KVC, 5120 + (m + 1) * KVC),
        ])
        wq = np.ascontiguousarray(w_qkv[rows].T)
        bq = np.ascontiguousarray(b_qkv[rows])[None, :]
        sel = np.zeros((128, 4), np.float32)
        sel[:, m] = 1.0
        in_maps.append({
            "x_batch": xb[B],
            "x_own": np.ascontiguousarray(xb[B][m * 1024:(m + 1) * 1024]),
            "wqkvT": wq, "bqkv": bq,
            "woT": woT, "bo": b_o[None, :],
            "ffnT": ffnT, "bffn": b_ffn[None, :],
            "lAT": lAT, "lBT": lBT, "lsi": lsi,
            "bplus1": np.array([[float(B + 1)]], np.float32),
            "selmask": sel,
            "cosT": cosT, "sinT": sinT,
            "identt": ident, "mask4": mask4,
        })

    res = run_bass_kernel_spmd(nc, in_maps, core_ids=list(range(8)))
    full = np.empty((2, 4096, 4096), np.float32)
    for c in range(8):
        B, m = c // 4, c % 4
        full[B, m * 1024:(m + 1) * 1024] = res.results[c]["out"]
    return full


# version 04
# speedup vs baseline: 1.1339x; 1.1339x over previous
import math
import numpy as np
import ml_dtypes
import concourse.bass as bass
import concourse.mybir as mybir
from concourse.bass_utils import run_bass_kernel_spmd

F32 = mybir.dt.float32
BF16 = mybir.dt.bfloat16
I32 = mybir.dt.int32
U8 = mybir.dt.uint8
AF = mybir.ActivationFunctionType
ALU = mybir.AluOpType
AX = mybir.AxisListType

D = 4096
S = 4096
HD = 128
NHL = 8
QC = NHL * HD
KVC = 256
COLS = QC + 2 * KVC
NT = S // 128
ND = D // 128
NR = S // 512
NFO = 8
SCALE = 1.0 / math.sqrt(HD)
LN_EPS = 1e-5


class Plan:

    ENGS = ("sync", "gpsimd", "vector", "scalar", "tensor")

    def __init__(self, nc):
        self.nc = nc
        self.q = {e: [] for e in self.ENGS}
        self.h = {}
        self.c = {}
        self.cuts = []

    def sem(self, name):
        self.h[name] = self.nc.alloc_semaphore(name)
        self.c[name] = 0

    def emit(self, eng, fn, waits=(), inc=None, incby=1):
        if inc is not None:
            self.c[inc] += incby
        self.q[eng].append((tuple(waits), fn, inc, incby))

    def v(self, name):
        return self.c[name]

    def barrier(self):
        self.cuts.append({e: len(self.q[e]) for e in self.ENGS})

    def replay(self):
        nc = self.nc
        self.cuts.append({e: len(self.q[e]) for e in self.ENGS})
        start = {e: 0 for e in self.ENGS}
        lastw = {e: {} for e in self.ENGS}
        for seg_i, cut in enumerate(self.cuts):
            with nc.Block(f"ph{seg_i}") as block:
                for ename in self.ENGS:
                    items = self.q[ename][start[ename]:cut[ename]]
                    if not items:
                        continue

                    def make_body(items=items, lw=lastw[ename], plan=self):
                        def body(eng):
                            for waits, fn, inc, incby in items:
                                for sname, val in waits:
                                    if val > 0 and lw.get(sname, 0) < val:
                                        eng.wait_ge(plan.h[sname], val)
                                        lw[sname] = val
                                inst = fn(eng)
                                if inc is not None:
                                    inst.then_inc(plan.h[inc], incby)
                        return body

                    getattr(block, ename)(make_body())
            start = dict(cut)
            if seg_i < len(self.cuts) - 1:
                nc.all_engine_barrier()


class Carver:
    def __init__(self, arena_ap, nbytes):
        self.a = arena_ap
        self.n = nbytes
        self.cur = 0

    def take(self, nbytes, dtype, parts=128, base=0):
        off = (self.cur + 31) & ~31
        self.cur = off + nbytes
        assert self.cur <= self.n, f"arena overflow: {self.cur} > {self.n}"
        return self.a[base:base + parts, off:off + nbytes].bitcast(dtype)

    def mark(self):
        return self.cur

    def reset(self, pos):
        self.cur = pos


def build_nc():
    nc = bass.Bass("TRN2", target_bir_lowering=False, debug=False)
    ap = lambda t: t.ap()

    x_batch = ap(nc.dram_tensor("x_batch", [S, D], F32, kind="ExternalInput"))
    x_own = ap(nc.dram_tensor("x_own", [1024, D], F32, kind="ExternalInput"))
    wqkvT = ap(nc.dram_tensor("wqkvT", [D, COLS], F32, kind="ExternalInput"))
    bqkv = ap(nc.dram_tensor("bqkv", [1, COLS], F32, kind="ExternalInput"))
    woT = ap(nc.dram_tensor("woT", [D, D], F32, kind="ExternalInput"))
    bo = ap(nc.dram_tensor("bo", [1, D], F32, kind="ExternalInput"))
    ffnT = ap(nc.dram_tensor("ffnT", [D, D], F32, kind="ExternalInput"))
    bffn = ap(nc.dram_tensor("bffn", [1, D], F32, kind="ExternalInput"))
    lAT = ap(nc.dram_tensor("lAT", [D, 48], F32, kind="ExternalInput"))
    lBT = ap(nc.dram_tensor("lBT", [48, D], F32, kind="ExternalInput"))
    lsi = ap(nc.dram_tensor("lsi", [4], I32, kind="ExternalInput"))
    bplus1 = ap(nc.dram_tensor("bplus1", [1, 1], F32, kind="ExternalInput"))
    selmask = ap(nc.dram_tensor("selmask", [128, 4], F32, kind="ExternalInput"))
    cosT = ap(nc.dram_tensor("cosT", [S, 64], F32, kind="ExternalInput"))
    sinT = ap(nc.dram_tensor("sinT", [S, 64], F32, kind="ExternalInput"))
    identt = ap(nc.dram_tensor("identt", [128, 128], BF16, kind="ExternalInput"))
    mask4 = ap(nc.dram_tensor("mask4", [4, 128, 512], BF16, kind="ExternalInput"))
    out = ap(nc.dram_tensor("out", [1024, D], F32, kind="ExternalOutput"))

    ln1f = ap(nc.dram_tensor("ln1f", [D, S], BF16, kind="Internal"))
    qdram = ap(nc.dram_tensor("qdram", [S, QC], BF16, kind="Internal"))
    cc2_in = ap(nc.dram_tensor("cc2_in", [4 * QC, 1024], BF16, kind="Internal"))
    cc2_all = ap(nc.dram_tensor("cc2_all", [4, 4 * QC, 1024], BF16, kind="Internal"))
    x2d = ap(nc.dram_tensor("x2d", [1024, D], BF16, kind="Internal"))

    A = lambda *a, **k: nc.alloc_sbuf_tensor(*a, **k).ap()
    ident_sb = A("ident_sb", [128, 128], BF16)
    ones_r16 = A("ones_r16", [1, 128], BF16)
    ones_c16 = A("ones_c16", [128, 1], BF16)
    ones_rf = A("ones_rf", [1, 128], F32)
    epscol = A("epscol", [128, 1], F32)
    eps30 = A("eps30", [1, 1], F32)
    sel_sb = A("sel_sb", [128, 4], F32)
    bp1_sb = A("bp1_sb", [1, 1], F32)
    lsi_sb = A("lsi_sb", [1, 4], I32)
    idxf = A("idxf", [1, 4], F32)
    gsc = A("gsc", [1, 16], F32)
    gex = A("gex", [1, 48], F32)
    gmask = A("gmask", [48, 1], F32)
    normsq = A("normsq", [128, 10 * NT], F32)
    nmax = A("nmax", [128, 1], F32)
    nmax16 = A("nmax16", [128, 1], BF16)
    maxsq = A("maxsq", [1, 16], F32)
    negbias = A("negbias", [128, NHL], F32)
    sums = A("sums", [128, NT], F32)
    sqs = A("sqs", [128, NT], F32)
    mall = A("mall", [128, NT], F32)
    msqall = A("msqall", [128, NT], F32)
    varall = A("varall", [128, NT], F32)
    rstdall = A("rstdall", [128, NT], F32)
    negmrall = A("negmrall", [128, NT], F32)
    csc = A("csc", [1, 2], F32)
    recipD2 = [A(f"recipD{i}", [1, 512], F32) for i in range(2)]
    dbc_sb = A("dbc_sb", [128, 512], F32)
    masks_sb = A("masks_sb", [128, 4 * 512], BF16)
    sqscr = A("sqscr", [128, 512], BF16)

    ARENA = 190 * 1024
    arena = A("arena", [128, ARENA], U8)
    cv = Carver(arena, ARENA)

    kv_tm = cv.take(NT * 512 * 2, BF16)
    kfm = cv.take(2 * S * 2, BF16)
    ab_end = cv.mark()

    xh = [cv.take(D * 4, F32) for _ in range(2)]
    ln1_tm = [cv.take(D * 2, BF16) for _ in range(2)]
    evacb = [cv.take(D * 2, BF16) for _ in range(2)]
    wbufA = [cv.take(ND * 512 * 2, BF16)]
    lbuf = [cv.take(ND * 128 * 2, BF16) for _ in range(2)]
    qstage = [cv.take(512 * 2, BF16) for _ in range(2)]
    t1f = cv.take(256 * 4, F32)
    t2f = cv.take(256 * 4, F32)
    cos_sb = cv.take(NT * 64 * 4, F32)
    sin_sb = cv.take(NT * 64 * 4, F32)
    bqkv16 = cv.take(COLS * 2, BF16, parts=1)

    cv.reset(ab_end)
    qtm = [cv.take(512 * 2, BF16) for _ in range(2)]
    qfm = [cv.take(512 * 2, BF16) for _ in range(2)]
    attnT = [cv.take(512 * 2, BF16) for _ in range(4)]
    av_sb = [cv.take(512 * 2, BF16) for _ in range(2)]

    cv.reset(0)
    av_fm = cv.take(NT * 1024 * 2, BF16)
    wbufC = [cv.take(ND * 512 * 2, BF16) for _ in range(2)]
    ln2_tm = wbufC[0][:, 0:D]
    x2s = [cv.take(D * 2, BF16) for _ in range(2)]
    x2cw = [cv.take(512 * 2, BF16) for _ in range(2)]
    x2cr = [cv.take(512 * 2, BF16) for _ in range(2)]
    xoC = [cv.take(512 * 4, F32) for _ in range(2)]
    chkb = [cv.take(1024 * 2, BF16) for _ in range(2)]
    p_sb = cv.take(1024 * 2, BF16, parts=48)
    lBT_sb = cv.take(D * 2, BF16, parts=48)
    lAT_sb = cv.take(ND * 48 * 2, BF16)
    bias16 = cv.take(D * 2, BF16, parts=1)
    outsb = [cv.take(512 * 4, F32) for _ in range(2)]
    ln2_fm = av_fm

    pb = [nc.alloc_psum_tensor(f"pb{i}", [128, 512], F32).ap() for i in range(8)]
    pb16 = [p.bitcast(BF16) for p in pb]

    P = Plan(nc)
    for s in ("sx0", "sx1", "slnw0", "slnw1", "slnr0", "slnr1", "sw",
              "swc0", "swc1", "sqw0", "sqw1", "sqr0", "sqr1", "savw0", "savw1",
              "scc0", "scc1", "scc2", "scc3", "sccr0", "sccr1",
              "sxo0", "sxo1", "sx2w0", "sx2w1", "sx2r0", "sx2r1",
              "sx2c0", "sx2c1", "sk", "sout0", "sout1",
              "spe", "sact", "sdve"):
        P.sem(s)

    war_bank = {i: [] for i in range(8)}

    P.emit("sync", lambda e: e.dma_start(out=ident_sb[:, :], in_=identt[:, :]), inc="sk", incby=16)
    P.emit("sync", lambda e: e.dma_start(out=sel_sb[:, :], in_=selmask[:, :]), inc="sk", incby=16)
    P.emit("sync", lambda e: e.dma_start(out=bp1_sb[:, :], in_=bplus1[:, :]), inc="sk", incby=16)
    P.emit("sync", lambda e: e.dma_start(out=lsi_sb[:, :], in_=lsi.unsqueeze(0)), inc="sk", incby=16)
    P.emit("sync", lambda e: e.dma_start(
        out=masks_sb.rearrange("p (d q) -> p d q", d=4),
        in_=mask4.rearrange("d p q -> p d q")), inc="sk", incby=16)
    P.emit("sync", lambda e: e.dma_start(
        out=cos_sb.rearrange("p (t f) -> p t f", f=64),
        in_=cosT.rearrange("(t p) f -> p t f", p=128)), inc="sk", incby=16)
    P.emit("sync", lambda e: e.dma_start(
        out=sin_sb.rearrange("p (t f) -> p t f", f=64),
        in_=sinT.rearrange("(t p) f -> p t f", p=128)), inc="sk", incby=16)
    SK_ALL = P.v("sk")
    P.emit("gpsimd", lambda e: e.dma_start(out=bqkv16[:, :], in_=bqkv[:, :]), inc="sw", incby=16)
    P.emit("vector", lambda e: e.memset(ones_r16[:, :], 1.0))
    P.emit("vector", lambda e: e.memset(ones_c16[:, :], 1.0))
    P.emit("vector", lambda e: e.memset(ones_rf[:, :], 1.0))
    P.emit("vector", lambda e: e.memset(epscol[:, :], LN_EPS))
    P.emit("vector", lambda e: e.memset(eps30[:, :], 1e-30), inc="sdve")
    DVE_CONST = P.v("sdve")

    war_x = [[], []]
    for st in range(NT):
        bi = st % 2
        sxp = f"sx{bi}"
        for hf in range(2):
            P.emit("sync", lambda e, st=st, hf=hf, bi=bi: e.dma_start(
                out=xh[bi][:, hf * 2048:(hf + 1) * 2048],
                in_=x_batch[st * 128:(st + 1) * 128, hf * 2048:(hf + 1) * 2048]),
                waits=war_x[bi], inc=sxp, incby=16)
        xload = P.v(sxp)
        P.emit("vector", lambda e, bi=bi, st=st: e.reduce_sum(
            sums[:, st:st + 1], xh[bi][:, :], axis=AX.X),
            waits=[(sxp, xload)], inc="sdve")
        P.emit("scalar", lambda e, bi=bi, st=st: e.activation(
            ln1_tm[bi][:, :], xh[bi][:, :], AF.Square, accum_out=sqs[:, st:st + 1]),
            waits=[(sxp, xload)], inc="sact")
        war_x[bi] = [("sdve", P.v("sdve")), ("sact", P.v("sact"))]
    P1_DVE = P.v("sdve")
    P1_ACT = P.v("sact")

    def ln_stats_batch(ncols):
        P.emit("vector", lambda e: e.tensor_scalar_mul(mall[:, 0:ncols], sums[:, 0:ncols], 1.0 / D),
               waits=[("sdve", P.v("sdve")), ("sact", P.v("sact"))], inc="sdve")
        P.emit("vector", lambda e: e.tensor_mul(msqall[:, 0:ncols], mall[:, 0:ncols], mall[:, 0:ncols]),
               waits=[("sdve", P.v("sdve"))], inc="sdve")
        P.emit("vector", lambda e: e.scalar_tensor_tensor(
            varall[:, 0:ncols], sqs[:, 0:ncols], 1.0 / D, msqall[:, 0:ncols],
            ALU.mult, ALU.subtract),
            waits=[("sdve", P.v("sdve"))], inc="sdve")
        vd = P.v("sdve")
        P.emit("scalar", lambda e: e.activation(rstdall[:, 0:ncols], varall[:, 0:ncols], AF.Ln, bias=epscol[:, :]),
               waits=[("sdve", vd)])
        P.emit("scalar", lambda e: e.activation(rstdall[:, 0:ncols], rstdall[:, 0:ncols], AF.Exp, scale=-0.5),
               inc="sact")
        rstd_done = P.v("sact")
        P.emit("vector", lambda e: e.scalar_tensor_tensor(
            negmrall[:, 0:ncols], mall[:, 0:ncols], -1.0, rstdall[:, 0:ncols], ALU.mult, ALU.mult),
            waits=[("sact", rstd_done), ("sdve", vd)], inc="sdve")
        return rstd_done, P.v("sdve")

    RSTD1_ACT, NEGMR1_DVE = ln_stats_batch(NT)

    war_x3 = [war_x[0], war_x[1]]
    war_lntm = [[], []]
    war_evacb = [[], []]
    for st in range(NT):
        bi = st % 2
        sxp = f"sx{bi}"
        P.emit("sync", lambda e, st=st, bi=bi: e.dma_start(
            out=xh[bi][:, :], in_=x_batch[st * 128:(st + 1) * 128, :]),
            waits=war_x3[bi], inc=sxp, incby=16)
        xload = P.v(sxp)
        P.emit("scalar", lambda e, bi=bi, st=st: e.activation(
            ln1_tm[bi][:, :], xh[bi][:, :], AF.Identity,
            bias=negmrall[:, st:st + 1], scale=rstdall[:, st:st + 1]),
            waits=[(sxp, xload), ("sdve", NEGMR1_DVE), ("sact", RSTD1_ACT)] + war_lntm[bi],
            inc="sact")
        apply_done = P.v("sact")
        war_x3[bi] = [("sact", apply_done)]
        for k in range(8):
            bank = 6 + (k % 2)
            for i in range(4):
                dt = 4 * k + i
                P.emit("tensor", lambda e, bi=bi, dt=dt, bank=bank, i=i: e.transpose(
                    pb16[bank][:, i * 128:(i + 1) * 128],
                    ln1_tm[bi][:, dt * 128:(dt + 1) * 128], ident_sb[:, :]),
                    waits=([("sact", apply_done), ("sk", SK_ALL)] + war_bank[bank]) if i == 0 else [],
                    inc=("spe" if i == 3 else None))
            tr_done = P.v("spe")
            P.emit("vector", lambda e, bank=bank, bi=bi, k=k: e.tensor_copy(
                evacb[bi][:, k * 512:(k + 1) * 512], pb16[bank][:, 0:512]),
                waits=[("spe", tr_done)] + (war_evacb[bi] if k == 0 else []), inc="sdve")
            war_bank[bank] = [("sdve", P.v("sdve"))]
        ev_done = P.v("sdve")
        lnwp = f"slnw{bi}"
        P.emit("sync", lambda e, st=st, bi=bi: e.dma_start(
            out=ln1f.rearrange("(j p) t -> p j t", p=128)[:, :, st * 128:(st + 1) * 128],
            in_=evacb[bi][:, :].rearrange("p (j t) -> p j t", t=128)),
            waits=[("sdve", ev_done)], inc=lnwp, incby=16)
        war_evacb[bi] = [(lnwp, P.v(lnwp))]
        war_lntm[bi] = [("spe", P.v("spe"))]

    def rope_emit(bank, tt, nh, dst, pre_waits):
        src = pb[bank][:, 0:nh * 128].rearrange("p (h r two) -> p h r two", h=nh, two=2)
        xr, xi = src[:, :, :, 0], src[:, :, :, 1]
        cosb = cos_sb.rearrange("p (t f) -> p t f", f=64)[:, tt, :].unsqueeze(1).broadcast_to([128, nh, 64])
        sinb = sin_sb.rearrange("p (t f) -> p t f", f=64)[:, tt, :].unsqueeze(1).broadcast_to([128, nh, 64])
        t1 = t1f[:, 0:nh * 64].rearrange("p (h r) -> p h r", h=nh)
        t2 = t2f[:, 0:nh * 64].rearrange("p (h r) -> p h r", h=nh)
        dstv = dst.rearrange("p (h r two) -> p h r two", h=nh, two=2)
        P.emit("vector", lambda e, t1=t1, xr=xr, cosb=cosb: e.tensor_tensor(t1, xr, cosb, ALU.mult),
               waits=pre_waits + [("sk", SK_ALL)])
        P.emit("vector", lambda e, t2=t2, xi=xi, sinb=sinb: e.tensor_tensor(t2, xi, sinb, ALU.mult))
        P.emit("vector", lambda e, dstv=dstv, t1=t1, t2=t2: e.tensor_tensor(dstv[:, :, :, 0], t1, t2, ALU.subtract))
        P.emit("vector", lambda e, t1=t1, xr=xr, sinb=sinb: e.tensor_tensor(t1, xr, sinb, ALU.mult))
        P.emit("vector", lambda e, t2=t2, xi=xi, cosb=cosb: e.tensor_tensor(t2, xi, cosb, ALU.mult))
        P.emit("vector", lambda e, dstv=dstv, t1=t1, t2=t2: e.tensor_tensor(dstv[:, :, :, 1], t1, t2, ALU.add),
               inc="sdve")

    war_w = []
    war_lbuf = [[], []]
    war_qkbank = {0: [], 1: [], 2: [], 3: []}
    war_qstage = [[], []]
    for ch in range(3):
        P.emit("gpsimd", lambda e, ch=ch: e.dma_start(
            out=wbufA[0].rearrange("p (j n) -> p j n", n=512),
            in_=wqkvT.rearrange("(j p) n -> p j n", p=128)[:, :, ch * 512:(ch + 1) * 512]),
            waits=war_w, inc="sw", incby=16)
        wready = P.v("sw")
        for tt in range(NT):
            lb = tt % 2
            lnrp = f"slnr{lb}"
            P.emit("sync", lambda e, tt=tt, lb=lb: e.dma_start(
                out=lbuf[lb].rearrange("p (j t) -> p j t", t=128),
                in_=ln1f.rearrange("(j p) t -> p j t", p=128)[:, :, tt * 128:(tt + 1) * 128]),
                waits=[(f"slnw{tt % 2}", 16 * (tt // 2 + 1))] + war_lbuf[lb], inc=lnrp, incby=16)
            lready = (lnrp, P.v(lnrp))
            bank = tt % 4
            for dt in range(ND):
                P.emit("tensor", lambda e, lb=lb, dt=dt, bank=bank: e.matmul(
                    pb[bank][:, :], lbuf[lb][:, dt * 128:(dt + 1) * 128],
                    wbufA[0][:, dt * 512:(dt + 1) * 512], start=(dt == 0), stop=False),
                    waits=([lready, ("sw", wready)] + war_qkbank[bank]) if dt == 0 else [])
            P.emit("tensor", lambda e, ch=ch, bank=bank: e.matmul(
                pb[bank][:, :], ones_r16[:, :], bqkv16[:, ch * 512:(ch + 1) * 512],
                start=False, stop=True),
                waits=[("sw", 16)], inc="spe")
            mm_done = P.v("spe")
            war_lbuf[lb] = [("spe", mm_done)]
            if tt == NT - 1:
                war_w = [("spe", mm_done)]
            if ch < 2:
                qs = tt % 2
                rope_emit(bank, tt, 4, qstage[qs][:, :],
                          [("spe", mm_done)] + war_qstage[qs])
                rope_done = P.v("sdve")
                for i in range(4):
                    hd_i = ch * 4 + i
                    P.emit("scalar", lambda e, qs=qs, i=i, hd_i=hd_i, tt=tt: e.activation(
                        sqscr[:, 0:128], qstage[qs][:, i * 128:(i + 1) * 128], AF.Square,
                        accum_out=normsq[:, hd_i * NT + tt:hd_i * NT + tt + 1]),
                        waits=[("sdve", rope_done)], inc="sact")
                sqn_done = P.v("sact")
                qwp = f"sqw{qs}"
                P.emit("sync", lambda e, qs=qs, tt=tt, ch=ch: e.dma_start(
                    out=qdram[tt * 128:(tt + 1) * 128, ch * 512:(ch + 1) * 512],
                    in_=qstage[qs][:, :]),
                    waits=[("sdve", rope_done)], inc=qwp, incby=16)
                war_qstage[qs] = [(qwp, P.v(qwp)), ("sact", sqn_done)]
                war_qkbank[bank] = [("sdve", rope_done)]
            else:
                rope_emit(bank, tt, 2, kv_tm[:, tt * 512:tt * 512 + 256],
                          [("spe", mm_done)])
                krope_done = P.v("sdve")
                for i in range(2):
                    P.emit("scalar", lambda e, i=i, tt=tt: e.activation(
                        sqscr[:, 0:128], kv_tm[:, tt * 512 + i * 128:tt * 512 + (i + 1) * 128],
                        AF.Square, accum_out=normsq[:, (8 + i) * NT + tt:(8 + i) * NT + tt + 1]),
                        waits=[("sdve", krope_done)])
                P.emit("scalar", lambda e, tt=tt, bank=bank: e.activation(
                    kv_tm[:, tt * 512 + 256:tt * 512 + 512], pb[bank][:, 256:512], AF.Copy),
                    waits=[("spe", mm_done)], inc="sact")
                war_qkbank[bank] = [("sdve", krope_done), ("sact", P.v("sact"))]
    A2_ACT = P.v("sact")
    A2_DVE = P.v("sdve")

    for g in range(2):
        for k4 in range(NT // 4):
            bank = 6 + (k4 % 2)
            for i in range(4):
                tt = 4 * k4 + i
                P.emit("tensor", lambda e, g=g, tt=tt, bank=bank, i=i: e.transpose(
                    pb16[bank][:, i * 128:(i + 1) * 128],
                    kv_tm[:, tt * 512 + g * 128:tt * 512 + (g + 1) * 128], ident_sb[:, :]),
                    waits=([("sact", A2_ACT), ("sdve", A2_DVE)] + war_bank[bank]) if i == 0 else [],
                    inc=("spe" if i == 3 else None))
            tr_done = P.v("spe")
            P.emit("vector", lambda e, g=g, k4=k4, bank=bank: e.tensor_copy(
                kfm[:, (g * NT + 4 * k4) * 128:(g * NT + 4 * k4) * 128 + 512], pb16[bank][:, 0:512]),
                waits=[("spe", tr_done)], inc="sdve")
            war_bank[bank] = [("sdve", P.v("sdve"))]

    war_nmax16 = []
    for hd_i in range(10):
        P.emit("vector", lambda e, hd_i=hd_i: e.reduce_max(
            nmax[:, :], normsq[:, hd_i * NT:(hd_i + 1) * NT], axis=AX.X),
            waits=[("sact", A2_ACT)], inc="sdve")
        P.emit("vector", lambda e: e.tensor_copy(nmax16[:, :], nmax[:, :]),
               waits=war_nmax16 + [("sdve", P.v("sdve"))], inc="sdve")
        rm = P.v("sdve")
        P.emit("tensor", lambda e: e.transpose(pb16[7][0:1, 0:128], nmax16[:, :], ident_sb[:, :]),
               waits=[("sdve", rm)] + war_bank[7], inc="spe")
        trm = P.v("spe")
        P.emit("vector", lambda e, hd_i=hd_i: e.reduce_max(
            maxsq[:, hd_i:hd_i + 1], pb16[7][0:1, 0:128], axis=AX.X),
            waits=[("spe", trm)], inc="sdve")
        war_bank[7] = [("sdve", P.v("sdve"))]
        war_nmax16 = [("spe", trm)]
    war_csc = []
    for lh in range(NHL):
        g = lh // 4
        P.emit("vector", lambda e, lh=lh, g=g: e.tensor_mul(
            csc[:, 0:1], maxsq[:, lh:lh + 1], maxsq[:, 8 + g:9 + g]),
            waits=war_csc + [("sdve", P.v("sdve"))], inc="sdve")
        md = P.v("sdve")
        P.emit("scalar", lambda e: e.activation(csc[:, 1:2], csc[:, 0:1], AF.Ln, bias=eps30[:, :]),
               waits=[("sdve", md)])
        P.emit("scalar", lambda e: e.activation(csc[:, 1:2], csc[:, 1:2], AF.Exp, scale=0.5), inc="sact")
        cd = P.v("sact")
        P.emit("vector", lambda e: e.tensor_scalar_mul(csc[:, 0:1], csc[:, 1:2], -SCALE),
               waits=[("sact", cd)], inc="sdve")
        nd_ = P.v("sdve")
        P.emit("tensor", lambda e: e.matmul(pb[7][:, 0:1], ones_rf[:, :], csc[:, 0:1],
                                            start=True, stop=True),
               waits=[("sdve", nd_)] + war_bank[7], inc="spe")
        bd = P.v("spe")
        war_csc = [("spe", bd)]
        P.emit("scalar", lambda e, lh=lh: e.activation(negbias[:, lh:lh + 1], pb[7][:, 0:1], AF.Copy),
               waits=[("spe", bd)], inc="sact")
        war_bank[7] = [("sact", P.v("sact"))]
    NEGB_ACT = P.v("sact")
    SQW0_ALL = P.v("sqw0")
    SQW1_ALL = P.v("sqw1")

    P.barrier()

    SCB = [0, 1, 5, 4]
    war_qtm = [[], []]
    war_qfm = [[], []]
    war_attnT = [[], [], [], []]
    war_avsb = [[], []]
    war_pb4 = []
    war_sc = {0: [], 1: [], 5: [], 4: []}
    war_avbk = {2: [], 6: []}
    war_Dslot = [[], []]
    war_dbc_ps = []
    war_dbc_sb = []
    deferred = [None]

    def flush_epilogue():
        if deferred[0] is None:
            return
        (par, avbank, dslot, rdone) = deferred[0]
        deferred[0] = None
        P.emit("tensor", lambda e, par=par: e.matmul(
            pb[7][:, :], ones_rf[:, :], recipD2[par][:, :], start=True, stop=True),
            waits=[("sdve", rdone)] + war_dbc_ps, inc="spe")
        bc = P.v("spe")
        P.emit("scalar", lambda e: e.activation(dbc_sb[:, :], pb[7][:, :], AF.Copy),
               waits=[("spe", bc)] + war_dbc_sb, inc="sact")
        dbce = P.v("sact")
        war_dbc_ps[:] = [("sact", dbce)]
        P.emit("vector", lambda e, par=par, avbank=avbank: e.tensor_mul(
            av_sb[par][:, :], pb[avbank][:, :], dbc_sb[:, :]),
            waits=[("sact", dbce)] + war_avsb[par], inc="sdve")
        avn = P.v("sdve")
        war_dbc_sb[:] = [("sdve", avn)]
        war_avbk[avbank] = [("sdve", avn)]
        r_, lh_ = divmod_idx[0]
        j = r_ // 2
        avp = f"savw{par}"
        P.emit("sync", lambda e, lh_=lh_, r_=r_, j=j, par=par: e.dma_start(
            out=cc2_in[j * 1024 + lh_ * 128:j * 1024 + (lh_ + 1) * 128,
                       (r_ % 2) * 512:(r_ % 2) * 512 + 512],
            in_=av_sb[par][:, :]),
            waits=[("sdve", avn)], inc=avp, incby=16)
        war_avsb[par] = [(avp, P.v(avp))]

    divmod_idx = [None]
    for r in range(NR):
        for lh in range(NHL):
            g = lh // 4
            it = r * NHL + lh
            par = it % 2
            qrp = f"sqr{par}"
            P.emit("sync", lambda e, r=r, lh=lh, par=par: e.dma_start(
                out=qtm[par].rearrange("p (t c) -> p t c", c=128),
                in_=qdram.rearrange("(t p) c -> p t c", p=128)[:, 4 * r:4 * r + 4, lh * 128:(lh + 1) * 128]),
                waits=[("sqw0", SQW0_ALL), ("sqw1", SQW1_ALL)] + war_qtm[par], inc=qrp, incby=16)
            qld = (qrp, P.v(qrp))
            for i in range(4):
                P.emit("tensor", lambda e, par=par, i=i: e.transpose(
                    pb16[4][:, i * 128:(i + 1) * 128], qtm[par][:, i * 128:(i + 1) * 128],
                    ident_sb[:, :]),
                    waits=([qld] + war_pb4 + war_sc[4] + war_qfm[par]) if i == 0 else [],
                    inc=("spe" if i == 3 else None))
            war_qtm[par] = [("spe", P.v("spe"))]
            qtr = P.v("spe")
            P.emit("vector", lambda e, par=par: e.tensor_copy(qfm[par][:, :], pb16[4][:, 0:512]),
                   waits=[("spe", qtr)], inc="sdve")
            war_pb4 = [("sdve", P.v("sdve"))]
            war_sc[4] = []
            qfm_ready = P.v("sdve")

            kmax = 4 * r + 4
            exp_done = [0] * kmax
            mask_done = [0] * kmax
            avbank = [2, 6][it % 2]
            dslot = it % 2
            pend = []

            def emit_dav(kt, last, g=g, avbank=avbank, dslot=dslot, it=it):
                at = kt % 4
                wts = [("sact", exp_done[kt])]
                if mask_done[kt]:
                    wts.append(("sdve", mask_done[kt]))
                if kt == 0:
                    wts += war_Dslot[dslot] + war_avbk[avbank]
                P.emit("tensor", lambda e, at=at, kt=kt, last=last, dslot=dslot: e.matmul(
                    pb[3][64 * dslot:64 * dslot + 1, :], ones_c16[:, :], attnT[at][:, :],
                    start=(kt == 0), stop=last), waits=wts)
                P.emit("tensor", lambda e, at=at, kt=kt, g=g, last=last, avbank=avbank: e.matmul(
                    pb[avbank][:, :], kv_tm[:, kt * 512 + 256 + g * 128:kt * 512 + 256 + (g + 1) * 128],
                    attnT[at][:, :], start=(kt == 0), stop=last), inc="spe")
                war_attnT[at] = [("spe", P.v("spe"))]

            for kt in range(kmax):
                sb_ = SCB[kt % 4]
                P.emit("tensor", lambda e, g=g, kt=kt, sb_=sb_, par=par: e.matmul(
                    pb[sb_][:, :], kfm[:, (g * NT + kt) * 128:(g * NT + kt + 1) * 128],
                    qfm[par][:, :], start=True, stop=True),
                    waits=[("sdve", qfm_ready)] + war_sc[sb_], inc="spe")
                qk_done = P.v("spe")
                if kt == 1:
                    flush_epilogue()
                at = kt % 4
                P.emit("scalar", lambda e, sb_=sb_, at=at, lh=lh: e.activation(
                    attnT[at][:, :], pb[sb_][:, :], AF.Exp,
                    bias=negbias[:, lh:lh + 1], scale=SCALE),
                    waits=[("spe", qk_done), ("sact", NEGB_ACT)] + war_attnT[at], inc="sact")
                exp_done[kt] = P.v("sact")
                war_sc[sb_] = [("sact", exp_done[kt])]
                if kt >= 4 * r:
                    dm = kt - 4 * r
                    P.emit("vector", lambda e, at=at, dm=dm: e.tensor_mul(
                        attnT[at][:, :], attnT[at][:, :], masks_sb[:, dm * 512:(dm + 1) * 512]),
                        waits=[("sact", exp_done[kt]), ("sk", SK_ALL)], inc="sdve")
                    mask_done[kt] = P.v("sdve")
                pend.append(kt)
                if len(pend) >= 4:
                    emit_dav(pend.pop(0), False)
            while pend:
                k0 = pend.pop(0)
                emit_dav(k0, k0 == kmax - 1)
            av_done = P.v("spe")

            P.emit("vector", lambda e, par=par, dslot=dslot: e.reciprocal(
                recipD2[par][:, :], pb[3][64 * dslot:64 * dslot + 1, :]),
                waits=[("spe", av_done)], inc="sdve")
            rdone = P.v("sdve")
            war_Dslot[dslot] = [("sdve", rdone)]
            war_qfm[par] = [("spe", av_done)]
            divmod_idx[0] = (r, lh)
            deferred[0] = (par, avbank, dslot, rdone)
            if lh == NHL - 1 and r % 2 == 1:
                flush_epilogue()
                j = r // 2
                P.emit("gpsimd", lambda e, j=j: e.collective_compute(
                    "AllGather", ALU.bypass, replica_groups=[[0, 1, 2, 3], [4, 5, 6, 7]],
                    ins=[cc2_in[j * 1024:(j + 1) * 1024, :]], outs=[cc2_all[j]]),
                    waits=[("savw0", P.v("savw0")), ("savw1", P.v("savw1"))], inc=f"scc{j}")
    flush_epilogue()

    P.barrier()

    P.emit("vector", lambda e: e.tensor_copy(idxf[:, :], lsi_sb[:, :]),
           waits=[("sk", SK_ALL)], inc="sdve")
    P.emit("vector", lambda e: e.tensor_scalar(
        idxf[:, :], idxf[:, :], -1.0, bp1_sb[0:1, 0:1], ALU.mult, ALU.add),
        waits=[("sdve", P.v("sdve"))], inc="sdve")
    P.emit("vector", lambda e: e.tensor_scalar(
        idxf[:, :], idxf[:, :], 0.0, 1.0, ALU.max, ALU.min),
        waits=[("sdve", P.v("sdve"))], inc="sdve")
    g0, g1, g2, g3 = (idxf[0:1, i:i + 1] for i in range(4))

    def _gstep(fn):
        P.emit("vector", fn, waits=[("sdve", P.v("sdve"))], inc="sdve")

    _gstep(lambda e: e.tensor_scalar(gsc[0:1, 0:1], g1, -1.0, 1.0, ALU.mult, ALU.add))
    _gstep(lambda e: e.tensor_mul(gsc[0:1, 1:2], g0, gsc[0:1, 0:1]))
    _gstep(lambda e: e.tensor_scalar(gsc[0:1, 2:3], g1, 2.0, None, ALU.mult))
    _gstep(lambda e: e.tensor_add(gsc[0:1, 1:2], gsc[0:1, 1:2], gsc[0:1, 2:3]))
    _gstep(lambda e: e.tensor_scalar(gsc[0:1, 3:4], g2, -1.0, 1.0, ALU.mult, ALU.add))
    _gstep(lambda e: e.tensor_mul(gsc[0:1, 1:2], gsc[0:1, 1:2], gsc[0:1, 3:4]))
    _gstep(lambda e: e.tensor_scalar(gsc[0:1, 4:5], g3, 1.0, 3.0, ALU.mult, ALU.add))
    _gstep(lambda e: e.tensor_mul(gsc[0:1, 4:5], gsc[0:1, 4:5], g2))
    _gstep(lambda e: e.tensor_add(gsc[0:1, 1:2], gsc[0:1, 1:2], gsc[0:1, 4:5]))
    for j in range(3):
        _gstep(lambda e, j=j: e.tensor_scalar(
            gsc[0:1, 5:6], gsc[0:1, 1:2], 1.0, -float(j + 2), ALU.mult, ALU.add))
        _gstep(lambda e: e.tensor_mul(gsc[0:1, 6:7], gsc[0:1, 5:6], gsc[0:1, 5:6]))
        _gstep(lambda e: e.tensor_scalar(gsc[0:1, 6:7], gsc[0:1, 6:7], -1.0, 1.0, ALU.mult, ALU.add))
        _gstep(lambda e: e.tensor_scalar(gsc[0:1, 6:7], gsc[0:1, 6:7], 0.0, 1.0, ALU.max, ALU.min))
        _gstep(lambda e, j=j: e.tensor_scalar_mul(
            gex[0:1, 16 * j:16 * (j + 1)], ones_rf[0:1, 0:16], gsc[0:1, 6:7]))
    gexd = P.v("sdve")
    P.emit("tensor", lambda e: e.matmul(pb[7][0:48, 0:1], gex[:, :], ones_rf[0:1, 0:1],
                                        start=True, stop=True),
           waits=[("sdve", gexd)] + war_bank[7] + war_dbc_ps, inc="spe")
    gb = P.v("spe")
    P.emit("scalar", lambda e: e.activation(gmask[:, :], pb[7][0:48, 0:1], AF.Copy),
           waits=[("spe", gb)], inc="sact")
    GMASK_ACT = P.v("sact")

    P.emit("gpsimd", lambda e: e.dma_start(out=bias16[:, :], in_=bo[:, :]), inc="sw", incby=16)
    P.emit("gpsimd", lambda e: e.dma_start(out=lBT_sb[:, :], in_=lBT[:, :]), inc="sw", incby=16)
    P.emit("gpsimd", lambda e: e.dma_start(
        out=lAT_sb.rearrange("p (j r) -> p j r", r=48),
        in_=lAT.rearrange("(j p) r -> p j r", p=128)), inc="sw", incby=16)
    CCONST = P.v("sw")

    war_chk = [[], []]
    for ft in range(NT):
        for j in range(4):
            cb = (ft * 4 + j) % 2
            ccp = f"sccr{cb}"
            P.emit("sync", lambda e, ft=ft, j=j, cb=cb: e.dma_start(
                out=chkb[cb][:, :],
                in_=cc2_all[j, ft * 128:(ft + 1) * 128, :]),
                waits=[(f"scc{j}", 1)] + war_chk[cb], inc=ccp, incby=16)
            ld = (ccp, P.v(ccp))
            if j == 0:
                P.emit("vector", lambda e, ft=ft, cb=cb: e.tensor_scalar_mul(
                    av_fm[:, ft * 1024:(ft + 1) * 1024], chkb[cb][:, :], sel_sb[:, 0:1]),
                    waits=[ld, ("sk", SK_ALL)], inc="sdve")
            else:
                P.emit("vector", lambda e, ft=ft, j=j, cb=cb: e.scalar_tensor_tensor(
                    av_fm[:, ft * 1024:(ft + 1) * 1024], chkb[cb][:, :], sel_sb[:, j:j + 1],
                    av_fm[:, ft * 1024:(ft + 1) * 1024], ALU.mult, ALU.add),
                    waits=[ld, ("sdve", P.v("sdve"))], inc="sdve")
            war_chk[cb] = [("sdve", P.v("sdve"))]
    AVSEL = P.v("sdve")

    war_wc = [[], []]
    war_xo = [[], []]
    war_x2cw = [[], []]
    war_cbank = {0: [], 1: []}
    for fo in range(NFO):
        wb = fo % 2
        wcp = f"swc{wb}"
        P.emit("gpsimd", lambda e, fo=fo, wb=wb: e.dma_start(
            out=wbufC[wb].rearrange("p (j n) -> p j n", n=512),
            in_=woT.rearrange("(j p) n -> p j n", p=128)[:, :, fo * 512:(fo + 1) * 512]),
            waits=war_wc[wb], inc=wcp, incby=16)
        wld = (wcp, P.v(wcp))
        for ts in range(8):
            xb_ = (fo * 8 + ts) % 2
            xop = f"sxo{xb_}"
            P.emit("sync", lambda e, fo=fo, ts=ts, xb_=xb_: e.dma_start(
                out=xoC[xb_][:, :],
                in_=x_own[ts * 128:(ts + 1) * 128, fo * 512:(fo + 1) * 512]),
                waits=war_xo[xb_], inc=xop, incby=16)
            xld = (xop, P.v(xop))
            bank = ts % 2
            for dt in range(ND):
                P.emit("tensor", lambda e, dt=dt, ts=ts, wb=wb, bank=bank: e.matmul(
                    pb[bank][:, :], av_fm[:, dt * 1024 + ts * 128:dt * 1024 + (ts + 1) * 128],
                    wbufC[wb][:, dt * 512:(dt + 1) * 512], start=(dt == 0), stop=False),
                    waits=([("sdve", AVSEL), wld] + war_cbank[bank]) if dt == 0 else [])
            P.emit("tensor", lambda e, fo=fo, bank=bank: e.matmul(
                pb[bank][:, :], ones_r16[:, :], bias16[:, fo * 512:(fo + 1) * 512],
                start=False, stop=True),
                waits=[("sw", CCONST)], inc="spe")
            mmd = P.v("spe")
            P.emit("vector", lambda e, xb_=xb_, bank=bank: e.tensor_add(
                x2cw[xb_][:, :], pb[bank][:, :], xoC[xb_][:, :]),
                waits=[("spe", mmd), xld] + war_x2cw[xb_], inc="sdve")
            od = P.v("sdve")
            war_cbank[bank] = [("sdve", od)]
            war_xo[xb_] = [("sdve", od)]
            x2p = f"sx2w{xb_}"
            P.emit("sync", lambda e, ts=ts, fo=fo, xb_=xb_: e.dma_start(
                out=x2d[ts * 128:(ts + 1) * 128, fo * 512:(fo + 1) * 512], in_=x2cw[xb_][:, :]),
                waits=[("sdve", od)], inc=x2p, incby=16)
            war_x2cw[xb_] = [(x2p, P.v(x2p))]
        war_wc[wb] = [("spe", P.v("spe"))]
    C1_PE = P.v("spe")
    X2W0 = P.v("sx2w0")
    X2W1 = P.v("sx2w1")
    P.emit("gpsimd", lambda e: e.dma_start(out=bias16[:, :], in_=bffn[:, :]),
           waits=[("spe", C1_PE)], inc="sw", incby=16)
    BFFN_SW = P.v("sw")

    war_x2s = [[], []]
    for ts in range(8):
        bi = ts % 2
        xrp = f"sx2r{bi}"
        P.emit("sync", lambda e, ts=ts, bi=bi: e.dma_start(
            out=x2s[bi][:, :], in_=x2d[ts * 128:(ts + 1) * 128, :]),
            waits=[("sx2w0", X2W0), ("sx2w1", X2W1)] + war_x2s[bi], inc=xrp, incby=16)
        xld = P.v(xrp)
        P.emit("vector", lambda e, bi=bi, ts=ts: e.reduce_sum(
            sums[:, ts:ts + 1], x2s[bi][:, :], axis=AX.X),
            waits=[(xrp, xld)], inc="sdve")
        P.emit("scalar", lambda e, bi=bi, ts=ts: e.activation(
            ln2_tm[:, :], x2s[bi][:, :], AF.Square, accum_out=sqs[:, ts:ts + 1]),
            waits=[(xrp, xld), ("spe", C1_PE), ("sact", GMASK_ACT)], inc="sact")
        war_x2s[bi] = [("sdve", P.v("sdve")), ("sact", P.v("sact"))]
    LN2P1_ACT = P.v("sact")
    RSTD2_ACT, NEGMR2_DVE = ln_stats_batch(8)

    war_x2s3 = [war_x2s[0], war_x2s[1]]
    war_ln2tm = [("spe", 0)]
    war_p = []
    for ts in range(8):
        bi = ts % 2
        xrp = f"sx2r{bi}"
        P.emit("sync", lambda e, ts=ts, bi=bi: e.dma_start(
            out=x2s[bi][:, :], in_=x2d[ts * 128:(ts + 1) * 128, :]),
            waits=war_x2s3[bi], inc=xrp, incby=16)
        xld = P.v(xrp)
        P.emit("scalar", lambda e, bi=bi, ts=ts: e.activation(
            ln2_tm[:, :], x2s[bi][:, :], AF.Identity,
            bias=negmrall[:, ts:ts + 1], scale=rstdall[:, ts:ts + 1]),
            waits=[(xrp, xld), ("sdve", NEGMR2_DVE), ("sact", RSTD2_ACT), war_ln2tm[0]],
            inc="sact")
        appd = P.v("sact")
        war_x2s3[bi] = [("sact", appd)]
        for k in range(8):
            bank2 = 6 + (k % 2)
            for i in range(4):
                dt = 4 * k + i
                P.emit("tensor", lambda e, dt=dt, bank2=bank2, i=i: e.transpose(
                    pb16[bank2][:, i * 128:(i + 1) * 128], ln2_tm[:, dt * 128:(dt + 1) * 128],
                    ident_sb[:, :]),
                    waits=([("sact", appd)] + war_bank[bank2]) if i == 0 else [],
                    inc=("spe" if i == 3 else None))
            trd = P.v("spe")
            P.emit("vector", lambda e, k=k, ts=ts, bank2=bank2: e.tensor_copy(
                ln2_fm.rearrange("p (j t) -> p j t", t=1024)[:, 4 * k:4 * k + 4, ts * 128:(ts + 1) * 128],
                pb16[bank2][:, 0:512].rearrange("p (j t) -> p j t", t=128)),
                waits=[("spe", trd)], inc="sdve")
            war_bank[bank2] = [("sdve", P.v("sdve"))]
        war_ln2tm = [("spe", P.v("spe"))]
        ln2_done = P.v("sdve")
        for dt in range(ND):
            P.emit("tensor", lambda e, dt=dt, ts=ts: e.matmul(
                pb[2][0:48, 0:128], lAT_sb[:, dt * 48:(dt + 1) * 48],
                ln2_fm[:, dt * 1024 + ts * 128:dt * 1024 + (ts + 1) * 128],
                start=(dt == 0), stop=(dt == ND - 1)),
                waits=([("sdve", ln2_done), ("sw", CCONST), ("sact", GMASK_ACT)] + war_p) if dt == 0 else [],
                inc=("spe" if dt == ND - 1 else None))
        pd = P.v("spe")
        P.emit("vector", lambda e, ts=ts: e.tensor_scalar_mul(
            p_sb[:, ts * 128:(ts + 1) * 128], pb[2][0:48, 0:128], gmask[:, 0:1]),
            waits=[("spe", pd), ("sact", GMASK_ACT)], inc="sdve")
        war_p = [("sdve", P.v("sdve"))]
    LN2_ALL = P.v("sdve")
    P_ALL = P.v("sdve")

    war_wc2 = [war_wc[0], war_wc[1]]
    war_out = [[], []]
    war_x2cr = [[], []]
    war_cbank2 = {0: war_cbank[0], 1: war_cbank[1]}
    for fo in range(NFO):
        wb = fo % 2
        wcp = f"swc{wb}"
        wts = list(war_wc2[wb])
        if fo == 0:
            wts += [war_ln2tm[0]]
        P.emit("gpsimd", lambda e, fo=fo, wb=wb: e.dma_start(
            out=wbufC[wb].rearrange("p (j n) -> p j n", n=512),
            in_=ffnT.rearrange("(j p) n -> p j n", p=128)[:, :, fo * 512:(fo + 1) * 512]),
            waits=wts, inc=wcp, incby=16)
        wld = (wcp, P.v(wcp))
        for ts in range(8):
            xb_ = (fo * 8 + ts) % 2
            xrp = f"sx2c{xb_}"
            P.emit("sync", lambda e, fo=fo, ts=ts, xb_=xb_: e.dma_start(
                out=x2cr[xb_][:, :],
                in_=x2d[ts * 128:(ts + 1) * 128, fo * 512:(fo + 1) * 512]),
                waits=[("sx2w0", X2W0), ("sx2w1", X2W1)] + war_x2cr[xb_], inc=xrp, incby=16)
            x2ld = (xrp, P.v(xrp))
            bank = ts % 2
            for dt in range(ND):
                P.emit("tensor", lambda e, dt=dt, ts=ts, wb=wb, bank=bank: e.matmul(
                    pb[bank][:, :], ln2_fm[:, dt * 1024 + ts * 128:dt * 1024 + (ts + 1) * 128],
                    wbufC[wb][:, dt * 512:(dt + 1) * 512], start=(dt == 0), stop=False),
                    waits=([("sdve", LN2_ALL), wld] + war_cbank2[bank]) if dt == 0 else [])
            P.emit("tensor", lambda e, fo=fo, bank=bank: e.matmul(
                pb[bank][:, :], ones_r16[:, :], bias16[:, fo * 512:(fo + 1) * 512],
                start=False, stop=False),
                waits=[("sw", BFFN_SW)])
            P.emit("tensor", lambda e, fo=fo, ts=ts, bank=bank: e.matmul(
                pb[bank][:, :], p_sb[:, ts * 128:(ts + 1) * 128],
                lBT_sb[:, fo * 512:(fo + 1) * 512], start=False, stop=True),
                waits=[("sdve", P_ALL)], inc="spe")
            mmd = P.v("spe")
            ob = (fo * 8 + ts) % 2
            P.emit("vector", lambda e, bank=bank, ob=ob, xb_=xb_: e.tensor_add(
                outsb[ob][:, :], pb[bank][:, :], x2cr[xb_][:, :]),
                waits=[("spe", mmd), x2ld] + war_out[ob], inc="sdve")
            od = P.v("sdve")
            war_cbank2[bank] = [("sdve", od)]
            war_x2cr[xb_] = [("sdve", od)]
            sop = f"sout{ob}"
            P.emit("sync", lambda e, ts=ts, fo=fo, ob=ob: e.dma_start(
                out=out[ts * 128:(ts + 1) * 128, fo * 512:(fo + 1) * 512], in_=outsb[ob][:, :]),
                waits=[("sdve", od)], inc=sop, incby=16)
            war_out[ob] = [(sop, P.v(sop))]
        war_wc2[wb] = [("spe", P.v("spe"))]

    P.emit("sync", lambda e: e.nop(), waits=[("sout0", P.v("sout0")), ("sout1", P.v("sout1"))])

    P.replay()
    return nc


_NC_CACHE = None


def kernel(L0_attn_x, L0_attn_w_qkv, L0_attn_b_qkv, L0_attn_w_o, L0_attn_b_o,
           L0_ffn_weight, L0_ffn_bias, L0_ffn_lora_A, L0_ffn_lora_B,
           L0_ffn_lora_start_indices):
    global _NC_CACHE
    x = np.asarray(L0_attn_x, np.float32)
    w_qkv = np.asarray(L0_attn_w_qkv, np.float32)
    b_qkv = np.asarray(L0_attn_b_qkv, np.float32)
    w_o = np.asarray(L0_attn_w_o, np.float32)
    b_o = np.asarray(L0_attn_b_o, np.float32)
    w_ffn = np.asarray(L0_ffn_weight, np.float32)
    b_ffn = np.asarray(L0_ffn_bias, np.float32)
    lora_A = np.asarray(L0_ffn_lora_A, np.float32)
    lora_B = np.asarray(L0_ffn_lora_B, np.float32)
    lsi = np.asarray(L0_ffn_lora_start_indices).astype(np.int32, copy=False)
    lsi = np.ascontiguousarray(lsi)

    woT = np.ascontiguousarray(w_o.T)
    ffnT = np.ascontiguousarray(w_ffn.T)
    lAT = np.ascontiguousarray(lora_A.transpose(2, 0, 1).reshape(4096, 48))
    lBT = np.ascontiguousarray(lora_B.transpose(0, 2, 1).reshape(48, 4096))

    inv_freq = (1.0 / 10000.0 ** (np.arange(0, HD, 2, dtype=np.float32) / HD)).astype(np.float32)
    pos = np.arange(S, dtype=np.float32)
    ang = pos[:, None] * inv_freq[None, :]
    cosT = np.cos(ang).astype(np.float32)
    sinT = np.sin(ang).astype(np.float32)
    ident = np.eye(128, dtype=ml_dtypes.bfloat16)
    kk = np.arange(128)[:, None]
    qq = np.arange(512)[None, :]
    mask4 = np.stack([(kk + 128 * d <= qq) for d in range(4)]).astype(ml_dtypes.bfloat16)

    if _NC_CACHE is None:
        _NC_CACHE = build_nc()
    nc = _NC_CACHE

    xb = [np.ascontiguousarray(x[0]), np.ascontiguousarray(x[1])]
    in_maps = []
    for c in range(8):
        B, m = c // 4, c % 4
        rows = np.concatenate([
            np.arange(m * QC, (m + 1) * QC),
            np.arange(4096 + m * KVC, 4096 + (m + 1) * KVC),
            np.arange(5120 + m * KVC, 5120 + (m + 1) * KVC),
        ])
        wq = np.ascontiguousarray(w_qkv[rows].T)
        bq = np.ascontiguousarray(b_qkv[rows])[None, :]
        sel = np.zeros((128, 4), np.float32)
        sel[:, m] = 1.0
        in_maps.append({
            "x_batch": xb[B],
            "x_own": np.ascontiguousarray(xb[B][m * 1024:(m + 1) * 1024]),
            "wqkvT": wq, "bqkv": bq,
            "woT": woT, "bo": b_o[None, :],
            "ffnT": ffnT, "bffn": b_ffn[None, :],
            "lAT": lAT, "lBT": lBT, "lsi": lsi,
            "bplus1": np.array([[float(B + 1)]], np.float32),
            "selmask": sel,
            "cosT": cosT, "sinT": sinT,
            "identt": ident, "mask4": mask4,
        })

    res = run_bass_kernel_spmd(nc, in_maps, core_ids=list(range(8)))
    full = np.empty((2, 4096, 4096), np.float32)
    for c in range(8):
        B, m = c // 4, c % 4
        full[B, m * 1024:(m + 1) * 1024] = res.results[c]["out"]
    return full


# version 05
# speedup vs baseline: 1.2258x; 1.0810x over previous
import math
import numpy as np
import ml_dtypes
import concourse.bass as bass
import concourse.mybir as mybir
from concourse.bass_utils import run_bass_kernel_spmd

F32 = mybir.dt.float32
BF16 = mybir.dt.bfloat16
I32 = mybir.dt.int32
U8 = mybir.dt.uint8
AF = mybir.ActivationFunctionType
ALU = mybir.AluOpType
AX = mybir.AxisListType

D = 4096
S = 4096
HD = 128
NHL = 8
QC = NHL * HD
KVC = 256
COLS = QC + 2 * KVC
NT = S // 128
ND = D // 128
NR = S // 512
NFO = 8
SCALE = 1.0 / math.sqrt(HD)
LN_EPS = 1e-5


class Plan:

    ENGS = ("sync", "gpsimd", "vector", "scalar", "tensor")

    def __init__(self, nc):
        self.nc = nc
        self.q = {e: [] for e in self.ENGS}
        self.h = {}
        self.c = {}
        self.cuts = []

    def sem(self, name):
        self.h[name] = self.nc.alloc_semaphore(name)
        self.c[name] = 0

    def emit(self, eng, fn, waits=(), inc=None, incby=1):
        if inc is not None:
            self.c[inc] += incby
        self.q[eng].append((tuple(waits), fn, inc, incby))

    def v(self, name):
        return self.c[name]

    def barrier(self):
        self.cuts.append({e: len(self.q[e]) for e in self.ENGS})

    def replay(self):
        nc = self.nc
        self.cuts.append({e: len(self.q[e]) for e in self.ENGS})
        start = {e: 0 for e in self.ENGS}
        lastw = {e: {} for e in self.ENGS}
        for seg_i, cut in enumerate(self.cuts):
            with nc.Block(f"ph{seg_i}") as block:
                for ename in self.ENGS:
                    items = self.q[ename][start[ename]:cut[ename]]
                    if not items:
                        continue

                    def make_body(items=items, lw=lastw[ename], plan=self):
                        def body(eng):
                            for waits, fn, inc, incby in items:
                                for sname, val in waits:
                                    if val > 0 and lw.get(sname, 0) < val:
                                        eng.wait_ge(plan.h[sname], val)
                                        lw[sname] = val
                                inst = fn(eng)
                                if inc is not None:
                                    inst.then_inc(plan.h[inc], incby)
                        return body

                    getattr(block, ename)(make_body())
            start = dict(cut)
            if seg_i < len(self.cuts) - 1:
                nc.all_engine_barrier()


class Carver:
    def __init__(self, arena_ap, nbytes):
        self.a = arena_ap
        self.n = nbytes
        self.cur = 0

    def take(self, nbytes, dtype, parts=128, base=0):
        off = (self.cur + 31) & ~31
        self.cur = off + nbytes
        assert self.cur <= self.n, f"arena overflow: {self.cur} > {self.n}"
        return self.a[base:base + parts, off:off + nbytes].bitcast(dtype)

    def mark(self):
        return self.cur

    def reset(self, pos):
        self.cur = pos


def build_nc():
    nc = bass.Bass("TRN2", target_bir_lowering=False, debug=False)
    ap = lambda t: t.ap()

    x_batch = ap(nc.dram_tensor("x_batch", [S, D], F32, kind="ExternalInput"))
    x_own = ap(nc.dram_tensor("x_own", [1024, D], F32, kind="ExternalInput"))
    wqkvT = ap(nc.dram_tensor("wqkvT", [D, COLS], F32, kind="ExternalInput"))
    bqkv = ap(nc.dram_tensor("bqkv", [1, COLS], F32, kind="ExternalInput"))
    woT = ap(nc.dram_tensor("woT", [D, D], F32, kind="ExternalInput"))
    bo = ap(nc.dram_tensor("bo", [1, D], F32, kind="ExternalInput"))
    ffnT = ap(nc.dram_tensor("ffnT", [D, D], F32, kind="ExternalInput"))
    bffn = ap(nc.dram_tensor("bffn", [1, D], F32, kind="ExternalInput"))
    lAT = ap(nc.dram_tensor("lAT", [D, 48], F32, kind="ExternalInput"))
    lBT = ap(nc.dram_tensor("lBT", [48, D], F32, kind="ExternalInput"))
    lsi = ap(nc.dram_tensor("lsi", [4], I32, kind="ExternalInput"))
    bplus1 = ap(nc.dram_tensor("bplus1", [1, 1], F32, kind="ExternalInput"))
    selmask = ap(nc.dram_tensor("selmask", [128, 4], F32, kind="ExternalInput"))
    cosT = ap(nc.dram_tensor("cosT", [S, 64], F32, kind="ExternalInput"))
    sinT = ap(nc.dram_tensor("sinT", [S, 64], F32, kind="ExternalInput"))
    identt = ap(nc.dram_tensor("identt", [128, 128], BF16, kind="ExternalInput"))
    mask4 = ap(nc.dram_tensor("mask4", [4, 128, 512], BF16, kind="ExternalInput"))
    out = ap(nc.dram_tensor("out", [1024, D], F32, kind="ExternalOutput"))

    ln1f = ap(nc.dram_tensor("ln1f", [D, S], BF16, kind="Internal"))
    qdram = ap(nc.dram_tensor("qdram", [S, QC], BF16, kind="Internal"))
    cc2_in = ap(nc.dram_tensor("cc2_in", [4 * QC, 1024], BF16, kind="Internal"))
    cc2_all = ap(nc.dram_tensor("cc2_all", [4, 4 * QC, 1024], BF16, kind="Internal"))
    x2d = ap(nc.dram_tensor("x2d", [1024, D], BF16, kind="Internal"))

    A = lambda *a, **k: nc.alloc_sbuf_tensor(*a, **k).ap()
    ident_sb = A("ident_sb", [128, 128], BF16)
    ones_r16 = A("ones_r16", [1, 128], BF16)
    ones_c16 = A("ones_c16", [128, 1], BF16)
    ones_rf = A("ones_rf", [1, 128], F32)
    epscol = A("epscol", [128, 1], F32)
    eps30 = A("eps30", [1, 1], F32)
    sel_sb = A("sel_sb", [128, 4], F32)
    bp1_sb = A("bp1_sb", [1, 1], F32)
    lsi_sb = A("lsi_sb", [1, 4], I32)
    idxf = A("idxf", [1, 4], F32)
    gsc = A("gsc", [1, 16], F32)
    gex = A("gex", [1, 48], F32)
    gmask = A("gmask", [48, 1], F32)
    normsq = A("normsq", [128, 10 * NT], F32)
    nmax = A("nmax", [128, 1], F32)
    nmax16 = A("nmax16", [128, 1], BF16)
    maxsq = A("maxsq", [1, 16], F32)
    negbias = A("negbias", [128, NHL], F32)
    sums = A("sums", [128, NT], F32)
    sqs = A("sqs", [128, NT], F32)
    mall = A("mall", [128, NT], F32)
    msqall = A("msqall", [128, NT], F32)
    varall = A("varall", [128, NT], F32)
    rstdall = A("rstdall", [128, NT], F32)
    negmrall = A("negmrall", [128, NT], F32)
    csc = A("csc", [1, 2], F32)
    recipD2 = [A(f"recipD{i}", [1, 512], F32) for i in range(2)]
    dbc_sb = A("dbc_sb", [128, 512], F32)
    masks_sb = A("masks_sb", [128, 4 * 512], BF16)
    sqscr = A("sqscr", [128, 512], BF16)

    ARENA = 190 * 1024
    arena = A("arena", [128, ARENA], U8)
    cv = Carver(arena, ARENA)

    kv_tm = cv.take(NT * 512 * 2, BF16)
    kfm = cv.take(2 * S * 2, BF16)
    ab_end = cv.mark()

    xh = [cv.take(D * 4, F32) for _ in range(2)]
    ln1_tm = [cv.take(D * 2, BF16) for _ in range(2)]
    evacb = [cv.take(D * 2, BF16) for _ in range(2)]
    wbufA = [cv.take(ND * 512 * 2, BF16)]
    lbuf = [cv.take(ND * 128 * 2, BF16) for _ in range(2)]
    lbuf = lbuf + [xh[i].bitcast(BF16)[:, 0:ND * 128] for i in range(2)]
    qstage = [cv.take(512 * 2, BF16) for _ in range(2)]
    t1f = cv.take(256 * 4, F32)
    t2f = cv.take(256 * 4, F32)
    cos_sb = cv.take(NT * 64 * 4, F32)
    sin_sb = cv.take(NT * 64 * 4, F32)
    bqkv16 = cv.take(COLS * 2, BF16, parts=1)

    cv.reset(ab_end)
    qtm = [cv.take(512 * 2, BF16) for _ in range(2)]
    qfm = [cv.take(512 * 2, BF16) for _ in range(2)]
    attnT = [cv.take(512 * 2, BF16) for _ in range(4)]
    attnP = [cv.take(512 * 2, BF16) for _ in range(2)]
    av_sb = [cv.take(512 * 2, BF16) for _ in range(2)]

    cv.reset(0)
    av_fm = cv.take(NT * 1024 * 2, BF16)
    wbufC = [cv.take(ND * 512 * 2, BF16) for _ in range(2)]
    ln2_tm = wbufC[0][:, 0:D]
    x2s = [cv.take(D * 2, BF16) for _ in range(2)]
    x2cw = [cv.take(512 * 2, BF16) for _ in range(2)]
    x2cr = [cv.take(512 * 2, BF16) for _ in range(2)]
    xoC = [cv.take(512 * 4, F32) for _ in range(2)]
    chkb = [cv.take(1024 * 2, BF16) for _ in range(2)]
    p_sb = cv.take(1024 * 2, BF16, parts=48)
    lBT_sb = cv.take(D * 2, BF16, parts=48)
    lAT_sb = cv.take(ND * 48 * 2, BF16)
    bias16 = cv.take(D * 2, BF16, parts=1)
    outsb = [cv.take(512 * 4, F32) for _ in range(2)]
    ln2_fm = av_fm

    pb = [nc.alloc_psum_tensor(f"pb{i}", [128, 512], F32).ap() for i in range(8)]
    pb16 = [p.bitcast(BF16) for p in pb]

    P = Plan(nc)
    for s in ("sx0", "sx1", "slnw0", "slnw1", "slnr0", "slnr1", "sw",
              "swc0", "swc1", "sqw0", "sqw1", "sqr0", "sqr1", "savw0", "savw1",
              "scc0", "scc1", "scc2", "scc3", "sccr0", "sccr1",
              "sxo0", "sxo1", "sx2w0", "sx2w1", "sx2r0", "sx2r1",
              "sx2c0", "sx2c1", "sk", "sout0", "sout1",
              "spe", "sact", "sdve"):
        P.sem(s)

    war_bank = {i: [] for i in range(8)}

    P.emit("sync", lambda e: e.dma_start(out=ident_sb[:, :], in_=identt[:, :]), inc="sk", incby=16)
    P.emit("sync", lambda e: e.dma_start(out=sel_sb[:, :], in_=selmask[:, :]), inc="sk", incby=16)
    P.emit("sync", lambda e: e.dma_start(out=bp1_sb[:, :], in_=bplus1[:, :]), inc="sk", incby=16)
    P.emit("sync", lambda e: e.dma_start(out=lsi_sb[:, :], in_=lsi.unsqueeze(0)), inc="sk", incby=16)
    P.emit("sync", lambda e: e.dma_start(
        out=masks_sb.rearrange("p (d q) -> p d q", d=4),
        in_=mask4.rearrange("d p q -> p d q")), inc="sk", incby=16)
    P.emit("sync", lambda e: e.dma_start(
        out=cos_sb.rearrange("p (t f) -> p t f", f=64),
        in_=cosT.rearrange("(t p) f -> p t f", p=128)), inc="sk", incby=16)
    P.emit("sync", lambda e: e.dma_start(
        out=sin_sb.rearrange("p (t f) -> p t f", f=64),
        in_=sinT.rearrange("(t p) f -> p t f", p=128)), inc="sk", incby=16)
    SK_ALL = P.v("sk")
    P.emit("gpsimd", lambda e: e.dma_start(out=bqkv16[:, :], in_=bqkv[:, :]), inc="sw", incby=16)
    P.emit("vector", lambda e: e.memset(ones_r16[:, :], 1.0))
    P.emit("vector", lambda e: e.memset(ones_c16[:, :], 1.0))
    P.emit("vector", lambda e: e.memset(ones_rf[:, :], 1.0))
    P.emit("vector", lambda e: e.memset(epscol[:, :], LN_EPS))
    P.emit("vector", lambda e: e.memset(eps30[:, :], 1e-30), inc="sdve")
    DVE_CONST = P.v("sdve")

    war_x = [[], []]
    for st in range(NT):
        bi = st % 2
        sxp = f"sx{bi}"
        for hf in range(2):
            P.emit("sync", lambda e, st=st, hf=hf, bi=bi: e.dma_start(
                out=xh[bi][:, hf * 2048:(hf + 1) * 2048],
                in_=x_batch[st * 128:(st + 1) * 128, hf * 2048:(hf + 1) * 2048]),
                waits=war_x[bi], inc=sxp, incby=16)
        xload = P.v(sxp)
        P.emit("vector", lambda e, bi=bi, st=st: e.reduce_sum(
            sums[:, st:st + 1], xh[bi][:, :], axis=AX.X),
            waits=[(sxp, xload)], inc="sdve")
        P.emit("scalar", lambda e, bi=bi, st=st: e.activation(
            ln1_tm[bi][:, :], xh[bi][:, :], AF.Square, accum_out=sqs[:, st:st + 1]),
            waits=[(sxp, xload)], inc="sact")
        war_x[bi] = [("sdve", P.v("sdve")), ("sact", P.v("sact"))]
    P1_DVE = P.v("sdve")
    P1_ACT = P.v("sact")

    def ln_stats_batch(ncols):
        P.emit("vector", lambda e: e.tensor_scalar_mul(mall[:, 0:ncols], sums[:, 0:ncols], 1.0 / D),
               waits=[("sdve", P.v("sdve")), ("sact", P.v("sact"))], inc="sdve")
        P.emit("vector", lambda e: e.tensor_mul(msqall[:, 0:ncols], mall[:, 0:ncols], mall[:, 0:ncols]),
               waits=[("sdve", P.v("sdve"))], inc="sdve")
        P.emit("vector", lambda e: e.scalar_tensor_tensor(
            varall[:, 0:ncols], sqs[:, 0:ncols], 1.0 / D, msqall[:, 0:ncols],
            ALU.mult, ALU.subtract),
            waits=[("sdve", P.v("sdve"))], inc="sdve")
        vd = P.v("sdve")
        P.emit("scalar", lambda e: e.activation(rstdall[:, 0:ncols], varall[:, 0:ncols], AF.Ln, bias=epscol[:, :]),
               waits=[("sdve", vd)])
        P.emit("scalar", lambda e: e.activation(rstdall[:, 0:ncols], rstdall[:, 0:ncols], AF.Exp, scale=-0.5),
               inc="sact")
        rstd_done = P.v("sact")
        P.emit("vector", lambda e: e.scalar_tensor_tensor(
            negmrall[:, 0:ncols], mall[:, 0:ncols], -1.0, rstdall[:, 0:ncols], ALU.mult, ALU.mult),
            waits=[("sact", rstd_done), ("sdve", vd)], inc="sdve")
        return rstd_done, P.v("sdve")

    RSTD1_ACT, NEGMR1_DVE = ln_stats_batch(NT)

    def rope_emit(bank, tt, nh, dst, pre_waits):
        src_ = pb[bank][:, 0:nh * 128].rearrange("p (h r two) -> p h r two", h=nh, two=2)
        xr, xi = src_[:, :, :, 0], src_[:, :, :, 1]
        cosb = cos_sb.rearrange("p (t f) -> p t f", f=64)[:, tt, :].unsqueeze(1).broadcast_to([128, nh, 64])
        sinb = sin_sb.rearrange("p (t f) -> p t f", f=64)[:, tt, :].unsqueeze(1).broadcast_to([128, nh, 64])
        t1 = t1f[:, 0:nh * 64].rearrange("p (h r) -> p h r", h=nh)
        t2 = t2f[:, 0:nh * 64].rearrange("p (h r) -> p h r", h=nh)
        dstv = dst.rearrange("p (h r two) -> p h r two", h=nh, two=2)
        P.emit("vector", lambda e, t1=t1, xr=xr, cosb=cosb: e.tensor_tensor(t1, xr, cosb, ALU.mult),
               waits=pre_waits + [("sk", SK_ALL)])
        P.emit("vector", lambda e, t2=t2, xi=xi, sinb=sinb: e.tensor_tensor(t2, xi, sinb, ALU.mult))
        P.emit("vector", lambda e, dstv=dstv, t1=t1, t2=t2: e.tensor_tensor(dstv[:, :, :, 0], t1, t2, ALU.subtract))
        P.emit("vector", lambda e, t1=t1, xr=xr, sinb=sinb: e.tensor_tensor(t1, xr, sinb, ALU.mult))
        P.emit("vector", lambda e, t2=t2, xi=xi, cosb=cosb: e.tensor_tensor(t2, xi, cosb, ALU.mult))
        P.emit("vector", lambda e, dstv=dstv, t1=t1, t2=t2: e.tensor_tensor(dstv[:, :, :, 1], t1, t2, ALU.add),
               inc="sdve")

    P.emit("gpsimd", lambda e: e.dma_start(
        out=wbufA[0].rearrange("p (j n) -> p j n", n=512),
        in_=wqkvT.rearrange("(j p) n -> p j n", p=128)[:, :, 0:512]),
        inc="sw", incby=16)
    W0_SW = P.v("sw")
    war_x3 = [war_x[0], war_x[1]]
    war_lntm = [[], []]
    war_evacb = [[], []]
    war_qkbank = {0: [], 1: [], 2: [], 3: []}
    war_qstage = [[], []]
    for st in range(NT):
        bi = st % 2
        sxp = f"sx{bi}"
        P.emit("sync", lambda e, st=st, bi=bi: e.dma_start(
            out=xh[bi][:, :], in_=x_batch[st * 128:(st + 1) * 128, :]),
            waits=war_x3[bi], inc=sxp, incby=16)
        xload = P.v(sxp)
        P.emit("scalar", lambda e, bi=bi, st=st: e.activation(
            ln1_tm[bi][:, :], xh[bi][:, :], AF.Identity,
            bias=negmrall[:, st:st + 1], scale=rstdall[:, st:st + 1]),
            waits=[(sxp, xload), ("sdve", NEGMR1_DVE), ("sact", RSTD1_ACT)] + war_lntm[bi],
            inc="sact")
        apply_done = P.v("sact")
        war_x3[bi] = [("sact", apply_done)]
        for k in range(8):
            bank = 6 + (k % 2)
            for i in range(4):
                dt = 4 * k + i
                P.emit("tensor", lambda e, bi=bi, dt=dt, bank=bank, i=i: e.transpose(
                    pb16[bank][:, i * 128:(i + 1) * 128],
                    ln1_tm[bi][:, dt * 128:(dt + 1) * 128], ident_sb[:, :]),
                    waits=([("sact", apply_done), ("sk", SK_ALL)] + war_bank[bank]) if i == 0 else [],
                    inc=("spe" if i == 3 else None))
            tr_done = P.v("spe")
            P.emit("vector", lambda e, bank=bank, bi=bi, k=k: e.tensor_copy(
                evacb[bi][:, k * 512:(k + 1) * 512], pb16[bank][:, 0:512]),
                waits=[("spe", tr_done)] + (war_evacb[bi] if k == 0 else []), inc="sdve")
            war_bank[bank] = [("sdve", P.v("sdve"))]
        ev_done = P.v("sdve")
        lnwp = f"slnw{bi}"
        P.emit("sync", lambda e, st=st, bi=bi: e.dma_start(
            out=ln1f.rearrange("(j p) t -> p j t", p=128)[:, :, st * 128:(st + 1) * 128],
            in_=evacb[bi][:, :].rearrange("p (j t) -> p j t", t=128)),
            waits=[("sdve", ev_done)], inc=lnwp, incby=16)
        war_lntm[bi] = [("spe", P.v("spe"))]
        bank = st % 4
        for dt in range(ND):
            P.emit("tensor", lambda e, bi=bi, dt=dt, bank=bank: e.matmul(
                pb[bank][:, :], evacb[bi][:, dt * 128:(dt + 1) * 128],
                wbufA[0][:, dt * 512:(dt + 1) * 512], start=(dt == 0), stop=False),
                waits=([("sdve", ev_done), ("sw", W0_SW)] + war_qkbank[bank]) if dt == 0 else [])
        P.emit("tensor", lambda e, bank=bank: e.matmul(
            pb[bank][:, :], ones_r16[:, :], bqkv16[:, 0:512],
            start=False, stop=True),
            waits=[("sw", 16)], inc="spe")
        mm_done = P.v("spe")
        war_evacb[bi] = [(lnwp, P.v(lnwp)), ("spe", mm_done)]
        qs = st % 2
        rope_emit(bank, st, 4, qstage[qs][:, :],
                  [("spe", mm_done)] + war_qstage[qs])
        rope_done = P.v("sdve")
        for i in range(4):
            P.emit("scalar", lambda e, qs=qs, i=i, st=st: e.activation(
                sqscr[:, 0:128], qstage[qs][:, i * 128:(i + 1) * 128], AF.Square,
                accum_out=normsq[:, i * NT + st:i * NT + st + 1]),
                waits=[("sdve", rope_done)], inc="sact")
        sqn_done = P.v("sact")
        qwp = f"sqw{qs}"
        P.emit("sync", lambda e, qs=qs, st=st: e.dma_start(
            out=qdram[st * 128:(st + 1) * 128, 0:512], in_=qstage[qs][:, :]),
            waits=[("sdve", rope_done)], inc=qwp, incby=16)
        war_qstage[qs] = [(qwp, P.v(qwp)), ("sact", sqn_done)]
        war_qkbank[bank] = [("sdve", rope_done)]
    A2CH0_ACT = P.v("sact")

    war_w = [("spe", P.v("spe"))]
    war_lbuf = [[], [], [("sact", A2CH0_ACT)], [("sact", A2CH0_ACT)]]
    for ch in (1, 2):
        P.emit("gpsimd", lambda e, ch=ch: e.dma_start(
            out=wbufA[0].rearrange("p (j n) -> p j n", n=512),
            in_=wqkvT.rearrange("(j p) n -> p j n", p=128)[:, :, ch * 512:(ch + 1) * 512]),
            waits=war_w, inc="sw", incby=16)
        wready = P.v("sw")
        for tt in range(NT):
            lb = tt % 4
            lnrp = f"slnr{lb % 2}"
            P.emit("sync", lambda e, tt=tt, lb=lb: e.dma_start(
                out=lbuf[lb].rearrange("p (j t) -> p j t", t=128),
                in_=ln1f.rearrange("(j p) t -> p j t", p=128)[:, :, tt * 128:(tt + 1) * 128]),
                waits=[(f"slnw{tt % 2}", 16 * (tt // 2 + 1))] + war_lbuf[lb], inc=lnrp, incby=16)
            lready = (lnrp, P.v(lnrp))
            bank = tt % 4
            for dt in range(ND):
                P.emit("tensor", lambda e, lb=lb, dt=dt, bank=bank: e.matmul(
                    pb[bank][:, :], lbuf[lb][:, dt * 128:(dt + 1) * 128],
                    wbufA[0][:, dt * 512:(dt + 1) * 512], start=(dt == 0), stop=False),
                    waits=([lready, ("sw", wready)] + war_qkbank[bank]) if dt == 0 else [])
            P.emit("tensor", lambda e, ch=ch, bank=bank: e.matmul(
                pb[bank][:, :], ones_r16[:, :], bqkv16[:, ch * 512:(ch + 1) * 512],
                start=False, stop=True),
                waits=[("sw", 16)], inc="spe")
            mm_done = P.v("spe")
            war_lbuf[lb] = [("spe", mm_done)]
            if tt == NT - 1:
                war_w = [("spe", mm_done)]
            if ch < 2:
                qs = tt % 2
                rope_emit(bank, tt, 4, qstage[qs][:, :],
                          [("spe", mm_done)] + war_qstage[qs])
                rope_done = P.v("sdve")
                for i in range(4):
                    hd_i = ch * 4 + i
                    P.emit("scalar", lambda e, qs=qs, i=i, hd_i=hd_i, tt=tt: e.activation(
                        sqscr[:, 0:128], qstage[qs][:, i * 128:(i + 1) * 128], AF.Square,
                        accum_out=normsq[:, hd_i * NT + tt:hd_i * NT + tt + 1]),
                        waits=[("sdve", rope_done)], inc="sact")
                sqn_done = P.v("sact")
                qwp = f"sqw{qs}"
                P.emit("sync", lambda e, qs=qs, tt=tt, ch=ch: e.dma_start(
                    out=qdram[tt * 128:(tt + 1) * 128, ch * 512:(ch + 1) * 512],
                    in_=qstage[qs][:, :]),
                    waits=[("sdve", rope_done)], inc=qwp, incby=16)
                war_qstage[qs] = [(qwp, P.v(qwp)), ("sact", sqn_done)]
                war_qkbank[bank] = [("sdve", rope_done)]
            else:
                rope_emit(bank, tt, 2, kv_tm[:, tt * 512:tt * 512 + 256],
                          [("spe", mm_done)])
                krope_done = P.v("sdve")
                for i in range(2):
                    P.emit("scalar", lambda e, i=i, tt=tt: e.activation(
                        sqscr[:, 0:128], kv_tm[:, tt * 512 + i * 128:tt * 512 + (i + 1) * 128],
                        AF.Square, accum_out=normsq[:, (8 + i) * NT + tt:(8 + i) * NT + tt + 1]),
                        waits=[("sdve", krope_done)])
                P.emit("scalar", lambda e, tt=tt, bank=bank: e.activation(
                    kv_tm[:, tt * 512 + 256:tt * 512 + 512], pb[bank][:, 256:512], AF.Copy),
                    waits=[("spe", mm_done)], inc="sact")
                war_qkbank[bank] = [("sdve", krope_done), ("sact", P.v("sact"))]
    A2_ACT = P.v("sact")
    A2_DVE = P.v("sdve")

    for g in range(2):
        for k4 in range(NT // 4):
            bank = 6 + (k4 % 2)
            for i in range(4):
                tt = 4 * k4 + i
                P.emit("tensor", lambda e, g=g, tt=tt, bank=bank, i=i: e.transpose(
                    pb16[bank][:, i * 128:(i + 1) * 128],
                    kv_tm[:, tt * 512 + g * 128:tt * 512 + (g + 1) * 128], ident_sb[:, :]),
                    waits=([("sact", A2_ACT), ("sdve", A2_DVE)] + war_bank[bank]) if i == 0 else [],
                    inc=("spe" if i == 3 else None))
            tr_done = P.v("spe")
            P.emit("vector", lambda e, g=g, k4=k4, bank=bank: e.tensor_copy(
                kfm[:, (g * NT + 4 * k4) * 128:(g * NT + 4 * k4) * 128 + 512], pb16[bank][:, 0:512]),
                waits=[("spe", tr_done)], inc="sdve")
            war_bank[bank] = [("sdve", P.v("sdve"))]

    war_nmax16 = []
    for hd_i in range(10):
        P.emit("vector", lambda e, hd_i=hd_i: e.reduce_max(
            nmax[:, :], normsq[:, hd_i * NT:(hd_i + 1) * NT], axis=AX.X),
            waits=[("sact", A2_ACT)], inc="sdve")
        P.emit("vector", lambda e: e.tensor_copy(nmax16[:, :], nmax[:, :]),
               waits=war_nmax16 + [("sdve", P.v("sdve"))], inc="sdve")
        rm = P.v("sdve")
        P.emit("tensor", lambda e: e.transpose(pb16[7][0:1, 0:128], nmax16[:, :], ident_sb[:, :]),
               waits=[("sdve", rm)] + war_bank[7], inc="spe")
        trm = P.v("spe")
        P.emit("vector", lambda e, hd_i=hd_i: e.reduce_max(
            maxsq[:, hd_i:hd_i + 1], pb16[7][0:1, 0:128], axis=AX.X),
            waits=[("spe", trm)], inc="sdve")
        war_bank[7] = [("sdve", P.v("sdve"))]
        war_nmax16 = [("spe", trm)]
    war_csc = []
    for lh in range(NHL):
        g = lh // 4
        P.emit("vector", lambda e, lh=lh, g=g: e.tensor_mul(
            csc[:, 0:1], maxsq[:, lh:lh + 1], maxsq[:, 8 + g:9 + g]),
            waits=war_csc + [("sdve", P.v("sdve"))], inc="sdve")
        md = P.v("sdve")
        P.emit("scalar", lambda e: e.activation(csc[:, 1:2], csc[:, 0:1], AF.Ln, bias=eps30[:, :]),
               waits=[("sdve", md)])
        P.emit("scalar", lambda e: e.activation(csc[:, 1:2], csc[:, 1:2], AF.Exp, scale=0.5), inc="sact")
        cd = P.v("sact")
        P.emit("vector", lambda e: e.tensor_scalar_mul(csc[:, 0:1], csc[:, 1:2], -SCALE),
               waits=[("sact", cd)], inc="sdve")
        nd_ = P.v("sdve")
        P.emit("tensor", lambda e: e.matmul(pb[7][:, 0:1], ones_rf[:, :], csc[:, 0:1],
                                            start=True, stop=True),
               waits=[("sdve", nd_)] + war_bank[7], inc="spe")
        bd = P.v("spe")
        war_csc = [("spe", bd)]
        P.emit("scalar", lambda e, lh=lh: e.activation(negbias[:, lh:lh + 1], pb[7][:, 0:1], AF.Copy),
               waits=[("spe", bd)], inc="sact")
        war_bank[7] = [("sact", P.v("sact"))]
    NEGB_ACT = P.v("sact")
    SQW0_ALL = P.v("sqw0")
    SQW1_ALL = P.v("sqw1")

    P.barrier()

    SCB = [0, 1, 5, 4]
    war_qtm = [[], []]
    war_qfm = [[], []]
    war_attnT = [[], [], [], []]
    war_attnP = [[], []]
    war_avsb = [[], []]
    war_pb4 = []
    war_sc = {0: [], 1: [], 5: [], 4: []}
    war_avbk = {2: [], 6: []}
    war_Dslot = [[], []]
    war_dbc_ps = []
    war_dbc_sb = []
    deferred = [None]

    def flush_epilogue():
        if deferred[0] is None:
            return
        (par, avbank, dslot, rdone) = deferred[0]
        deferred[0] = None
        P.emit("tensor", lambda e, par=par: e.matmul(
            pb[7][:, :], ones_rf[:, :], recipD2[par][:, :], start=True, stop=True),
            waits=[("sdve", rdone)] + war_dbc_ps, inc="spe")
        bc = P.v("spe")
        P.emit("scalar", lambda e: e.activation(dbc_sb[:, :], pb[7][:, :], AF.Copy),
               waits=[("spe", bc)] + war_dbc_sb, inc="sact")
        dbce = P.v("sact")
        war_dbc_ps[:] = [("sact", dbce)]
        P.emit("vector", lambda e, par=par, avbank=avbank: e.tensor_mul(
            av_sb[par][:, :], pb[avbank][:, :], dbc_sb[:, :]),
            waits=[("sact", dbce)] + war_avsb[par], inc="sdve")
        avn = P.v("sdve")
        war_dbc_sb[:] = [("sdve", avn)]
        war_avbk[avbank] = [("sdve", avn)]
        r_, lh_ = divmod_idx[0]
        j = r_ // 2
        avp = f"savw{par}"
        P.emit("sync", lambda e, lh_=lh_, r_=r_, j=j, par=par: e.dma_start(
            out=cc2_in[j * 1024 + lh_ * 128:j * 1024 + (lh_ + 1) * 128,
                       (r_ % 2) * 512:(r_ % 2) * 512 + 512],
            in_=av_sb[par][:, :]),
            waits=[("sdve", avn)], inc=avp, incby=16)
        war_avsb[par] = [(avp, P.v(avp))]

    divmod_idx = [None]
    for r in reversed(range(NR)):
        for lh in range(NHL):
            g = lh // 4
            it = (NR - 1 - r) * NHL + lh
            par = it % 2
            qrp = f"sqr{par}"
            P.emit("sync", lambda e, r=r, lh=lh, par=par: e.dma_start(
                out=qtm[par].rearrange("p (t c) -> p t c", c=128),
                in_=qdram.rearrange("(t p) c -> p t c", p=128)[:, 4 * r:4 * r + 4, lh * 128:(lh + 1) * 128]),
                waits=[("sqw0", SQW0_ALL), ("sqw1", SQW1_ALL)] + war_qtm[par], inc=qrp, incby=16)
            qld = (qrp, P.v(qrp))
            for i in range(4):
                P.emit("tensor", lambda e, par=par, i=i: e.transpose(
                    pb16[4][:, i * 128:(i + 1) * 128], qtm[par][:, i * 128:(i + 1) * 128],
                    ident_sb[:, :]),
                    waits=([qld] + war_pb4 + war_sc[4] + war_qfm[par]) if i == 0 else [],
                    inc=("spe" if i == 3 else None))
            war_qtm[par] = [("spe", P.v("spe"))]
            qtr = P.v("spe")
            P.emit("vector", lambda e, par=par: e.tensor_copy(qfm[par][:, :], pb16[4][:, 0:512]),
                   waits=[("spe", qtr)], inc="sdve")
            war_pb4 = [("sdve", P.v("sdve"))]
            war_sc[4] = []
            qfm_ready = P.v("sdve")

            kmax = 4 * r + 4
            exp_done = [0] * kmax
            mask_done = [0] * kmax
            padd_done = [0] * (kmax // 2)
            avbank = [2, 6][it % 2]
            dslot = it % 2
            pendAV = []
            pendD = []

            def emit_av(kt, last, g=g, avbank=avbank):
                at = kt % 4
                wts = [("sact", exp_done[kt])]
                if mask_done[kt]:
                    wts.append(("sdve", mask_done[kt]))
                if kt == 0:
                    wts += war_avbk[avbank]
                P.emit("tensor", lambda e, at=at, kt=kt, g=g, last=last, avbank=avbank: e.matmul(
                    pb[avbank][:, :], kv_tm[:, kt * 512 + 256 + g * 128:kt * 512 + 256 + (g + 1) * 128],
                    attnT[at][:, :], start=(kt == 0), stop=last), waits=wts, inc="spe")
                war_attnT[at] = war_attnT[at] + [("spe", P.v("spe"))]

            def emit_d(pp, last, dslot=dslot):
                wts = [("sdve", padd_done[pp])]
                if pp == 0:
                    wts += war_Dslot[dslot]
                P.emit("tensor", lambda e, pp=pp, last=last, dslot=dslot: e.matmul(
                    pb[3][64 * dslot:64 * dslot + 1, :], ones_c16[:, :], attnP[pp % 2][:, :],
                    start=(pp == 0), stop=last), waits=wts, inc="spe")
                war_attnP[pp % 2] = [("spe", P.v("spe"))]

            for kt in range(kmax):
                sb_ = SCB[kt % 4]
                P.emit("tensor", lambda e, g=g, kt=kt, sb_=sb_, par=par: e.matmul(
                    pb[sb_][:, :], kfm[:, (g * NT + kt) * 128:(g * NT + kt + 1) * 128],
                    qfm[par][:, :], start=True, stop=True),
                    waits=[("sdve", qfm_ready)] + war_sc[sb_], inc="spe")
                qk_done = P.v("spe")
                if kt == 1:
                    flush_epilogue()
                at = kt % 4
                P.emit("scalar", lambda e, sb_=sb_, at=at, lh=lh: e.activation(
                    attnT[at][:, :], pb[sb_][:, :], AF.Exp,
                    bias=negbias[:, lh:lh + 1], scale=SCALE),
                    waits=[("spe", qk_done), ("sact", NEGB_ACT)] + war_attnT[at], inc="sact")
                exp_done[kt] = P.v("sact")
                war_attnT[at] = []
                war_sc[sb_] = [("sact", exp_done[kt])]
                if kt >= 4 * r:
                    dm = kt - 4 * r
                    P.emit("vector", lambda e, at=at, dm=dm: e.tensor_mul(
                        attnT[at][:, :], attnT[at][:, :], masks_sb[:, dm * 512:(dm + 1) * 512]),
                        waits=[("sact", exp_done[kt]), ("sk", SK_ALL)], inc="sdve")
                    mask_done[kt] = P.v("sdve")
                if kt % 2 == 1:
                    pp = kt // 2
                    a0, a1 = (kt - 1) % 4, kt % 4
                    wts = [("sact", exp_done[kt])]
                    for kq in (kt - 1, kt):
                        if mask_done[kq]:
                            wts.append(("sdve", mask_done[kq]))
                    P.emit("vector", lambda e, a0=a0, a1=a1, pp=pp: e.tensor_add(
                        attnP[pp % 2][:, :], attnT[a0][:, :], attnT[a1][:, :]),
                        waits=wts + war_attnP[pp % 2], inc="sdve")
                    padd_done[pp] = P.v("sdve")
                    war_attnT[a0] = war_attnT[a0] + [("sdve", padd_done[pp])]
                    war_attnT[a1] = war_attnT[a1] + [("sdve", padd_done[pp])]
                    pendD.append(pp)
                pendAV.append(kt)
                if len(pendAV) >= 4:
                    emit_av(pendAV.pop(0), False)
                if len(pendD) >= 2:
                    emit_d(pendD.pop(0), False)
            while pendAV:
                k0 = pendAV.pop(0)
                emit_av(k0, k0 == kmax - 1)
            while pendD:
                p0 = pendD.pop(0)
                emit_d(p0, p0 == kmax // 2 - 1)
            av_done = P.v("spe")

            P.emit("vector", lambda e, par=par, dslot=dslot: e.reciprocal(
                recipD2[par][:, :], pb[3][64 * dslot:64 * dslot + 1, :]),
                waits=[("spe", av_done)], inc="sdve")
            rdone = P.v("sdve")
            war_Dslot[dslot] = [("sdve", rdone)]
            war_qfm[par] = [("spe", av_done)]
            divmod_idx[0] = (r, lh)
            deferred[0] = (par, avbank, dslot, rdone)
            if lh == NHL - 1 and r % 2 == 0:
                flush_epilogue()
                j = r // 2
                P.emit("gpsimd", lambda e, j=j: e.collective_compute(
                    "AllGather", ALU.bypass, replica_groups=[[0, 1, 2, 3], [4, 5, 6, 7]],
                    ins=[cc2_in[j * 1024:(j + 1) * 1024, :]], outs=[cc2_all[j]]),
                    waits=[("savw0", P.v("savw0")), ("savw1", P.v("savw1"))], inc=f"scc{j}")
    flush_epilogue()

    P.barrier()

    P.emit("vector", lambda e: e.tensor_copy(idxf[:, :], lsi_sb[:, :]),
           waits=[("sk", SK_ALL)], inc="sdve")
    P.emit("vector", lambda e: e.tensor_scalar(
        idxf[:, :], idxf[:, :], -1.0, bp1_sb[0:1, 0:1], ALU.mult, ALU.add),
        waits=[("sdve", P.v("sdve"))], inc="sdve")
    P.emit("vector", lambda e: e.tensor_scalar(
        idxf[:, :], idxf[:, :], 0.0, 1.0, ALU.max, ALU.min),
        waits=[("sdve", P.v("sdve"))], inc="sdve")
    g0, g1, g2, g3 = (idxf[0:1, i:i + 1] for i in range(4))

    def _gstep(fn):
        P.emit("vector", fn, waits=[("sdve", P.v("sdve"))], inc="sdve")

    _gstep(lambda e: e.tensor_scalar(gsc[0:1, 0:1], g1, -1.0, 1.0, ALU.mult, ALU.add))
    _gstep(lambda e: e.tensor_mul(gsc[0:1, 1:2], g0, gsc[0:1, 0:1]))
    _gstep(lambda e: e.tensor_scalar(gsc[0:1, 2:3], g1, 2.0, None, ALU.mult))
    _gstep(lambda e: e.tensor_add(gsc[0:1, 1:2], gsc[0:1, 1:2], gsc[0:1, 2:3]))
    _gstep(lambda e: e.tensor_scalar(gsc[0:1, 3:4], g2, -1.0, 1.0, ALU.mult, ALU.add))
    _gstep(lambda e: e.tensor_mul(gsc[0:1, 1:2], gsc[0:1, 1:2], gsc[0:1, 3:4]))
    _gstep(lambda e: e.tensor_scalar(gsc[0:1, 4:5], g3, 1.0, 3.0, ALU.mult, ALU.add))
    _gstep(lambda e: e.tensor_mul(gsc[0:1, 4:5], gsc[0:1, 4:5], g2))
    _gstep(lambda e: e.tensor_add(gsc[0:1, 1:2], gsc[0:1, 1:2], gsc[0:1, 4:5]))
    for j in range(3):
        _gstep(lambda e, j=j: e.tensor_scalar(
            gsc[0:1, 5:6], gsc[0:1, 1:2], 1.0, -float(j + 2), ALU.mult, ALU.add))
        _gstep(lambda e: e.tensor_mul(gsc[0:1, 6:7], gsc[0:1, 5:6], gsc[0:1, 5:6]))
        _gstep(lambda e: e.tensor_scalar(gsc[0:1, 6:7], gsc[0:1, 6:7], -1.0, 1.0, ALU.mult, ALU.add))
        _gstep(lambda e: e.tensor_scalar(gsc[0:1, 6:7], gsc[0:1, 6:7], 0.0, 1.0, ALU.max, ALU.min))
        _gstep(lambda e, j=j: e.tensor_scalar_mul(
            gex[0:1, 16 * j:16 * (j + 1)], ones_rf[0:1, 0:16], gsc[0:1, 6:7]))
    gexd = P.v("sdve")
    P.emit("tensor", lambda e: e.matmul(pb[7][0:48, 0:1], gex[:, :], ones_rf[0:1, 0:1],
                                        start=True, stop=True),
           waits=[("sdve", gexd)] + war_bank[7] + war_dbc_ps, inc="spe")
    gb = P.v("spe")
    P.emit("scalar", lambda e: e.activation(gmask[:, :], pb[7][0:48, 0:1], AF.Copy),
           waits=[("spe", gb)], inc="sact")
    GMASK_ACT = P.v("sact")

    P.emit("gpsimd", lambda e: e.dma_start(out=bias16[:, :], in_=bo[:, :]), inc="sw", incby=16)
    P.emit("gpsimd", lambda e: e.dma_start(out=lBT_sb[:, :], in_=lBT[:, :]), inc="sw", incby=16)
    P.emit("gpsimd", lambda e: e.dma_start(
        out=lAT_sb.rearrange("p (j r) -> p j r", r=48),
        in_=lAT.rearrange("(j p) r -> p j r", p=128)), inc="sw", incby=16)
    CCONST = P.v("sw")

    war_chk = [[], []]
    for jn, j in enumerate((3, 2, 1, 0)):
        for ft in range(NT):
            cb = (jn * NT + ft) % 2
            ccp = f"sccr{cb}"
            P.emit("sync", lambda e, ft=ft, j=j, cb=cb: e.dma_start(
                out=chkb[cb][:, :],
                in_=cc2_all[j, ft * 128:(ft + 1) * 128, :]),
                waits=[(f"scc{j}", 1)] + war_chk[cb], inc=ccp, incby=16)
            ld = (ccp, P.v(ccp))
            if jn == 0:
                P.emit("vector", lambda e, ft=ft, cb=cb, j=j: e.tensor_scalar_mul(
                    av_fm[:, ft * 1024:(ft + 1) * 1024], chkb[cb][:, :], sel_sb[:, j:j + 1]),
                    waits=[ld, ("sk", SK_ALL)], inc="sdve")
            else:
                P.emit("vector", lambda e, ft=ft, j=j, cb=cb: e.scalar_tensor_tensor(
                    av_fm[:, ft * 1024:(ft + 1) * 1024], chkb[cb][:, :], sel_sb[:, j:j + 1],
                    av_fm[:, ft * 1024:(ft + 1) * 1024], ALU.mult, ALU.add),
                    waits=[ld, ("sdve", P.v("sdve"))], inc="sdve")
            war_chk[cb] = [("sdve", P.v("sdve"))]
    AVSEL = P.v("sdve")

    war_wc = [[], []]
    war_xo = [[], []]
    war_x2cw = [[], []]
    war_cbank = {0: [], 1: []}
    for fo in range(NFO):
        wb = fo % 2
        wcp = f"swc{wb}"
        P.emit("gpsimd", lambda e, fo=fo, wb=wb: e.dma_start(
            out=wbufC[wb].rearrange("p (j n) -> p j n", n=512),
            in_=woT.rearrange("(j p) n -> p j n", p=128)[:, :, fo * 512:(fo + 1) * 512]),
            waits=war_wc[wb], inc=wcp, incby=16)
        wld = (wcp, P.v(wcp))
        for ts in range(8):
            xb_ = (fo * 8 + ts) % 2
            xop = f"sxo{xb_}"
            P.emit("sync", lambda e, fo=fo, ts=ts, xb_=xb_: e.dma_start(
                out=xoC[xb_][:, :],
                in_=x_own[ts * 128:(ts + 1) * 128, fo * 512:(fo + 1) * 512]),
                waits=war_xo[xb_], inc=xop, incby=16)
            xld = (xop, P.v(xop))
            bank = ts % 2
            for dt in range(ND):
                P.emit("tensor", lambda e, dt=dt, ts=ts, wb=wb, bank=bank: e.matmul(
                    pb[bank][:, :], av_fm[:, dt * 1024 + ts * 128:dt * 1024 + (ts + 1) * 128],
                    wbufC[wb][:, dt * 512:(dt + 1) * 512], start=(dt == 0), stop=False),
                    waits=([("sdve", AVSEL), wld] + war_cbank[bank]) if dt == 0 else [])
            P.emit("tensor", lambda e, fo=fo, bank=bank: e.matmul(
                pb[bank][:, :], ones_r16[:, :], bias16[:, fo * 512:(fo + 1) * 512],
                start=False, stop=True),
                waits=[("sw", CCONST)], inc="spe")
            mmd = P.v("spe")
            P.emit("vector", lambda e, xb_=xb_, bank=bank: e.tensor_add(
                x2cw[xb_][:, :], pb[bank][:, :], xoC[xb_][:, :]),
                waits=[("spe", mmd), xld] + war_x2cw[xb_], inc="sdve")
            od = P.v("sdve")
            war_cbank[bank] = [("sdve", od)]
            war_xo[xb_] = [("sdve", od)]
            x2p = f"sx2w{xb_}"
            P.emit("sync", lambda e, ts=ts, fo=fo, xb_=xb_: e.dma_start(
                out=x2d[ts * 128:(ts + 1) * 128, fo * 512:(fo + 1) * 512], in_=x2cw[xb_][:, :]),
                waits=[("sdve", od)], inc=x2p, incby=16)
            war_x2cw[xb_] = [(x2p, P.v(x2p))]
        war_wc[wb] = [("spe", P.v("spe"))]
    C1_PE = P.v("spe")
    X2W0 = P.v("sx2w0")
    X2W1 = P.v("sx2w1")
    P.emit("gpsimd", lambda e: e.dma_start(out=bias16[:, :], in_=bffn[:, :]),
           waits=[("spe", C1_PE)], inc="sw", incby=16)
    BFFN_SW = P.v("sw")

    war_x2s = [[], []]
    for ts in range(8):
        bi = ts % 2
        xrp = f"sx2r{bi}"
        P.emit("sync", lambda e, ts=ts, bi=bi: e.dma_start(
            out=x2s[bi][:, :], in_=x2d[ts * 128:(ts + 1) * 128, :]),
            waits=[("sx2w0", X2W0), ("sx2w1", X2W1)] + war_x2s[bi], inc=xrp, incby=16)
        xld = P.v(xrp)
        P.emit("vector", lambda e, bi=bi, ts=ts: e.reduce_sum(
            sums[:, ts:ts + 1], x2s[bi][:, :], axis=AX.X),
            waits=[(xrp, xld)], inc="sdve")
        P.emit("scalar", lambda e, bi=bi, ts=ts: e.activation(
            ln2_tm[:, :], x2s[bi][:, :], AF.Square, accum_out=sqs[:, ts:ts + 1]),
            waits=[(xrp, xld), ("spe", C1_PE), ("sact", GMASK_ACT)], inc="sact")
        war_x2s[bi] = [("sdve", P.v("sdve")), ("sact", P.v("sact"))]
    LN2P1_ACT = P.v("sact")
    RSTD2_ACT, NEGMR2_DVE = ln_stats_batch(8)

    war_x2s3 = [war_x2s[0], war_x2s[1]]
    war_ln2tm = [("spe", 0)]
    war_p = []
    for ts in range(8):
        bi = ts % 2
        xrp = f"sx2r{bi}"
        P.emit("sync", lambda e, ts=ts, bi=bi: e.dma_start(
            out=x2s[bi][:, :], in_=x2d[ts * 128:(ts + 1) * 128, :]),
            waits=war_x2s3[bi], inc=xrp, incby=16)
        xld = P.v(xrp)
        P.emit("scalar", lambda e, bi=bi, ts=ts: e.activation(
            ln2_tm[:, :], x2s[bi][:, :], AF.Identity,
            bias=negmrall[:, ts:ts + 1], scale=rstdall[:, ts:ts + 1]),
            waits=[(xrp, xld), ("sdve", NEGMR2_DVE), ("sact", RSTD2_ACT), war_ln2tm[0]],
            inc="sact")
        appd = P.v("sact")
        war_x2s3[bi] = [("sact", appd)]
        for k in range(8):
            bank2 = 6 + (k % 2)
            for i in range(4):
                dt = 4 * k + i
                P.emit("tensor", lambda e, dt=dt, bank2=bank2, i=i: e.transpose(
                    pb16[bank2][:, i * 128:(i + 1) * 128], ln2_tm[:, dt * 128:(dt + 1) * 128],
                    ident_sb[:, :]),
                    waits=([("sact", appd)] + war_bank[bank2]) if i == 0 else [],
                    inc=("spe" if i == 3 else None))
            trd = P.v("spe")
            P.emit("vector", lambda e, k=k, ts=ts, bank2=bank2: e.tensor_copy(
                ln2_fm.rearrange("p (j t) -> p j t", t=1024)[:, 4 * k:4 * k + 4, ts * 128:(ts + 1) * 128],
                pb16[bank2][:, 0:512].rearrange("p (j t) -> p j t", t=128)),
                waits=[("spe", trd)], inc="sdve")
            war_bank[bank2] = [("sdve", P.v("sdve"))]
        war_ln2tm = [("spe", P.v("spe"))]
        ln2_done = P.v("sdve")
        for dt in range(ND):
            P.emit("tensor", lambda e, dt=dt, ts=ts: e.matmul(
                pb[2][0:48, 0:128], lAT_sb[:, dt * 48:(dt + 1) * 48],
                ln2_fm[:, dt * 1024 + ts * 128:dt * 1024 + (ts + 1) * 128],
                start=(dt == 0), stop=(dt == ND - 1)),
                waits=([("sdve", ln2_done), ("sw", CCONST), ("sact", GMASK_ACT)] + war_p) if dt == 0 else [],
                inc=("spe" if dt == ND - 1 else None))
        pd = P.v("spe")
        P.emit("vector", lambda e, ts=ts: e.tensor_scalar_mul(
            p_sb[:, ts * 128:(ts + 1) * 128], pb[2][0:48, 0:128], gmask[:, 0:1]),
            waits=[("spe", pd), ("sact", GMASK_ACT)], inc="sdve")
        war_p = [("sdve", P.v("sdve"))]
    LN2_ALL = P.v("sdve")
    P_ALL = P.v("sdve")

    war_wc2 = [war_wc[0], war_wc[1]]
    war_out = [[], []]
    war_x2cr = [[], []]
    war_cbank2 = {0: war_cbank[0], 1: war_cbank[1]}
    for fo in range(NFO):
        wb = fo % 2
        wcp = f"swc{wb}"
        wts = list(war_wc2[wb])
        if fo == 0:
            wts += [war_ln2tm[0]]
        P.emit("gpsimd", lambda e, fo=fo, wb=wb: e.dma_start(
            out=wbufC[wb].rearrange("p (j n) -> p j n", n=512),
            in_=ffnT.rearrange("(j p) n -> p j n", p=128)[:, :, fo * 512:(fo + 1) * 512]),
            waits=wts, inc=wcp, incby=16)
        wld = (wcp, P.v(wcp))
        for ts in range(8):
            xb_ = (fo * 8 + ts) % 2
            xrp = f"sx2c{xb_}"
            P.emit("sync", lambda e, fo=fo, ts=ts, xb_=xb_: e.dma_start(
                out=x2cr[xb_][:, :],
                in_=x2d[ts * 128:(ts + 1) * 128, fo * 512:(fo + 1) * 512]),
                waits=[("sx2w0", X2W0), ("sx2w1", X2W1)] + war_x2cr[xb_], inc=xrp, incby=16)
            x2ld = (xrp, P.v(xrp))
            bank = ts % 2
            for dt in range(ND):
                P.emit("tensor", lambda e, dt=dt, ts=ts, wb=wb, bank=bank: e.matmul(
                    pb[bank][:, :], ln2_fm[:, dt * 1024 + ts * 128:dt * 1024 + (ts + 1) * 128],
                    wbufC[wb][:, dt * 512:(dt + 1) * 512], start=(dt == 0), stop=False),
                    waits=([("sdve", LN2_ALL), wld] + war_cbank2[bank]) if dt == 0 else [])
            P.emit("tensor", lambda e, fo=fo, bank=bank: e.matmul(
                pb[bank][:, :], ones_r16[:, :], bias16[:, fo * 512:(fo + 1) * 512],
                start=False, stop=False),
                waits=[("sw", BFFN_SW)])
            P.emit("tensor", lambda e, fo=fo, ts=ts, bank=bank: e.matmul(
                pb[bank][:, :], p_sb[:, ts * 128:(ts + 1) * 128],
                lBT_sb[:, fo * 512:(fo + 1) * 512], start=False, stop=True),
                waits=[("sdve", P_ALL)], inc="spe")
            mmd = P.v("spe")
            ob = (fo * 8 + ts) % 2
            P.emit("vector", lambda e, bank=bank, ob=ob, xb_=xb_: e.tensor_add(
                outsb[ob][:, :], pb[bank][:, :], x2cr[xb_][:, :]),
                waits=[("spe", mmd), x2ld] + war_out[ob], inc="sdve")
            od = P.v("sdve")
            war_cbank2[bank] = [("sdve", od)]
            war_x2cr[xb_] = [("sdve", od)]
            sop = f"sout{ob}"
            P.emit("sync", lambda e, ts=ts, fo=fo, ob=ob: e.dma_start(
                out=out[ts * 128:(ts + 1) * 128, fo * 512:(fo + 1) * 512], in_=outsb[ob][:, :]),
                waits=[("sdve", od)], inc=sop, incby=16)
            war_out[ob] = [(sop, P.v(sop))]
        war_wc2[wb] = [("spe", P.v("spe"))]

    P.emit("sync", lambda e: e.nop(), waits=[("sout0", P.v("sout0")), ("sout1", P.v("sout1"))])

    P.replay()
    return nc


_NC_CACHE = None


def kernel(L0_attn_x, L0_attn_w_qkv, L0_attn_b_qkv, L0_attn_w_o, L0_attn_b_o,
           L0_ffn_weight, L0_ffn_bias, L0_ffn_lora_A, L0_ffn_lora_B,
           L0_ffn_lora_start_indices):
    global _NC_CACHE
    x = np.asarray(L0_attn_x, np.float32)
    w_qkv = np.asarray(L0_attn_w_qkv, np.float32)
    b_qkv = np.asarray(L0_attn_b_qkv, np.float32)
    w_o = np.asarray(L0_attn_w_o, np.float32)
    b_o = np.asarray(L0_attn_b_o, np.float32)
    w_ffn = np.asarray(L0_ffn_weight, np.float32)
    b_ffn = np.asarray(L0_ffn_bias, np.float32)
    lora_A = np.asarray(L0_ffn_lora_A, np.float32)
    lora_B = np.asarray(L0_ffn_lora_B, np.float32)
    lsi = np.asarray(L0_ffn_lora_start_indices).astype(np.int32, copy=False)
    lsi = np.ascontiguousarray(lsi)

    woT = np.ascontiguousarray(w_o.T)
    ffnT = np.ascontiguousarray(w_ffn.T)
    lAT = np.ascontiguousarray(lora_A.transpose(2, 0, 1).reshape(4096, 48))
    lBT = np.ascontiguousarray(lora_B.transpose(0, 2, 1).reshape(48, 4096))

    inv_freq = (1.0 / 10000.0 ** (np.arange(0, HD, 2, dtype=np.float32) / HD)).astype(np.float32)
    pos = np.arange(S, dtype=np.float32)
    ang = pos[:, None] * inv_freq[None, :]
    cosT = np.cos(ang).astype(np.float32)
    sinT = np.sin(ang).astype(np.float32)
    ident = np.eye(128, dtype=ml_dtypes.bfloat16)
    kk = np.arange(128)[:, None]
    qq = np.arange(512)[None, :]
    mask4 = np.stack([(kk + 128 * d <= qq) for d in range(4)]).astype(ml_dtypes.bfloat16)

    if _NC_CACHE is None:
        _NC_CACHE = build_nc()
    nc = _NC_CACHE

    xb = [np.ascontiguousarray(x[0]), np.ascontiguousarray(x[1])]
    in_maps = []
    for c in range(8):
        B, m = c // 4, c % 4
        rows = np.concatenate([
            np.arange(m * QC, (m + 1) * QC),
            np.arange(4096 + m * KVC, 4096 + (m + 1) * KVC),
            np.arange(5120 + m * KVC, 5120 + (m + 1) * KVC),
        ])
        wq = np.ascontiguousarray(w_qkv[rows].T)
        bq = np.ascontiguousarray(b_qkv[rows])[None, :]
        sel = np.zeros((128, 4), np.float32)
        sel[:, m] = 1.0
        in_maps.append({
            "x_batch": xb[B],
            "x_own": np.ascontiguousarray(xb[B][m * 1024:(m + 1) * 1024]),
            "wqkvT": wq, "bqkv": bq,
            "woT": woT, "bo": b_o[None, :],
            "ffnT": ffnT, "bffn": b_ffn[None, :],
            "lAT": lAT, "lBT": lBT, "lsi": lsi,
            "bplus1": np.array([[float(B + 1)]], np.float32),
            "selmask": sel,
            "cosT": cosT, "sinT": sinT,
            "identt": ident, "mask4": mask4,
        })

    res = run_bass_kernel_spmd(nc, in_maps, core_ids=list(range(8)))
    full = np.empty((2, 4096, 4096), np.float32)
    for c in range(8):
        B, m = c // 4, c % 4
        full[B, m * 1024:(m + 1) * 1024] = res.results[c]["out"]
    return full


# version 06
# speedup vs baseline: 1.3176x; 1.0748x over previous
import math
import numpy as np
import ml_dtypes
import concourse.bass as bass
import concourse.mybir as mybir
from concourse.bass_utils import run_bass_kernel_spmd

F32 = mybir.dt.float32
BF16 = mybir.dt.bfloat16
I32 = mybir.dt.int32
U8 = mybir.dt.uint8
AF = mybir.ActivationFunctionType
ALU = mybir.AluOpType
AX = mybir.AxisListType

D = 4096
S = 4096
HD = 128
NHL = 8
QC = NHL * HD
KVC = 256
COLS = QC + 2 * KVC
NT = S // 128
ND = D // 128
NR = S // 512
NFO = 8
SCALE = 1.0 / math.sqrt(HD)
LN_EPS = 1e-5


class Plan:

    ENGS = ("sync", "gpsimd", "vector", "scalar", "tensor")

    def __init__(self, nc):
        self.nc = nc
        self.q = {e: [] for e in self.ENGS}
        self.h = {}
        self.c = {}
        self.cuts = []

    def sem(self, name):
        self.h[name] = self.nc.alloc_semaphore(name)
        self.c[name] = 0

    def emit(self, eng, fn, waits=(), inc=None, incby=1):
        if inc is not None:
            self.c[inc] += incby
        self.q[eng].append((tuple(waits), fn, inc, incby))

    def v(self, name):
        return self.c[name]

    def barrier(self):
        self.cuts.append({e: len(self.q[e]) for e in self.ENGS})

    def replay(self):
        nc = self.nc
        self.cuts.append({e: len(self.q[e]) for e in self.ENGS})
        start = {e: 0 for e in self.ENGS}
        lastw = {e: {} for e in self.ENGS}
        for seg_i, cut in enumerate(self.cuts):
            with nc.Block(f"ph{seg_i}") as block:
                for ename in self.ENGS:
                    items = self.q[ename][start[ename]:cut[ename]]
                    if not items:
                        continue

                    def make_body(items=items, lw=lastw[ename], plan=self):
                        def body(eng):
                            for waits, fn, inc, incby in items:
                                for sname, val in waits:
                                    if val > 0 and lw.get(sname, 0) < val:
                                        eng.wait_ge(plan.h[sname], val)
                                        lw[sname] = val
                                inst = fn(eng)
                                if inc is not None:
                                    inst.then_inc(plan.h[inc], incby)
                        return body

                    getattr(block, ename)(make_body())
            start = dict(cut)
            if seg_i < len(self.cuts) - 1:
                nc.all_engine_barrier()


class Carver:
    def __init__(self, arena_ap, nbytes):
        self.a = arena_ap
        self.n = nbytes
        self.cur = 0

    def take(self, nbytes, dtype, parts=128, base=0):
        off = (self.cur + 31) & ~31
        self.cur = off + nbytes
        assert self.cur <= self.n, f"arena overflow: {self.cur} > {self.n}"
        return self.a[base:base + parts, off:off + nbytes].bitcast(dtype)

    def mark(self):
        return self.cur

    def reset(self, pos):
        self.cur = pos


def build_nc():
    nc = bass.Bass("TRN2", target_bir_lowering=False, debug=False)
    ap = lambda t: t.ap()

    x_batch = ap(nc.dram_tensor("x_batch", [S, D], F32, kind="ExternalInput"))
    x_own = ap(nc.dram_tensor("x_own", [1024, D], F32, kind="ExternalInput"))
    wqkvT = ap(nc.dram_tensor("wqkvT", [D, COLS], F32, kind="ExternalInput"))
    bqkv = ap(nc.dram_tensor("bqkv", [1, COLS], F32, kind="ExternalInput"))
    woT = ap(nc.dram_tensor("woT", [D, D], F32, kind="ExternalInput"))
    bo = ap(nc.dram_tensor("bo", [1, D], F32, kind="ExternalInput"))
    ffnT = ap(nc.dram_tensor("ffnT", [D, D], F32, kind="ExternalInput"))
    bffn = ap(nc.dram_tensor("bffn", [1, D], F32, kind="ExternalInput"))
    lAT = ap(nc.dram_tensor("lAT", [D, 48], F32, kind="ExternalInput"))
    lBT = ap(nc.dram_tensor("lBT", [48, D], F32, kind="ExternalInput"))
    lsi = ap(nc.dram_tensor("lsi", [4], I32, kind="ExternalInput"))
    bplus1 = ap(nc.dram_tensor("bplus1", [1, 1], F32, kind="ExternalInput"))
    selmask = ap(nc.dram_tensor("selmask", [128, 4], F32, kind="ExternalInput"))
    cosT = ap(nc.dram_tensor("cosT", [S, 64], F32, kind="ExternalInput"))
    sinT = ap(nc.dram_tensor("sinT", [S, 64], F32, kind="ExternalInput"))
    identt = ap(nc.dram_tensor("identt", [128, 128], BF16, kind="ExternalInput"))
    mask4 = ap(nc.dram_tensor("mask4", [4, 128, 512], BF16, kind="ExternalInput"))
    out = ap(nc.dram_tensor("out", [1024, D], F32, kind="ExternalOutput"))

    ln1f = ap(nc.dram_tensor("ln1f", [D, S], BF16, kind="Internal"))
    qdram = ap(nc.dram_tensor("qdram", [S, QC], BF16, kind="Internal"))
    cc2_in = ap(nc.dram_tensor("cc2_in", [8 * QC, 512], BF16, kind="Internal"))
    cc2_all = ap(nc.dram_tensor("cc2_all", [8, 4 * QC, 512], BF16, kind="Internal"))
    x2d = ap(nc.dram_tensor("x2d", [1024, D], BF16, kind="Internal"))

    A = lambda *a, **k: nc.alloc_sbuf_tensor(*a, **k).ap()
    ident_sb = A("ident_sb", [128, 128], BF16)
    ones_r16 = A("ones_r16", [1, 128], BF16)
    ones_c16 = A("ones_c16", [128, 1], BF16)
    ones_rf = A("ones_rf", [1, 128], F32)
    epscol = A("epscol", [128, 1], F32)
    eps30 = A("eps30", [1, 1], F32)
    sel_sb = A("sel_sb", [128, 4], F32)
    bp1_sb = A("bp1_sb", [1, 1], F32)
    lsi_sb = A("lsi_sb", [1, 4], I32)
    idxf = A("idxf", [1, 4], F32)
    gsc = A("gsc", [1, 16], F32)
    gex = A("gex", [1, 48], F32)
    gmask = A("gmask", [48, 1], F32)
    normsq = A("normsq", [128, 10 * NT], F32)
    nmax = A("nmax", [128, 1], F32)
    nmax16 = A("nmax16", [128, 1], BF16)
    maxsq = A("maxsq", [1, 16], F32)
    negbias = A("negbias", [128, NHL], F32)
    sums = A("sums", [128, NT], F32)
    sqs = A("sqs", [128, NT], F32)
    mall = A("mall", [128, NT], F32)
    msqall = A("msqall", [128, NT], F32)
    varall = A("varall", [128, NT], F32)
    rstdall = A("rstdall", [128, NT], F32)
    negmrall = A("negmrall", [128, NT], F32)
    csc = A("csc", [1, 2], F32)
    recipD2 = [A(f"recipD{i}", [1, 512], F32) for i in range(2)]
    dbc_sb = A("dbc_sb", [128, 512], F32)
    masks_sb = A("masks_sb", [128, 4 * 512], BF16)
    sqscr = A("sqscr", [128, 128], BF16)

    ARENA = 192 * 1024
    arena = A("arena", [128, ARENA], U8)
    cv = Carver(arena, ARENA)

    kv_tm = cv.take(NT * 512 * 2, BF16)
    kfm = cv.take(2 * S * 2, BF16)
    ab_end = cv.mark()
    assert ab_end <= 48 * 1024

    cv.reset(48 * 1024)
    qtm = [cv.take(512 * 2, BF16) for _ in range(2)]
    qfm = [cv.take(512 * 2, BF16) for _ in range(2)]
    attnT = [cv.take(512 * 2, BF16) for _ in range(4)]
    attnP = [cv.take(512 * 2, BF16) for _ in range(2)]
    av_sb = [cv.take(512 * 2, BF16) for _ in range(2)]
    assert cv.mark() <= 62 * 1024
    cv.reset(62 * 1024)
    chkb = [cv.take(512 * 2, BF16) for _ in range(2)]
    assert cv.mark() <= 64 * 1024
    cv.reset(64 * 1024)
    av_fm = cv.take(NT * 1024 * 2, BF16)
    ln2_fm = av_fm
    assert cv.mark() <= 128 * 1024
    cv.reset(128 * 1024)
    wbufC = [cv.take(ND * 512 * 2, BF16) for _ in range(2)]
    ln2_tm = wbufC[0][:, 0:D]
    assert cv.mark() <= ARENA

    cv.reset(48 * 1024)
    xh = [cv.take(D * 4, F32) for _ in range(2)]
    ln1_tm = [cv.take(D * 2, BF16) for _ in range(2)]
    evacb = [cv.take(D * 2, BF16) for _ in range(2)]
    wbufA = [cv.take(ND * 512 * 2, BF16)]
    lbuf = [cv.take(ND * 128 * 2, BF16) for _ in range(2)]
    lbuf = lbuf + [xh[i].bitcast(BF16)[:, 0:ND * 128] for i in range(2)]
    qstage = [cv.take(512 * 2, BF16) for _ in range(2)]
    t1f = cv.take(256 * 4, F32)
    t2f = cv.take(256 * 4, F32)
    cos_sb = cv.take(NT * 64 * 4, F32)
    sin_sb = cv.take(NT * 64 * 4, F32)
    bqkv16 = cv.take(COLS * 2, BF16, parts=1)
    assert cv.mark() <= ARENA

    cv.reset(0)
    x2s = [cv.take(D * 2, BF16) for _ in range(2)]
    x2cw = [cv.take(512 * 2, BF16) for _ in range(2)]
    x2cr = [cv.take(512 * 2, BF16) for _ in range(2)]
    xoC = [cv.take(512 * 4, F32) for _ in range(2)]
    outsb = [cv.take(512 * 4, F32) for _ in range(2)]
    p_sb = cv.take(1024 * 2, BF16, parts=48)
    assert cv.mark() <= 32 * 1024
    cv.reset(32 * 1024)
    lBT_sb = cv.take(D * 2, BF16, parts=48)
    lAT_sb = cv.take(ND * 48 * 2, BF16)
    assert cv.mark() <= 48 * 1024
    cv.reset(48 * 1024)
    bias16 = cv.take(D * 2, BF16, parts=1)
    assert cv.mark() <= 62 * 1024

    pb = [nc.alloc_psum_tensor(f"pb{i}", [128, 512], F32).ap() for i in range(8)]
    pb16 = [p.bitcast(BF16) for p in pb]

    P = Plan(nc)
    for s in ("sx0", "sx1", "slnw0", "slnw1", "slnr0", "slnr1", "slnr2", "slnr3", "sw",
              "swc0", "swc1", "sqw0", "sqw1", "sqr0", "sqr1", "savw0", "savw1",
              "scc0", "scc1", "scc2", "scc3", "scc4", "scc5", "scc6", "scc7", "sccr0", "sccr1",
              "sxo0", "sxo1", "sx2w0", "sx2w1", "sx2r0", "sx2r1",
              "sx2c0", "sx2c1", "sk", "sout0", "sout1",
              "spe", "sact", "sdve"):
        P.sem(s)

    war_bank = {i: [] for i in range(8)}

    P.emit("sync", lambda e: e.dma_start(out=ident_sb[:, :], in_=identt[:, :]), inc="sk", incby=16)
    P.emit("sync", lambda e: e.dma_start(out=sel_sb[:, :], in_=selmask[:, :]), inc="sk", incby=16)
    P.emit("sync", lambda e: e.dma_start(out=bp1_sb[:, :], in_=bplus1[:, :]), inc="sk", incby=16)
    P.emit("sync", lambda e: e.dma_start(out=lsi_sb[:, :], in_=lsi.unsqueeze(0)), inc="sk", incby=16)
    P.emit("sync", lambda e: e.dma_start(
        out=masks_sb.rearrange("p (d q) -> p d q", d=4),
        in_=mask4.rearrange("d p q -> p d q")), inc="sk", incby=16)
    P.emit("sync", lambda e: e.dma_start(
        out=cos_sb.rearrange("p (t f) -> p t f", f=64),
        in_=cosT.rearrange("(t p) f -> p t f", p=128)), inc="sk", incby=16)
    P.emit("sync", lambda e: e.dma_start(
        out=sin_sb.rearrange("p (t f) -> p t f", f=64),
        in_=sinT.rearrange("(t p) f -> p t f", p=128)), inc="sk", incby=16)
    SK_ALL = P.v("sk")
    P.emit("gpsimd", lambda e: e.dma_start(out=bqkv16[:, :], in_=bqkv[:, :]), inc="sw", incby=16)
    P.emit("vector", lambda e: e.memset(ones_r16[:, :], 1.0))
    P.emit("vector", lambda e: e.memset(ones_c16[:, :], 1.0))
    P.emit("vector", lambda e: e.memset(ones_rf[:, :], 1.0))
    P.emit("vector", lambda e: e.memset(epscol[:, :], LN_EPS))
    P.emit("vector", lambda e: e.memset(eps30[:, :], 1e-30), inc="sdve")
    DVE_CONST = P.v("sdve")

    war_x = [[], []]
    for st in range(NT):
        bi = st % 2
        sxp = f"sx{bi}"
        for hf in range(2):
            P.emit("sync", lambda e, st=st, hf=hf, bi=bi: e.dma_start(
                out=xh[bi][:, hf * 2048:(hf + 1) * 2048],
                in_=x_batch[st * 128:(st + 1) * 128, hf * 2048:(hf + 1) * 2048]),
                waits=war_x[bi], inc=sxp, incby=16)
        xload = P.v(sxp)
        P.emit("vector", lambda e, bi=bi, st=st: e.reduce_sum(
            sums[:, st:st + 1], xh[bi][:, :], axis=AX.X),
            waits=[(sxp, xload)], inc="sdve")
        P.emit("scalar", lambda e, bi=bi, st=st: e.activation(
            ln1_tm[bi][:, :], xh[bi][:, :], AF.Square, accum_out=sqs[:, st:st + 1]),
            waits=[(sxp, xload)], inc="sact")
        war_x[bi] = [("sdve", P.v("sdve")), ("sact", P.v("sact"))]
    P1_DVE = P.v("sdve")
    P1_ACT = P.v("sact")

    def ln_stats_batch(ncols):
        P.emit("vector", lambda e: e.tensor_scalar_mul(mall[:, 0:ncols], sums[:, 0:ncols], 1.0 / D),
               waits=[("sdve", P.v("sdve")), ("sact", P.v("sact"))], inc="sdve")
        P.emit("vector", lambda e: e.tensor_mul(msqall[:, 0:ncols], mall[:, 0:ncols], mall[:, 0:ncols]),
               waits=[("sdve", P.v("sdve"))], inc="sdve")
        P.emit("vector", lambda e: e.scalar_tensor_tensor(
            varall[:, 0:ncols], sqs[:, 0:ncols], 1.0 / D, msqall[:, 0:ncols],
            ALU.mult, ALU.subtract),
            waits=[("sdve", P.v("sdve"))], inc="sdve")
        vd = P.v("sdve")
        P.emit("scalar", lambda e: e.activation(rstdall[:, 0:ncols], varall[:, 0:ncols], AF.Ln, bias=epscol[:, :]),
               waits=[("sdve", vd)])
        P.emit("scalar", lambda e: e.activation(rstdall[:, 0:ncols], rstdall[:, 0:ncols], AF.Exp, scale=-0.5),
               inc="sact")
        rstd_done = P.v("sact")
        P.emit("vector", lambda e: e.scalar_tensor_tensor(
            negmrall[:, 0:ncols], mall[:, 0:ncols], -1.0, rstdall[:, 0:ncols], ALU.mult, ALU.mult),
            waits=[("sact", rstd_done), ("sdve", vd)], inc="sdve")
        return rstd_done, P.v("sdve")

    RSTD1_ACT, NEGMR1_DVE = ln_stats_batch(NT)

    def rope_emit(bank, tt, nh, dst, pre_waits):
        src_ = pb[bank][:, 0:nh * 128].rearrange("p (h r two) -> p h r two", h=nh, two=2)
        xr, xi = src_[:, :, :, 0], src_[:, :, :, 1]
        cosb = cos_sb.rearrange("p (t f) -> p t f", f=64)[:, tt, :].unsqueeze(1).broadcast_to([128, nh, 64])
        sinb = sin_sb.rearrange("p (t f) -> p t f", f=64)[:, tt, :].unsqueeze(1).broadcast_to([128, nh, 64])
        t1 = t1f[:, 0:nh * 64].rearrange("p (h r) -> p h r", h=nh)
        t2 = t2f[:, 0:nh * 64].rearrange("p (h r) -> p h r", h=nh)
        dstv = dst.rearrange("p (h r two) -> p h r two", h=nh, two=2)
        P.emit("vector", lambda e, t1=t1, xr=xr, cosb=cosb: e.tensor_tensor(t1, xr, cosb, ALU.mult),
               waits=pre_waits + [("sk", SK_ALL)])
        P.emit("vector", lambda e, t2=t2, xi=xi, sinb=sinb: e.tensor_tensor(t2, xi, sinb, ALU.mult))
        P.emit("vector", lambda e, dstv=dstv, t1=t1, t2=t2: e.tensor_tensor(dstv[:, :, :, 0], t1, t2, ALU.subtract))
        P.emit("vector", lambda e, t1=t1, xr=xr, sinb=sinb: e.tensor_tensor(t1, xr, sinb, ALU.mult))
        P.emit("vector", lambda e, t2=t2, xi=xi, cosb=cosb: e.tensor_tensor(t2, xi, cosb, ALU.mult))
        P.emit("vector", lambda e, dstv=dstv, t1=t1, t2=t2: e.tensor_tensor(dstv[:, :, :, 1], t1, t2, ALU.add),
               inc="sdve")

    P.emit("gpsimd", lambda e: e.dma_start(
        out=wbufA[0].rearrange("p (j n) -> p j n", n=512),
        in_=wqkvT.rearrange("(j p) n -> p j n", p=128)[:, :, 0:512]),
        inc="sw", incby=16)
    W0_SW = P.v("sw")
    war_x3 = [war_x[0], war_x[1]]
    war_lntm = [[], []]
    war_evacb = [[], []]
    war_qkbank = {0: [], 1: [], 2: [], 3: []}
    war_qstage = [[], []]
    for st in range(NT):
        bi = st % 2
        sxp = f"sx{bi}"
        P.emit("sync", lambda e, st=st, bi=bi: e.dma_start(
            out=xh[bi][:, :], in_=x_batch[st * 128:(st + 1) * 128, :]),
            waits=war_x3[bi], inc=sxp, incby=16)
        xload = P.v(sxp)
        P.emit("scalar", lambda e, bi=bi, st=st: e.activation(
            ln1_tm[bi][:, :], xh[bi][:, :], AF.Identity,
            bias=negmrall[:, st:st + 1], scale=rstdall[:, st:st + 1]),
            waits=[(sxp, xload), ("sdve", NEGMR1_DVE), ("sact", RSTD1_ACT)] + war_lntm[bi],
            inc="sact")
        apply_done = P.v("sact")
        war_x3[bi] = [("sact", apply_done)]
        for k in range(4):
            bank = 6 + (k % 2)
            for i in range(8):
                dt = 8 * k + i
                P.emit("tensor", lambda e, bi=bi, dt=dt, bank=bank, i=i: e.transpose(
                    pb16[bank][:, i * 128:(i + 1) * 128],
                    ln1_tm[bi][:, dt * 128:(dt + 1) * 128], ident_sb[:, :]),
                    waits=([("sact", apply_done), ("sk", SK_ALL)] + war_bank[bank]) if i == 0 else [],
                    inc=("spe" if i == 7 else None))
            tr_done = P.v("spe")
            P.emit("vector", lambda e, bank=bank, bi=bi, k=k: e.tensor_copy(
                evacb[bi][:, k * 1024:(k + 1) * 1024], pb16[bank][:, :]),
                waits=[("spe", tr_done)] + (war_evacb[bi] if k == 0 else []), inc="sdve")
            war_bank[bank] = [("sdve", P.v("sdve"))]
        ev_done = P.v("sdve")
        lnwp = f"slnw{bi}"
        P.emit("sync", lambda e, st=st, bi=bi: e.dma_start(
            out=ln1f.rearrange("(j p) t -> p j t", p=128)[:, :, st * 128:(st + 1) * 128],
            in_=evacb[bi][:, :].rearrange("p (j t) -> p j t", t=128)),
            waits=[("sdve", ev_done)], inc=lnwp, incby=16)
        war_lntm[bi] = [("spe", P.v("spe"))]
        bank = st % 4
        for dt in range(ND):
            P.emit("tensor", lambda e, bi=bi, dt=dt, bank=bank: e.matmul(
                pb[bank][:, :], evacb[bi][:, dt * 128:(dt + 1) * 128],
                wbufA[0][:, dt * 512:(dt + 1) * 512], start=(dt == 0), stop=False),
                waits=([("sdve", ev_done), ("sw", W0_SW)] + war_qkbank[bank]) if dt == 0 else [])
        P.emit("tensor", lambda e, bank=bank: e.matmul(
            pb[bank][:, :], ones_r16[:, :], bqkv16[:, 0:512],
            start=False, stop=True),
            waits=[("sw", 16)], inc="spe")
        mm_done = P.v("spe")
        war_evacb[bi] = [(lnwp, P.v(lnwp)), ("spe", mm_done)]
        qs = st % 2
        rope_emit(bank, st, 4, qstage[qs][:, :],
                  [("spe", mm_done)] + war_qstage[qs])
        rope_done = P.v("sdve")
        for i in range(4):
            P.emit("scalar", lambda e, qs=qs, i=i, st=st: e.activation(
                sqscr[:, :], qstage[qs][:, i * 128:(i + 1) * 128], AF.Square,
                accum_out=normsq[:, i * NT + st:i * NT + st + 1]),
                waits=[("sdve", rope_done)], inc="sact")
        sqn_done = P.v("sact")
        qwp = f"sqw{qs}"
        P.emit("sync", lambda e, qs=qs, st=st: e.dma_start(
            out=qdram[st * 128:(st + 1) * 128, 0:512], in_=qstage[qs][:, :]),
            waits=[("sdve", rope_done)], inc=qwp, incby=16)
        war_qstage[qs] = [(qwp, P.v(qwp)), ("sact", sqn_done)]
        war_qkbank[bank] = [("sdve", rope_done)]
    A2CH0_ACT = P.v("sact")

    war_w = [("spe", P.v("spe"))]
    war_lbuf = [[], [], [("sact", A2CH0_ACT)], [("sact", A2CH0_ACT)]]
    for ch in (1, 2):
        P.emit("gpsimd", lambda e, ch=ch: e.dma_start(
            out=wbufA[0].rearrange("p (j n) -> p j n", n=512),
            in_=wqkvT.rearrange("(j p) n -> p j n", p=128)[:, :, ch * 512:(ch + 1) * 512]),
            waits=war_w, inc="sw", incby=16)
        wready = P.v("sw")
        for tt in range(NT):
            lb = tt % 4
            lnrp = f"slnr{lb}"
            P.emit("sync", lambda e, tt=tt, lb=lb: e.dma_start(
                out=lbuf[lb].rearrange("p (j t) -> p j t", t=128),
                in_=ln1f.rearrange("(j p) t -> p j t", p=128)[:, :, tt * 128:(tt + 1) * 128]),
                waits=[(f"slnw{tt % 2}", 16 * (tt // 2 + 1))] + war_lbuf[lb], inc=lnrp, incby=16)
            lready = (lnrp, P.v(lnrp))
            bank = tt % 4
            for dt in range(ND):
                P.emit("tensor", lambda e, lb=lb, dt=dt, bank=bank: e.matmul(
                    pb[bank][:, :], lbuf[lb][:, dt * 128:(dt + 1) * 128],
                    wbufA[0][:, dt * 512:(dt + 1) * 512], start=(dt == 0), stop=False),
                    waits=([lready, ("sw", wready)] + war_qkbank[bank]) if dt == 0 else [])
            P.emit("tensor", lambda e, ch=ch, bank=bank: e.matmul(
                pb[bank][:, :], ones_r16[:, :], bqkv16[:, ch * 512:(ch + 1) * 512],
                start=False, stop=True),
                waits=[("sw", 16)], inc="spe")
            mm_done = P.v("spe")
            war_lbuf[lb] = [("spe", mm_done)]
            if tt == NT - 1:
                war_w = [("spe", mm_done)]
            if ch < 2:
                qs = tt % 2
                rope_emit(bank, tt, 4, qstage[qs][:, :],
                          [("spe", mm_done)] + war_qstage[qs])
                rope_done = P.v("sdve")
                for i in range(4):
                    hd_i = ch * 4 + i
                    P.emit("scalar", lambda e, qs=qs, i=i, hd_i=hd_i, tt=tt: e.activation(
                        sqscr[:, :], qstage[qs][:, i * 128:(i + 1) * 128], AF.Square,
                        accum_out=normsq[:, hd_i * NT + tt:hd_i * NT + tt + 1]),
                        waits=[("sdve", rope_done)], inc="sact")
                sqn_done = P.v("sact")
                qwp = f"sqw{qs}"
                P.emit("sync", lambda e, qs=qs, tt=tt, ch=ch: e.dma_start(
                    out=qdram[tt * 128:(tt + 1) * 128, ch * 512:(ch + 1) * 512],
                    in_=qstage[qs][:, :]),
                    waits=[("sdve", rope_done)], inc=qwp, incby=16)
                war_qstage[qs] = [(qwp, P.v(qwp)), ("sact", sqn_done)]
                war_qkbank[bank] = [("sdve", rope_done)]
            else:
                rope_emit(bank, tt, 2, kv_tm[:, tt * 512:tt * 512 + 256],
                          [("spe", mm_done)])
                krope_done = P.v("sdve")
                for i in range(2):
                    P.emit("scalar", lambda e, i=i, tt=tt: e.activation(
                        sqscr[:, :], kv_tm[:, tt * 512 + i * 128:tt * 512 + (i + 1) * 128],
                        AF.Square, accum_out=normsq[:, (8 + i) * NT + tt:(8 + i) * NT + tt + 1]),
                        waits=[("sdve", krope_done)])
                P.emit("scalar", lambda e, tt=tt, bank=bank: e.activation(
                    kv_tm[:, tt * 512 + 256:tt * 512 + 512], pb[bank][:, 256:512], AF.Copy),
                    waits=[("spe", mm_done)], inc="sact")
                war_qkbank[bank] = [("sdve", krope_done), ("sact", P.v("sact"))]
    A2_ACT = P.v("sact")
    A2_DVE = P.v("sdve")

    for g in range(2):
        for k8 in range(NT // 8):
            bank = 6 + (k8 % 2)
            for i in range(8):
                tt = 8 * k8 + i
                P.emit("tensor", lambda e, g=g, tt=tt, bank=bank, i=i: e.transpose(
                    pb16[bank][:, i * 128:(i + 1) * 128],
                    kv_tm[:, tt * 512 + g * 128:tt * 512 + (g + 1) * 128], ident_sb[:, :]),
                    waits=([("sact", A2_ACT), ("sdve", A2_DVE)] + war_bank[bank]) if i == 0 else [],
                    inc=("spe" if i == 7 else None))
            tr_done = P.v("spe")
            P.emit("vector", lambda e, g=g, k8=k8, bank=bank: e.tensor_copy(
                kfm[:, (g * NT + 8 * k8) * 128:(g * NT + 8 * k8) * 128 + 1024], pb16[bank][:, :]),
                waits=[("spe", tr_done)], inc="sdve")
            war_bank[bank] = [("sdve", P.v("sdve"))]

    war_nmax16 = []
    for hd_i in range(10):
        P.emit("vector", lambda e, hd_i=hd_i: e.reduce_max(
            nmax[:, :], normsq[:, hd_i * NT:(hd_i + 1) * NT], axis=AX.X),
            waits=[("sact", A2_ACT)], inc="sdve")
        P.emit("vector", lambda e: e.tensor_copy(nmax16[:, :], nmax[:, :]),
               waits=war_nmax16 + [("sdve", P.v("sdve"))], inc="sdve")
        rm = P.v("sdve")
        P.emit("tensor", lambda e: e.transpose(pb16[7][0:1, 0:128], nmax16[:, :], ident_sb[:, :]),
               waits=[("sdve", rm)] + war_bank[7], inc="spe")
        trm = P.v("spe")
        P.emit("vector", lambda e, hd_i=hd_i: e.reduce_max(
            maxsq[:, hd_i:hd_i + 1], pb16[7][0:1, 0:128], axis=AX.X),
            waits=[("spe", trm)], inc="sdve")
        war_bank[7] = [("sdve", P.v("sdve"))]
        war_nmax16 = [("spe", trm)]
    war_csc = []
    for lh in range(NHL):
        g = lh // 4
        P.emit("vector", lambda e, lh=lh, g=g: e.tensor_mul(
            csc[:, 0:1], maxsq[:, lh:lh + 1], maxsq[:, 8 + g:9 + g]),
            waits=war_csc + [("sdve", P.v("sdve"))], inc="sdve")
        md = P.v("sdve")
        P.emit("scalar", lambda e: e.activation(csc[:, 1:2], csc[:, 0:1], AF.Ln, bias=eps30[:, :]),
               waits=[("sdve", md)])
        P.emit("scalar", lambda e: e.activation(csc[:, 1:2], csc[:, 1:2], AF.Exp, scale=0.5), inc="sact")
        cd = P.v("sact")
        P.emit("vector", lambda e: e.tensor_scalar_mul(csc[:, 0:1], csc[:, 1:2], -SCALE),
               waits=[("sact", cd)], inc="sdve")
        nd_ = P.v("sdve")
        P.emit("tensor", lambda e: e.matmul(pb[7][:, 0:1], ones_rf[:, :], csc[:, 0:1],
                                            start=True, stop=True),
               waits=[("sdve", nd_)] + war_bank[7], inc="spe")
        bd = P.v("spe")
        war_csc = [("spe", bd)]
        P.emit("scalar", lambda e, lh=lh: e.activation(negbias[:, lh:lh + 1], pb[7][:, 0:1], AF.Copy),
               waits=[("spe", bd)], inc="sact")
        war_bank[7] = [("sact", P.v("sact"))]
    NEGB_ACT = P.v("sact")
    SQW0_ALL = P.v("sqw0")
    SQW1_ALL = P.v("sqw1")

    P.barrier()

    SCB = [0, 1, 5, 4]
    war_qtm = [[], []]
    war_qfm = [[], []]
    war_attnT = [[], [], [], []]
    war_attnP = [[], []]
    war_avsb = [[], []]
    war_pb4 = []
    war_sc = {0: [], 1: [], 5: [], 4: []}
    war_avbk = {2: [], 6: []}
    war_Dslot = [[], []]
    war_dbc_ps = []
    war_dbc_sb = []
    deferred = [None]

    def flush_epilogue():
        if deferred[0] is None:
            return
        (par, avbank, dslot, rdone) = deferred[0]
        deferred[0] = None
        P.emit("tensor", lambda e, par=par: e.matmul(
            pb[7][:, :], ones_rf[:, :], recipD2[par][:, :], start=True, stop=True),
            waits=[("sact", rdone)] + war_dbc_ps, inc="spe")
        bc = P.v("spe")
        P.emit("scalar", lambda e: e.activation(dbc_sb[:, :], pb[7][:, :], AF.Copy),
               waits=[("spe", bc)] + war_dbc_sb, inc="sact")
        dbce = P.v("sact")
        war_dbc_ps[:] = [("sact", dbce)]
        P.emit("vector", lambda e, par=par, avbank=avbank: e.tensor_mul(
            av_sb[par][:, :], pb[avbank][:, :], dbc_sb[:, :]),
            waits=[("sact", dbce)] + war_avsb[par], inc="sdve")
        avn = P.v("sdve")
        war_dbc_sb[:] = [("sdve", avn)]
        war_avbk[avbank] = [("sdve", avn)]
        r_, lh_ = divmod_idx[0]
        avp = f"savw{par}"
        P.emit("sync", lambda e, lh_=lh_, r_=r_, par=par: e.dma_start(
            out=cc2_in[r_ * 1024 + lh_ * 128:r_ * 1024 + (lh_ + 1) * 128, :],
            in_=av_sb[par][:, :]),
            waits=[("sdve", avn)], inc=avp, incby=16)
        war_avsb[par] = [(avp, P.v(avp))]

    divmod_idx = [None]
    for r in reversed(range(NR)):
        for lh in range(NHL):
            g = lh // 4
            it = (NR - 1 - r) * NHL + lh
            par = it % 2
            qrp = f"sqr{par}"
            P.emit("sync", lambda e, r=r, lh=lh, par=par: e.dma_start(
                out=qtm[par].rearrange("p (t c) -> p t c", c=128),
                in_=qdram.rearrange("(t p) c -> p t c", p=128)[:, 4 * r:4 * r + 4, lh * 128:(lh + 1) * 128]),
                waits=[("sqw0", SQW0_ALL), ("sqw1", SQW1_ALL)] + war_qtm[par], inc=qrp, incby=16)
            qld = (qrp, P.v(qrp))
            for i in range(4):
                P.emit("tensor", lambda e, par=par, i=i: e.transpose(
                    pb16[4][:, i * 128:(i + 1) * 128], qtm[par][:, i * 128:(i + 1) * 128],
                    ident_sb[:, :]),
                    waits=([qld] + war_pb4 + war_sc[4] + war_qfm[par]) if i == 0 else [],
                    inc=("spe" if i == 3 else None))
            war_qtm[par] = [("spe", P.v("spe"))]
            qtr = P.v("spe")
            P.emit("vector", lambda e, par=par: e.tensor_copy(qfm[par][:, :], pb16[4][:, 0:512]),
                   waits=[("spe", qtr)], inc="sdve")
            war_pb4 = [("sdve", P.v("sdve"))]
            war_sc[4] = []
            qfm_ready = P.v("sdve")

            kmax = 4 * r + 4
            exp_done = [0] * kmax
            mask_done = [0] * kmax
            padd_done = [0] * (kmax // 2)
            avbank = [2, 6][it % 2]
            dslot = it % 2
            pendAV = []
            pendD = []

            def emit_av(kt, last, g=g, avbank=avbank):
                at = kt % 4
                wts = [("sact", exp_done[kt])]
                if mask_done[kt]:
                    wts.append(("sdve", mask_done[kt]))
                if kt == 0:
                    wts += war_avbk[avbank]
                P.emit("tensor", lambda e, at=at, kt=kt, g=g, last=last, avbank=avbank: e.matmul(
                    pb[avbank][:, :], kv_tm[:, kt * 512 + 256 + g * 128:kt * 512 + 256 + (g + 1) * 128],
                    attnT[at][:, :], start=(kt == 0), stop=last), waits=wts, inc="spe")
                war_attnT[at] = war_attnT[at] + [("spe", P.v("spe"))]

            def emit_d(pp, last, dslot=dslot):
                wts = [("sdve", padd_done[pp])]
                if pp == 0:
                    wts += war_Dslot[dslot]
                P.emit("tensor", lambda e, pp=pp, last=last, dslot=dslot: e.matmul(
                    pb[3][64 * dslot:64 * dslot + 1, :], ones_c16[:, :], attnP[pp % 2][:, :],
                    start=(pp == 0), stop=last), waits=wts, inc="spe")
                war_attnP[pp % 2] = [("spe", P.v("spe"))]

            for kt in range(kmax):
                sb_ = SCB[kt % 4]
                P.emit("tensor", lambda e, g=g, kt=kt, sb_=sb_, par=par: e.matmul(
                    pb[sb_][:, :], kfm[:, (g * NT + kt) * 128:(g * NT + kt + 1) * 128],
                    qfm[par][:, :], start=True, stop=True),
                    waits=[("sdve", qfm_ready)] + war_sc[sb_], inc="spe")
                qk_done = P.v("spe")
                if kt == min(6, kmax - 1):
                    flush_epilogue()
                at = kt % 4
                P.emit("scalar", lambda e, sb_=sb_, at=at, lh=lh: e.activation(
                    attnT[at][:, :], pb[sb_][:, :], AF.Exp,
                    bias=negbias[:, lh:lh + 1], scale=SCALE),
                    waits=[("spe", qk_done), ("sact", NEGB_ACT)] + war_attnT[at], inc="sact")
                exp_done[kt] = P.v("sact")
                war_attnT[at] = []
                war_sc[sb_] = [("sact", exp_done[kt])]
                if kt >= 4 * r:
                    dm = kt - 4 * r
                    P.emit("vector", lambda e, at=at, dm=dm: e.tensor_mul(
                        attnT[at][:, :], attnT[at][:, :], masks_sb[:, dm * 512:(dm + 1) * 512]),
                        waits=[("sact", exp_done[kt]), ("sk", SK_ALL)], inc="sdve")
                    mask_done[kt] = P.v("sdve")
                if kt % 2 == 1:
                    pp = kt // 2
                    a0, a1 = (kt - 1) % 4, kt % 4
                    wts = [("sact", exp_done[kt])]
                    for kq in (kt - 1, kt):
                        if mask_done[kq]:
                            wts.append(("sdve", mask_done[kq]))
                    P.emit("vector", lambda e, a0=a0, a1=a1, pp=pp: e.tensor_add(
                        attnP[pp % 2][:, :], attnT[a0][:, :], attnT[a1][:, :]),
                        waits=wts + war_attnP[pp % 2], inc="sdve")
                    padd_done[pp] = P.v("sdve")
                    war_attnT[a0] = war_attnT[a0] + [("sdve", padd_done[pp])]
                    war_attnT[a1] = war_attnT[a1] + [("sdve", padd_done[pp])]
                    pendD.append(pp)
                pendAV.append(kt)
                if len(pendAV) >= 4:
                    emit_av(pendAV.pop(0), False)
                if len(pendD) >= 2:
                    emit_d(pendD.pop(0), False)
            while pendAV:
                k0 = pendAV.pop(0)
                emit_av(k0, k0 == kmax - 1)
            while pendD:
                p0 = pendD.pop(0)
                emit_d(p0, p0 == kmax // 2 - 1)
            av_done = P.v("spe")

            P.emit("scalar", lambda e, par=par, dslot=dslot: e.activation(
                recipD2[par][:, :], pb[3][64 * dslot:64 * dslot + 1, :], AF.Ln,
                bias=eps30[:, :]),
                waits=[("spe", av_done)])
            P.emit("scalar", lambda e, par=par: e.activation(
                recipD2[par][:, :], recipD2[par][:, :], AF.Exp, scale=-1.0),
                inc="sact")
            rdone = P.v("sact")
            war_Dslot[dslot] = [("sact", rdone)]
            war_qfm[par] = [("spe", av_done)]
            divmod_idx[0] = (r, lh)
            deferred[0] = (par, avbank, dslot, rdone)
            if lh == NHL - 1:
                flush_epilogue()
                P.emit("gpsimd", lambda e, r=r: e.collective_compute(
                    "AllGather", ALU.bypass, replica_groups=[[0, 1, 2, 3], [4, 5, 6, 7]],
                    ins=[cc2_in[r * 1024:(r + 1) * 1024, :]], outs=[cc2_all[r]]),
                    waits=[("savw0", P.v("savw0")), ("savw1", P.v("savw1"))], inc=f"scc{r}")
    flush_epilogue()

    war_chk = [[], []]
    half_init = [False, False]
    for jn, j in enumerate(reversed(range(8))):
        hf = j % 2
        sj = j // 2
        for ft in range(NT):
            cb = (jn * NT + ft) % 2
            ccp = f"sccr{cb}"
            P.emit("sync", lambda e, ft=ft, j=j, cb=cb: e.dma_start(
                out=chkb[cb][:, :],
                in_=cc2_all[j, ft * 128:(ft + 1) * 128, :]),
                waits=[(f"scc{j}", 1)] + war_chk[cb], inc=ccp, incby=16)
            ld = (ccp, P.v(ccp))
            dst = av_fm[:, ft * 1024 + hf * 512:ft * 1024 + hf * 512 + 512]
            if not half_init[hf]:
                P.emit("vector", lambda e, dst=dst, cb=cb, sj=sj: e.tensor_scalar_mul(
                    dst, chkb[cb][:, :], sel_sb[:, sj:sj + 1]),
                    waits=[ld, ("sk", SK_ALL)], inc="sdve")
            else:
                P.emit("vector", lambda e, dst=dst, cb=cb, sj=sj: e.scalar_tensor_tensor(
                    dst, chkb[cb][:, :], sel_sb[:, sj:sj + 1], dst, ALU.mult, ALU.add),
                    waits=[ld, ("sdve", P.v("sdve"))], inc="sdve")
            war_chk[cb] = [("sdve", P.v("sdve"))]
        half_init[hf] = True
    AVSEL = P.v("sdve")

    P.barrier()

    P.emit("vector", lambda e: e.tensor_copy(idxf[:, :], lsi_sb[:, :]),
           waits=[("sk", SK_ALL)], inc="sdve")
    P.emit("vector", lambda e: e.tensor_scalar(
        idxf[:, :], idxf[:, :], -1.0, bp1_sb[0:1, 0:1], ALU.mult, ALU.add),
        waits=[("sdve", P.v("sdve"))], inc="sdve")
    P.emit("vector", lambda e: e.tensor_scalar(
        idxf[:, :], idxf[:, :], 0.0, 1.0, ALU.max, ALU.min),
        waits=[("sdve", P.v("sdve"))], inc="sdve")
    g0, g1, g2, g3 = (idxf[0:1, i:i + 1] for i in range(4))

    def _gstep(fn):
        P.emit("vector", fn, waits=[("sdve", P.v("sdve"))], inc="sdve")

    _gstep(lambda e: e.tensor_scalar(gsc[0:1, 0:1], g1, -1.0, 1.0, ALU.mult, ALU.add))
    _gstep(lambda e: e.tensor_mul(gsc[0:1, 1:2], g0, gsc[0:1, 0:1]))
    _gstep(lambda e: e.tensor_scalar(gsc[0:1, 2:3], g1, 2.0, None, ALU.mult))
    _gstep(lambda e: e.tensor_add(gsc[0:1, 1:2], gsc[0:1, 1:2], gsc[0:1, 2:3]))
    _gstep(lambda e: e.tensor_scalar(gsc[0:1, 3:4], g2, -1.0, 1.0, ALU.mult, ALU.add))
    _gstep(lambda e: e.tensor_mul(gsc[0:1, 1:2], gsc[0:1, 1:2], gsc[0:1, 3:4]))
    _gstep(lambda e: e.tensor_scalar(gsc[0:1, 4:5], g3, 1.0, 3.0, ALU.mult, ALU.add))
    _gstep(lambda e: e.tensor_mul(gsc[0:1, 4:5], gsc[0:1, 4:5], g2))
    _gstep(lambda e: e.tensor_add(gsc[0:1, 1:2], gsc[0:1, 1:2], gsc[0:1, 4:5]))
    for j in range(3):
        _gstep(lambda e, j=j: e.tensor_scalar(
            gsc[0:1, 5:6], gsc[0:1, 1:2], 1.0, -float(j + 2), ALU.mult, ALU.add))
        _gstep(lambda e: e.tensor_mul(gsc[0:1, 6:7], gsc[0:1, 5:6], gsc[0:1, 5:6]))
        _gstep(lambda e: e.tensor_scalar(gsc[0:1, 6:7], gsc[0:1, 6:7], -1.0, 1.0, ALU.mult, ALU.add))
        _gstep(lambda e: e.tensor_scalar(gsc[0:1, 6:7], gsc[0:1, 6:7], 0.0, 1.0, ALU.max, ALU.min))
        _gstep(lambda e, j=j: e.tensor_scalar_mul(
            gex[0:1, 16 * j:16 * (j + 1)], ones_rf[0:1, 0:16], gsc[0:1, 6:7]))
    gexd = P.v("sdve")
    P.emit("tensor", lambda e: e.matmul(pb[7][0:48, 0:1], gex[:, :], ones_rf[0:1, 0:1],
                                        start=True, stop=True),
           waits=[("sdve", gexd)] + war_bank[7] + war_dbc_ps, inc="spe")
    gb = P.v("spe")
    P.emit("scalar", lambda e: e.activation(gmask[:, :], pb[7][0:48, 0:1], AF.Copy),
           waits=[("spe", gb)], inc="sact")
    GMASK_ACT = P.v("sact")

    P.emit("gpsimd", lambda e: e.dma_start(out=bias16[:, :], in_=bo[:, :]), inc="sw", incby=16)
    P.emit("gpsimd", lambda e: e.dma_start(out=lBT_sb[:, :], in_=lBT[:, :]), inc="sw", incby=16)
    P.emit("gpsimd", lambda e: e.dma_start(
        out=lAT_sb.rearrange("p (j r) -> p j r", r=48),
        in_=lAT.rearrange("(j p) r -> p j r", p=128)), inc="sw", incby=16)
    CCONST = P.v("sw")


    war_wc = [[], []]
    war_xo = [[], []]
    war_x2cw = [[], []]
    war_cbank = {0: [], 1: []}
    for fo in range(NFO):
        wb = fo % 2
        wcp = f"swc{wb}"
        P.emit("gpsimd", lambda e, fo=fo, wb=wb: e.dma_start(
            out=wbufC[wb].rearrange("p (j n) -> p j n", n=512),
            in_=woT.rearrange("(j p) n -> p j n", p=128)[:, :, fo * 512:(fo + 1) * 512]),
            waits=war_wc[wb], inc=wcp, incby=16)
        wld = (wcp, P.v(wcp))
        for ts in range(8):
            xb_ = (fo * 8 + ts) % 2
            xop = f"sxo{xb_}"
            P.emit("sync", lambda e, fo=fo, ts=ts, xb_=xb_: e.dma_start(
                out=xoC[xb_][:, :],
                in_=x_own[ts * 128:(ts + 1) * 128, fo * 512:(fo + 1) * 512]),
                waits=war_xo[xb_], inc=xop, incby=16)
            xld = (xop, P.v(xop))
            bank = ts % 2
            for dt in range(ND):
                P.emit("tensor", lambda e, dt=dt, ts=ts, wb=wb, bank=bank: e.matmul(
                    pb[bank][:, :], av_fm[:, dt * 1024 + ts * 128:dt * 1024 + (ts + 1) * 128],
                    wbufC[wb][:, dt * 512:(dt + 1) * 512], start=(dt == 0), stop=False),
                    waits=([("sdve", AVSEL), wld] + war_cbank[bank]) if dt == 0 else [])
            P.emit("tensor", lambda e, fo=fo, bank=bank: e.matmul(
                pb[bank][:, :], ones_r16[:, :], bias16[:, fo * 512:(fo + 1) * 512],
                start=False, stop=True),
                waits=[("sw", CCONST)], inc="spe")
            mmd = P.v("spe")
            P.emit("vector", lambda e, xb_=xb_, bank=bank: e.tensor_add(
                x2cw[xb_][:, :], pb[bank][:, :], xoC[xb_][:, :]),
                waits=[("spe", mmd), xld] + war_x2cw[xb_], inc="sdve")
            od = P.v("sdve")
            war_cbank[bank] = [("sdve", od)]
            war_xo[xb_] = [("sdve", od)]
            x2p = f"sx2w{xb_}"
            P.emit("sync", lambda e, ts=ts, fo=fo, xb_=xb_: e.dma_start(
                out=x2d[ts * 128:(ts + 1) * 128, fo * 512:(fo + 1) * 512], in_=x2cw[xb_][:, :]),
                waits=[("sdve", od)], inc=x2p, incby=16)
            war_x2cw[xb_] = [(x2p, P.v(x2p))]
        war_wc[wb] = [("spe", P.v("spe"))]
    C1_PE = P.v("spe")
    X2W0 = P.v("sx2w0")
    X2W1 = P.v("sx2w1")
    P.emit("gpsimd", lambda e: e.dma_start(out=bias16[:, :], in_=bffn[:, :]),
           waits=[("spe", C1_PE)], inc="sw", incby=16)
    BFFN_SW = P.v("sw")

    war_x2s = [[], []]
    for ts in range(8):
        bi = ts % 2
        xrp = f"sx2r{bi}"
        P.emit("sync", lambda e, ts=ts, bi=bi: e.dma_start(
            out=x2s[bi][:, :], in_=x2d[ts * 128:(ts + 1) * 128, :]),
            waits=[("sx2w0", X2W0), ("sx2w1", X2W1)] + war_x2s[bi], inc=xrp, incby=16)
        xld = P.v(xrp)
        P.emit("vector", lambda e, bi=bi, ts=ts: e.reduce_sum(
            sums[:, ts:ts + 1], x2s[bi][:, :], axis=AX.X),
            waits=[(xrp, xld)], inc="sdve")
        P.emit("scalar", lambda e, bi=bi, ts=ts: e.activation(
            ln2_tm[:, :], x2s[bi][:, :], AF.Square, accum_out=sqs[:, ts:ts + 1]),
            waits=[(xrp, xld), ("spe", C1_PE), ("sact", GMASK_ACT)], inc="sact")
        war_x2s[bi] = [("sdve", P.v("sdve")), ("sact", P.v("sact"))]
    LN2P1_ACT = P.v("sact")
    RSTD2_ACT, NEGMR2_DVE = ln_stats_batch(8)

    war_x2s3 = [war_x2s[0], war_x2s[1]]
    war_ln2tm = [("spe", 0)]
    war_p = []
    for ts in range(8):
        bi = ts % 2
        xrp = f"sx2r{bi}"
        P.emit("sync", lambda e, ts=ts, bi=bi: e.dma_start(
            out=x2s[bi][:, :], in_=x2d[ts * 128:(ts + 1) * 128, :]),
            waits=war_x2s3[bi], inc=xrp, incby=16)
        xld = P.v(xrp)
        P.emit("scalar", lambda e, bi=bi, ts=ts: e.activation(
            ln2_tm[:, :], x2s[bi][:, :], AF.Identity,
            bias=negmrall[:, ts:ts + 1], scale=rstdall[:, ts:ts + 1]),
            waits=[(xrp, xld), ("sdve", NEGMR2_DVE), ("sact", RSTD2_ACT), war_ln2tm[0]],
            inc="sact")
        appd = P.v("sact")
        war_x2s3[bi] = [("sact", appd)]
        for k in range(4):
            bank2 = 6 + (k % 2)
            for i in range(8):
                dt = 8 * k + i
                P.emit("tensor", lambda e, dt=dt, bank2=bank2, i=i: e.transpose(
                    pb16[bank2][:, i * 128:(i + 1) * 128], ln2_tm[:, dt * 128:(dt + 1) * 128],
                    ident_sb[:, :]),
                    waits=([("sact", appd)] + war_bank[bank2]) if i == 0 else [],
                    inc=("spe" if i == 7 else None))
            trd = P.v("spe")
            P.emit("vector", lambda e, k=k, ts=ts, bank2=bank2: e.tensor_copy(
                ln2_fm.rearrange("p (j t) -> p j t", t=1024)[:, 8 * k:8 * k + 8, ts * 128:(ts + 1) * 128],
                pb16[bank2][:, :].rearrange("p (j t) -> p j t", t=128)),
                waits=[("spe", trd)], inc="sdve")
            war_bank[bank2] = [("sdve", P.v("sdve"))]
        war_ln2tm = [("spe", P.v("spe"))]
        ln2_done = P.v("sdve")
        for dt in range(ND):
            P.emit("tensor", lambda e, dt=dt, ts=ts: e.matmul(
                pb[2][0:48, 0:128], lAT_sb[:, dt * 48:(dt + 1) * 48],
                ln2_fm[:, dt * 1024 + ts * 128:dt * 1024 + (ts + 1) * 128],
                start=(dt == 0), stop=(dt == ND - 1)),
                waits=([("sdve", ln2_done), ("sw", CCONST), ("sact", GMASK_ACT)] + war_p) if dt == 0 else [],
                inc=("spe" if dt == ND - 1 else None))
        pd = P.v("spe")
        P.emit("vector", lambda e, ts=ts: e.tensor_scalar_mul(
            p_sb[:, ts * 128:(ts + 1) * 128], pb[2][0:48, 0:128], gmask[:, 0:1]),
            waits=[("spe", pd), ("sact", GMASK_ACT)], inc="sdve")
        war_p = [("sdve", P.v("sdve"))]
    LN2_ALL = P.v("sdve")
    P_ALL = P.v("sdve")

    war_wc2 = [war_wc[0], war_wc[1]]
    war_out = [[], []]
    war_x2cr = [[], []]
    war_cbank2 = {0: war_cbank[0], 1: war_cbank[1]}
    for fo in range(NFO):
        wb = fo % 2
        wcp = f"swc{wb}"
        wts = list(war_wc2[wb])
        if fo == 0:
            wts += [war_ln2tm[0]]
        P.emit("gpsimd", lambda e, fo=fo, wb=wb: e.dma_start(
            out=wbufC[wb].rearrange("p (j n) -> p j n", n=512),
            in_=ffnT.rearrange("(j p) n -> p j n", p=128)[:, :, fo * 512:(fo + 1) * 512]),
            waits=wts, inc=wcp, incby=16)
        wld = (wcp, P.v(wcp))
        for ts in range(8):
            xb_ = (fo * 8 + ts) % 2
            xrp = f"sx2c{xb_}"
            P.emit("sync", lambda e, fo=fo, ts=ts, xb_=xb_: e.dma_start(
                out=x2cr[xb_][:, :],
                in_=x2d[ts * 128:(ts + 1) * 128, fo * 512:(fo + 1) * 512]),
                waits=[("sx2w0", X2W0), ("sx2w1", X2W1)] + war_x2cr[xb_], inc=xrp, incby=16)
            x2ld = (xrp, P.v(xrp))
            bank = ts % 2
            for dt in range(ND):
                P.emit("tensor", lambda e, dt=dt, ts=ts, wb=wb, bank=bank: e.matmul(
                    pb[bank][:, :], ln2_fm[:, dt * 1024 + ts * 128:dt * 1024 + (ts + 1) * 128],
                    wbufC[wb][:, dt * 512:(dt + 1) * 512], start=(dt == 0), stop=False),
                    waits=([("sdve", LN2_ALL), wld] + war_cbank2[bank]) if dt == 0 else [])
            P.emit("tensor", lambda e, fo=fo, bank=bank: e.matmul(
                pb[bank][:, :], ones_r16[:, :], bias16[:, fo * 512:(fo + 1) * 512],
                start=False, stop=False),
                waits=[("sw", BFFN_SW)])
            P.emit("tensor", lambda e, fo=fo, ts=ts, bank=bank: e.matmul(
                pb[bank][:, :], p_sb[:, ts * 128:(ts + 1) * 128],
                lBT_sb[:, fo * 512:(fo + 1) * 512], start=False, stop=True),
                waits=[("sdve", P_ALL)], inc="spe")
            mmd = P.v("spe")
            ob = (fo * 8 + ts) % 2
            P.emit("vector", lambda e, bank=bank, ob=ob, xb_=xb_: e.tensor_add(
                outsb[ob][:, :], pb[bank][:, :], x2cr[xb_][:, :]),
                waits=[("spe", mmd), x2ld] + war_out[ob], inc="sdve")
            od = P.v("sdve")
            war_cbank2[bank] = [("sdve", od)]
            war_x2cr[xb_] = [("sdve", od)]
            sop = f"sout{ob}"
            P.emit("sync", lambda e, ts=ts, fo=fo, ob=ob: e.dma_start(
                out=out[ts * 128:(ts + 1) * 128, fo * 512:(fo + 1) * 512], in_=outsb[ob][:, :]),
                waits=[("sdve", od)], inc=sop, incby=16)
            war_out[ob] = [(sop, P.v(sop))]
        war_wc2[wb] = [("spe", P.v("spe"))]

    P.emit("sync", lambda e: e.nop(), waits=[("sout0", P.v("sout0")), ("sout1", P.v("sout1"))])

    P.replay()
    return nc


_NC_CACHE = None


def kernel(L0_attn_x, L0_attn_w_qkv, L0_attn_b_qkv, L0_attn_w_o, L0_attn_b_o,
           L0_ffn_weight, L0_ffn_bias, L0_ffn_lora_A, L0_ffn_lora_B,
           L0_ffn_lora_start_indices):
    global _NC_CACHE
    x = np.asarray(L0_attn_x, np.float32)
    w_qkv = np.asarray(L0_attn_w_qkv, np.float32)
    b_qkv = np.asarray(L0_attn_b_qkv, np.float32)
    w_o = np.asarray(L0_attn_w_o, np.float32)
    b_o = np.asarray(L0_attn_b_o, np.float32)
    w_ffn = np.asarray(L0_ffn_weight, np.float32)
    b_ffn = np.asarray(L0_ffn_bias, np.float32)
    lora_A = np.asarray(L0_ffn_lora_A, np.float32)
    lora_B = np.asarray(L0_ffn_lora_B, np.float32)
    lsi = np.asarray(L0_ffn_lora_start_indices).astype(np.int32, copy=False)
    lsi = np.ascontiguousarray(lsi)

    woT = np.ascontiguousarray(w_o.T)
    ffnT = np.ascontiguousarray(w_ffn.T)
    lAT = np.ascontiguousarray(lora_A.transpose(2, 0, 1).reshape(4096, 48))
    lBT = np.ascontiguousarray(lora_B.transpose(0, 2, 1).reshape(48, 4096))

    inv_freq = (1.0 / 10000.0 ** (np.arange(0, HD, 2, dtype=np.float32) / HD)).astype(np.float32)
    pos = np.arange(S, dtype=np.float32)
    ang = pos[:, None] * inv_freq[None, :]
    cosT = np.cos(ang).astype(np.float32)
    sinT = np.sin(ang).astype(np.float32)
    ident = np.eye(128, dtype=ml_dtypes.bfloat16)
    kk = np.arange(128)[:, None]
    qq = np.arange(512)[None, :]
    mask4 = np.stack([(kk + 128 * d <= qq) for d in range(4)]).astype(ml_dtypes.bfloat16)

    if _NC_CACHE is None:
        _NC_CACHE = build_nc()
    nc = _NC_CACHE

    xb = [np.ascontiguousarray(x[0]), np.ascontiguousarray(x[1])]
    in_maps = []
    for c in range(8):
        B, m = c // 4, c % 4
        rows = np.concatenate([
            np.arange(m * QC, (m + 1) * QC),
            np.arange(4096 + m * KVC, 4096 + (m + 1) * KVC),
            np.arange(5120 + m * KVC, 5120 + (m + 1) * KVC),
        ])
        wq = np.ascontiguousarray(w_qkv[rows].T)
        bq = np.ascontiguousarray(b_qkv[rows])[None, :]
        sel = np.zeros((128, 4), np.float32)
        sel[:, m] = 1.0
        in_maps.append({
            "x_batch": xb[B],
            "x_own": np.ascontiguousarray(xb[B][m * 1024:(m + 1) * 1024]),
            "wqkvT": wq, "bqkv": bq,
            "woT": woT, "bo": b_o[None, :],
            "ffnT": ffnT, "bffn": b_ffn[None, :],
            "lAT": lAT, "lBT": lBT, "lsi": lsi,
            "bplus1": np.array([[float(B + 1)]], np.float32),
            "selmask": sel,
            "cosT": cosT, "sinT": sinT,
            "identt": ident, "mask4": mask4,
        })

    res = run_bass_kernel_spmd(nc, in_maps, core_ids=list(range(8)))
    full = np.empty((2, 4096, 4096), np.float32)
    for c in range(8):
        B, m = c // 4, c % 4
        full[B, m * 1024:(m + 1) * 1024] = res.results[c]["out"]
    return full
